# version 23
# speedup vs baseline: 1.0016x; 1.0016x over previous
import numpy as np
import ml_dtypes
from contextlib import ExitStack
import concourse.bass as bass
import concourse.mybir as mybir
from concourse.bass_utils import run_bass_kernel_spmd

F32 = mybir.dt.float32
BF16 = mybir.dt.bfloat16
I32 = mybir.dt.int32
AF = mybir.ActivationFunctionType
ALU = mybir.AluOpType
AX = mybir.AxisListType

B, S, D, H, DK, DV, DFF = 4, 8192, 1024, 8, 128, 128, 4096
NE, NI, TOPK = 64, 512, 8
C = 128
NCH = S // C
HALF = S // 2
NHT = HALF // 128
SCALE = float(DK) ** -0.5
LN_EPS = 1e-5
EPS = 1e-6


class Sem:
    def __init__(self, nc, name, ctx):
        self.h = ctx.enter_context(nc.semaphore(name))
        self.n = 0

    def mark(self, inst, k=1):
        inst.then_inc(self.h, k)
        self.n += k
        return Ev(self, self.n)

    def at(self, n):
        return Ev(self, n)

    def now(self):
        return Ev(self, self.n)


class Ev:
    def __init__(self, sem, n):
        self.sem, self.n = sem, n

    def wait(self, eng):
        eng.wait_ge(self.sem.h, self.n)


def wait_all(eng, *evs):
    for ev in evs:
        if ev is not None:
            ev.wait(eng)


def newton_rsqrt(v, out, x, tmp_f, tmp_i):
    v.tensor_copy(tmp_f, x.bitcast(I32))
    v.drain()
    v.tensor_scalar(tmp_f, tmp_f, -0.5, float(0x5F3759DF), op0=ALU.mult, op1=ALU.add)
    v.drain()
    v.tensor_copy(tmp_i, tmp_f)
    v.drain()
    v.tensor_copy(out.bitcast(I32), tmp_i)
    v.drain()
    for _ in range(3):
        v.tensor_tensor(tmp_f, out, out, op=ALU.mult)
        v.drain()
        v.tensor_tensor(tmp_f, tmp_f, x, op=ALU.mult)
        v.drain()
        v.tensor_scalar(tmp_f, tmp_f, -0.5, 1.5, op0=ALU.mult, op1=ALU.add)
        v.drain()
        v.tensor_tensor(out, out, tmp_f, op=ALU.mult)
        v.drain()


def plan(sem, base, per_chunk, n_chunks):
    tabs = {name: [] for name, _ in per_chunk}
    n = base
    for _ in range(n_chunks):
        for name, k in per_chunk:
            n += k
            tabs[name].append(Ev(sem, n))
    return tabs, n




class _DryInst:
    def then_inc(self, h, k):
        return self


_DRY_INST = _DryInst()


class DryEng:
    def __getattr__(self, name):
        return lambda *a, **k: _DRY_INST


DRY = DryEng()


class EvBook:
    def __init__(self, sems):
        self.sems = sems
        self.tab = {}
        self.dry_n = {}
        self.mode = "dry"
        self.idx = {}

    def start_dry(self):
        self.mode = "dry"
        self.dry_n = {k: s.n for k, s in self.sems.items()}

    def start_real(self):
        self.mode = "real"
        self.idx = {k: 0 for k in self.tab}

    def mark(self, semname, evname, inst, k=1):
        sem = self.sems[semname]
        if self.mode == "dry":
            self.dry_n[semname] += k
            self.tab.setdefault(evname, []).append(Ev(sem, self.dry_n[semname]))
        else:
            ev = sem.mark(inst, k)
            i = self.idx[evname]
            want = self.tab[evname][i]
            assert ev.n == want.n and ev.sem is want.sem, \
                f"mark mismatch {evname}[{i}]: got {ev.n} want {want.n}"
            self.idx[evname] = i + 1

    def ev(self, evname, i=0):
        lst = self.tab.get(evname)
        if lst is None:
            raise KeyError(evname)
        return lst[i]

    def wait(self, eng, evname, i=0):
        if self.mode == "dry":
            return
        lst = self.tab.get(evname)
        if lst is None:
            raise KeyError(f"wait on unknown event {evname}")
        lst[i].wait(eng)

    def maybe_wait(self, eng, evname, i=0):
        if i < 0:
            return
        self.wait(eng, evname, i)




class Mark:
    __slots__ = ("semname", "k", "sem", "n", "eng", "idx")

    def __init__(self, semname, k):
        self.semname, self.k = semname, k
        self.sem = None
        self.n = None


class SOp:
    __slots__ = ("eng", "fn", "waits", "mark")

    def __init__(self, eng, fn, waits, mark):
        self.eng, self.fn, self.waits, self.mark = eng, fn, waits, mark


class Sched:

    def __init__(self, sems):
        self.sems = sems
        self.ops = []

    def add(self, eng, fn, waits=(), sem=None, k=1):
        mark = Mark(sem, k) if sem else None
        self.ops.append(SOp(eng, fn, [w for w in waits if w is not None], mark))
        return mark

    def dma(self, eng, fn, waits=(), sem=None):
        return self.add(eng, fn, waits, sem=sem, k=16)

    def plan(self):
        counts = {name: s.n for name, s in self.sems.items()}
        for op in self.ops:
            if op.mark is not None:
                m = op.mark
                m.sem = self.sems[m.semname]
                counts[m.semname] += m.k
                m.n = counts[m.semname]
        self.final_counts = counts

    ENTRY_SKIP = ("vec", "scl")

    def emit_engine(self, name, eng):
        if name not in self.ENTRY_SKIP:
            for w in getattr(self, "entry", []):
                eng.wait_ge(w.sem.h, w.n)
        for op in self.ops:
            if op.eng != name:
                continue
            for w in op.waits:
                assert w.n is not None, "wait on unplanned mark"
                eng.wait_ge(w.sem.h, w.n)
            inst = op.fn(eng)
            if op.mark is not None:
                inst.then_inc(op.mark.sem.h, op.mark.k)
                op.mark.sem.n = max(op.mark.sem.n, op.mark.n)


class BankTracker:

    def __init__(self, ap):
        self.ap = ap
        self.pending = []

    def write_waits(self):
        w = self.pending
        self.pending = []
        return w

    def add_use(self, mark):
        if mark is not None:
            self.pending.append(mark)


def pass_a1(nc, block, ctx, t_in, t_sc, sems):
    e = ctx.enter_context
    x_full = t_in["x_full"]
    identb = t_in["identb_sb"]

    wq = e(nc.sbuf_tensor("a1_wq", [128, 8 * D], BF16))
    wk = e(nc.sbuf_tensor("a1_wk", [128, 8 * D], BF16))
    wv = e(nc.sbuf_tensor("a1_wv", [128, 8 * D], BF16))
    wbar = e(nc.sbuf_tensor("a1_wbar", [128, 8 * 80], BF16))

    xbuf = [e(nc.sbuf_tensor(f"a1_x{i}", [128, D], F32)) for i in range(2)]
    xr = [e(nc.sbuf_tensor(f"a1_xr{i}", [128, D], BF16)) for i in range(2)]
    xrt = [e(nc.sbuf_tensor(f"a1_xrt{i}", [128, D], BF16)) for i in range(2)]
    qs = [e(nc.sbuf_tensor(f"a1_q{i}", [128, D], BF16)) for i in range(2)]
    ks = [e(nc.sbuf_tensor(f"a1_k{i}", [128, D], BF16)) for i in range(2)]
    vs = [e(nc.sbuf_tensor(f"a1_v{i}", [128, D], BF16)) for i in range(2)]
    bas = [e(nc.sbuf_tensor(f"a1_ba{i}", [128, 80], F32)) for i in range(2)]
    sqx = [e(nc.sbuf_tensor(f"a1_sqx{i}", [128, D], F32)) for i in range(2)]
    sqq = [e(nc.sbuf_tensor(f"a1_sqq{i}", [128, D], F32)) for i in range(2)]
    sqk = [e(nc.sbuf_tensor(f"a1_sqk{i}", [128, D], F32)) for i in range(2)]
    red = e(nc.sbuf_tensor("a1_red", [128, 26], F32))
    rs = e(nc.sbuf_tensor("a1_rs", [128, 18], F32))
    ntf = e(nc.sbuf_tensor("a1_ntf", [128, 18], F32))
    nti = e(nc.sbuf_tensor("a1_nti", [128, 18], I32))

    pA = e(nc.psum_tensor("a1_pA", [128, 512], F32))
    pB = e(nc.psum_tensor("a1_pB", [128, 512], F32))
    pq = e(nc.psum_tensor("a1_pq", [128, D], F32))
    pk = e(nc.psum_tensor("a1_pk", [128, D], F32))
    pv = e(nc.psum_tensor("a1_pv", [128, D], F32))
    pba = pA[:, 0:80]

    s_x2 = [sems["dma_a"], sems["dma_b"]]
    s_w = sems["w"]
    s_v, s_t, s_s = sems["vec"], sems["ten"], sems["scl"]
    s_sp2 = [sems["spill"], sems["spill2"]]

    VE, v_end = plan(s_v, s_v.n, [("s2red", 1), ("xr", 1), ("qred", 1), ("fin", 1)], NCH)
    SC, s_end = plan(s_s, s_s.n, [("sqx", 1), ("xrt", 1), ("evq", 1), ("sqq", 1),
                                  ("evk", 1), ("sqk", 1), ("evv", 1)], NCH)
    TE, t_end = plan(s_t, s_t.n, [("tp", 1), ("mmq", 1), ("mmk", 1), ("mmv", 1),
                                  ("mmba", 1)], NCH)
    XL = {"x": [Ev(s_x2[ci % 2], s_x2[ci % 2].n + 16 * (ci // 2 + 1)) for ci in range(NCH)]}
    SP = {"spba": [Ev(s_sp2[ci % 2], s_sp2[ci % 2].n + 64 * (ci // 2 + 1))
                   for ci in range(NCH)]}

    def P(tab, name, ci):
        return tab[name][ci] if ci >= 0 else None

    @block.gpsimd
    def _(g):
        for w_sb, name in [(wq, "Wq"), (wk, "Wk"), (wv, "Wv"), (wbar, "wbar")]:
            s_w.mark(g.dma_start(out=w_sb[:].rearrange("p (a j) -> p a j", a=8),
                                 in_=t_in[name].rearrange("(a p) j -> p a j", p=128)), 16)
    ev_w = s_w.now()

    def emit_spill(sy, ci):
        ib = ci % 2
        wait_all(sy, P(VE, "fin", ci), P(SC, "evv", ci))
        r = slice(ci * 128, (ci + 1) * 128)
        sp = s_sp2[ci % 2]
        sp.mark(sy.dma_start(out=t_sc["q_sc"][r, :], in_=qs[ib][:]), 16)
        sp.mark(sy.dma_start(out=t_sc["k_sc"][r, :], in_=ks[ib][:]), 16)
        sp.mark(sy.dma_start(out=t_sc["v_sc"][r, :], in_=vs[ib][:]), 16)
        ev = sp.mark(sy.dma_start(out=t_sc["balg_sc"][r, :], in_=bas[ib][:]), 16)
        assert ev.n == SP["spba"][ci].n and ev.sem is SP["spba"][ci].sem

    @block.sync
    def _(sy):
        for ci in range(NCH + 2):
            if ci < NCH:
                wait_all(sy, P(VE, "xr", ci - 2), P(SC, "sqx", ci - 2))
                ev = s_x2[ci % 2].mark(sy.dma_start(out=xbuf[ci % 2][:],
                                                    in_=x_full[ci * 128:(ci + 1) * 128, :]), 16)
                assert ev.n == XL["x"][ci].n and ev.sem is XL["x"][ci].sem
            if ci >= 2:
                emit_spill(sy, ci - 2)

    @block.scalar
    def _(sc):
        for ci in range(NCH):
            ib = ci % 2
            wait_all(sc, P(XL, "x", ci), P(VE, "s2red", ci - 2))
            ev = s_s.mark(sc.activation(out=sqx[ib][:], in_=xbuf[ib][:], func=AF.Square))
            assert ev.n == SC["sqx"][ci].n
            wait_all(sc, P(TE, "tp", ci), P(TE, "mmba", ci - 2))
            sc.copy(out=xrt[ib][:, 0:512], in_=pA[:])
            ev = s_s.mark(sc.copy(out=xrt[ib][:, 512:1024], in_=pB[:]))
            assert ev.n == SC["xrt"][ci].n
            wait_all(sc, P(TE, "mmq", ci), P(SP, "spba", ci - 2), P(VE, "qred", ci - 2))
            ev = s_s.mark(sc.activation(out=qs[ib][:], in_=pq[:], func=AF.Silu))
            assert ev.n == SC["evq"][ci].n
            sc.drain()
            ev = s_s.mark(sc.activation(out=sqq[ib][:], in_=qs[ib][:], func=AF.Square))
            assert ev.n == SC["sqq"][ci].n
            wait_all(sc, P(TE, "mmk", ci))
            ev = s_s.mark(sc.activation(out=ks[ib][:], in_=pk[:], func=AF.Silu))
            assert ev.n == SC["evk"][ci].n
            sc.drain()
            wait_all(sc, P(VE, "fin", ci - 2))
            ev = s_s.mark(sc.activation(out=sqk[ib][:], in_=ks[ib][:], func=AF.Square))
            assert ev.n == SC["sqk"][ci].n
            wait_all(sc, P(TE, "mmv", ci))
            ev = s_s.mark(sc.activation(out=vs[ib][:], in_=pv[:], func=AF.Silu))
            assert ev.n == SC["evv"][ci].n

    @block.vector
    def _(v):
        for ci in range(NCH):
            ib = ci % 2
            wait_all(v, P(XL, "x", ci))
            v.reduce_sum(red[:, 0:1], xbuf[ib][:], axis=AX.X)
            wait_all(v, P(SC, "sqx", ci))
            ev = s_v.mark(v.reduce_sum(red[:, 1:2], sqx[ib][:], axis=AX.X))
            assert ev.n == VE["s2red"][ci].n
            v.drain()
            v.tensor_scalar(red[:, 2:3], red[:, 0:1], 1.0 / D, None, op0=ALU.mult)
            v.drain()
            v.tensor_tensor(red[:, 3:4], red[:, 2:3], red[:, 2:3], op=ALU.mult)
            v.drain()
            v.tensor_scalar(red[:, 4:5], red[:, 1:2], 1.0 / D, None, op0=ALU.mult)
            v.drain()
            v.tensor_tensor(red[:, 3:4], red[:, 4:5], red[:, 3:4], op=ALU.subtract)
            v.drain()
            v.tensor_scalar(red[:, 4:5], red[:, 3:4], LN_EPS, None, op0=ALU.add)
            v.drain()
            v.reciprocal(red[:, 22:23], red[:, 4:5])
            v.drain()
            v.tensor_tensor(red[:, 23:24], red[:, 3:4], red[:, 22:23], op=ALU.mult)
            v.drain()
            v.tensor_scalar(red[:, 5:6], red[:, 23:24], EPS, None, op0=ALU.add)
            v.drain()
            newton_rsqrt(v, rs[:, 0:2], red[:, 4:6], ntf[:, 0:2], nti[:, 0:2])
            v.tensor_tensor(red[:, 24:25], rs[:, 0:1], rs[:, 1:2], op=ALU.mult)
            v.tensor_tensor(red[:, 25:26], rs[:, 1:2], red[:, 23:24], op=ALU.mult)
            v.drain()
            wait_all(v, P(TE, "tp", ci - 2))
            ev = s_v.mark(
                v.tensor_scalar(xr[ib][:], xbuf[ib][:], red[:, 2:3], red[:, 24:25],
                                op0=ALU.subtract, op1=ALU.mult))
            assert ev.n == VE["xr"][ci].n
            wait_all(v, P(SC, "sqq", ci))
            ev = s_v.mark(
                v.reduce_sum(red[:, 6:14], sqq[ib][:].rearrange("p (h d) -> p h d", h=8),
                             axis=AX.X))
            assert ev.n == VE["qred"][ci].n
            wait_all(v, P(SC, "sqk", ci))
            v.reduce_sum(red[:, 14:22], sqk[ib][:].rearrange("p (h d) -> p h d", h=8), axis=AX.X)
            v.drain()
            v.tensor_scalar(red[:, 6:22], red[:, 6:22], 1.0, EPS, op0=ALU.mult, op1=ALU.add)
            v.drain()
            newton_rsqrt(v, rs[:, 2:18], red[:, 6:22], ntf[:, 2:18], nti[:, 2:18])
            for h in range(8):
                v.tensor_scalar(qs[ib][:, h * 128:(h + 1) * 128],
                                qs[ib][:, h * 128:(h + 1) * 128],
                                rs[:, 2 + h:3 + h], SCALE, op0=ALU.mult, op1=ALU.mult)
            for h in range(8):
                v.tensor_scalar(ks[ib][:, h * 128:(h + 1) * 128],
                                ks[ib][:, h * 128:(h + 1) * 128],
                                rs[:, 10 + h:11 + h], None, op0=ALU.mult)
            wait_all(v, P(TE, "mmba", ci), P(SP, "spba", ci - 2))
            v.tensor_scalar(bas[ib][:, 16:80], pba[:, 16:80], red[:, 25:26], None, op0=ALU.mult)
            ev = s_v.mark(v.tensor_copy(bas[ib][:, 0:16], pba[:, 0:16]))
            assert ev.n == VE["fin"][ci].n

    @block.tensor
    def _(t):
        ev_w.wait(t)
        for ci in range(NCH):
            ib = ci % 2
            wait_all(t, P(VE, "xr", ci), P(VE, "fin", ci - 1), P(SC, "xrt", ci - 1))
            for a in range(4):
                mm = t.matmul(pA[:, a * 128:(a + 1) * 128], xr[ib][:, a * 128:(a + 1) * 128],
                              identb, start=True, stop=True)
            for a in range(4):
                mm = t.matmul(pB[:, a * 128:(a + 1) * 128], xr[ib][:, 512 + a * 128:640 + a * 128],
                              identb, start=True, stop=True)
            ev = s_t.mark(mm)
            assert ev.n == TE["tp"][ci].n
            wait_all(t, P(SC, "xrt", ci), P(SC, "evq", ci - 1))
            for half in range(2):
                for a in range(8):
                    mm = t.matmul(pq[:, half * 512:(half + 1) * 512],
                                  xrt[ib][:, a * 128:(a + 1) * 128],
                                  wq[:, a * 1024 + half * 512: a * 1024 + (half + 1) * 512],
                                  start=(a == 0), stop=(a == 7))
            ev = s_t.mark(mm)
            assert ev.n == TE["mmq"][ci].n
            wait_all(t, P(SC, "evk", ci - 1))
            for half in range(2):
                for a in range(8):
                    mm = t.matmul(pk[:, half * 512:(half + 1) * 512],
                                  xrt[ib][:, a * 128:(a + 1) * 128],
                                  wk[:, a * 1024 + half * 512: a * 1024 + (half + 1) * 512],
                                  start=(a == 0), stop=(a == 7))
            ev = s_t.mark(mm)
            assert ev.n == TE["mmk"][ci].n
            wait_all(t, P(SC, "evv", ci - 1))
            for half in range(2):
                for a in range(8):
                    mm = t.matmul(pv[:, half * 512:(half + 1) * 512],
                                  xrt[ib][:, a * 128:(a + 1) * 128],
                                  wv[:, a * 1024 + half * 512: a * 1024 + (half + 1) * 512],
                                  start=(a == 0), stop=(a == 7))
            ev = s_t.mark(mm)
            assert ev.n == TE["mmv"][ci].n
            wait_all(t, P(VE, "fin", ci - 1))
            for a in range(8):
                mm = t.matmul(pba, xrt[ib][:, a * 128:(a + 1) * 128],
                              wbar[:, a * 80:(a + 1) * 80], start=(a == 0), stop=(a == 7))
            ev = s_t.mark(mm)
            assert ev.n == TE["mmba"][ci].n

    return [s_sp2[0].now(), s_sp2[1].now()]


SEM_NAMES = ["dma_a", "dma_b", "w", "vec", "ten", "scl", "spill", "spill2", "mka", "mkb", "mkc", "pool", "ewa", "ewb", "ama", "amb", "amc", "amd"]


def make_sems(nc, ctx):
    return {k: Sem(nc, k, ctx) for k in SEM_NAMES}


def load_ident(nc, block, ctx, t_in, sems):
    e = ctx.enter_context
    identb_sb = e(nc.sbuf_tensor("identb_sb", [128, 128], BF16))
    identf_sb = e(nc.sbuf_tensor("identf_sb", [128, 128], F32))
    s_w = sems["w"]

    @block.gpsimd
    def _(g):
        s_w.mark(g.dma_start(out=identb_sb[:], in_=t_in["identf"]), 16)
        s_w.mark(g.dma_start(out=identf_sb[:], in_=t_in["identf"]), 16)
    t_in["identb_sb"] = identb_sb[:]
    t_in["identf_sb"] = identf_sb[:]


def host_consts():
    return {"identf": np.eye(128, dtype=np.float32)}


def a1_inputs(nc):
    return {
        "x_full": nc.dram_tensor("x_full", [S, D], F32, kind="ExternalInput").ap(),
        "Wq": nc.dram_tensor("Wq", [D, D], F32, kind="ExternalInput").ap(),
        "Wk": nc.dram_tensor("Wk", [D, D], F32, kind="ExternalInput").ap(),
        "Wv": nc.dram_tensor("Wv", [D, D], F32, kind="ExternalInput").ap(),
        "wbar": nc.dram_tensor("wbar", [D, 80], F32, kind="ExternalInput").ap(),
        "identf": nc.dram_tensor("identf", [128, 128], F32, kind="ExternalInput").ap(),
    }


def build_a1_standalone():
    nc = bass.Bass("TRN2", target_bir_lowering=False, debug=False)
    t_in = a1_inputs(nc)
    t_sc = {
        "q_sc": nc.dram_tensor("q_sc", [S, D], BF16, kind="ExternalOutput").ap(),
        "k_sc": nc.dram_tensor("k_sc", [S, D], BF16, kind="ExternalOutput").ap(),
        "v_sc": nc.dram_tensor("v_sc", [S, D], BF16, kind="ExternalOutput").ap(),
        "balg_sc": nc.dram_tensor("balg_sc", [S, 80], F32, kind="ExternalOutput").ap(),
    }
    ctx = ExitStack()
    with nc.Block() as block:
        sems = make_sems(nc, ctx)
        load_ident(nc, block, ctx, t_in, sems)
        evs_done = pass_a1(nc, block, ctx, t_in, t_sc, sems)

        @block.gpsimd
        def _(g):
            wait_all(g, *evs_done)
    ctx.close()
    return nc


def kernel(**inputs):
    raise NotImplementedError("under construction")


A2_SKIP = {"router"}
A2_SU_VARIANT = 0
A2_MAXFULL = 10000
A2_LEVEL = 0


def pass_a2(nc, block, ctx, t_in, t_sc, sems, entry=()):
    e = ctx.enter_context
    identb = t_in["identb_sb"]
    identf = t_in["identf_sb"]
    cumtri = t_in["cumtri_sb"]
    mstrict = t_in["mstrict_sb"]
    mincl = t_in["mincl_sb"]
    ones128 = t_in["ones128_sb"]
    ident4 = t_in["ident4_sb"]
    mstrict8 = t_in["mstrict8_sb"]
    mincl8 = t_in["mincl8_sb"]

    qld = [e(nc.sbuf_tensor(f"a2_q{i}", [128, D], BF16)) for i in range(2)]
    kld = [e(nc.sbuf_tensor(f"a2_k{i}", [128, D], BF16)) for i in range(2)]
    vld = [e(nc.sbuf_tensor(f"a2_v{i}", [128, D], BF16)) for i in range(2)]
    bald = [e(nc.sbuf_tensor(f"a2_ba{i}", [128, 80], F32)) for i in range(2)]

    ba_all = e(nc.sbuf_tensor("a2_ba_all", [128, NCH * 16], F32))
    g_all = e(nc.sbuf_tensor("a2_g_all", [128, NCH * 8 + 128], F32))
    beta_all = e(nc.sbuf_tensor("a2_beta_all", [128, NCH * 8], F32))
    nbeta_all = e(nc.sbuf_tensor("a2_nbeta_all", [128, NCH * 8], F32))
    prow = e(nc.sbuf_tensor("a2_prow", [1, 1024], F32))
    pbc = e(nc.sbuf_tensor("a2_pbc", [128, 1024], F32))
    ptmp = [e(nc.sbuf_tensor(f"a2_ptmp{i}", [128, NCH * 8], F32)) for i in range(2)]

    l_sb = e(nc.sbuf_tensor("a2_l", [128, 128], F32))
    nl = e(nc.sbuf_tensor("a2_nl", [128, 8], F32))
    A_all = e(nc.sbuf_tensor("a2_A", [128, 8], F32))
    lbc = e(nc.sbuf_tensor("a2_lbc", [128, 1024], F32))
    lrow8 = e(nc.sbuf_tensor("a2_lrow8", [64, 128], F32))
    lrow_dram = nc.dram_tensor("a2_lrow_dram", [8, 128], F32, kind="Internal").ap()
    nlexp = e(nc.sbuf_tensor("a2_nlexp", [128, 1024], F32))
    bexp = e(nc.sbuf_tensor("a2_bexp", [128, 1024], F32))
    E_all = e(nc.sbuf_tensor("a2_E", [128, 1024], F32))
    Eb_all = e(nc.sbuf_tensor("a2_Eb", [128, 1024], F32))

    kT_all = e(nc.sbuf_tensor("a2_kT", [128, 1024], BF16))
    qT_all = e(nc.sbuf_tensor("a2_qT", [128, 1024], BF16))
    KK_all = e(nc.sbuf_tensor("a2_KK", [128, 1024], BF16))
    QK_all = e(nc.sbuf_tensor("a2_QK", [128, 1024], BF16))
    Ntil_a = e(nc.sbuf_tensor("a2_N", [128, 1024], BF16))
    NtilT_a = e(nc.sbuf_tensor("a2_NT", [128, 1024], BF16))
    G_a = e(nc.sbuf_tensor("a2_G", [128, 1024], BF16))
    curU_a = e(nc.sbuf_tensor("a2_cu", [128, 1024], BF16))
    NTp_a = [e(nc.sbuf_tensor(f"a2_np{j}", [128, 1024], BF16)) for j in range(5)]
    Qb_a = [e(nc.sbuf_tensor(f"a2_Qb{i}", [128, 1024], BF16)) for i in range(2)]
    Zv_a = e(nc.sbuf_tensor("a2_Zv", [128, 1024], BF16))
    W2T_a = e(nc.sbuf_tensor("a2_W2T", [128, 1024], BF16))
    ak_a = e(nc.sbuf_tensor("a2_ak", [128, 1024], BF16))
    kp_a = e(nc.sbuf_tensor("a2_kp", [128, 1024], BF16))
    delta_a = e(nc.sbuf_tensor("a2_dl", [128, 1024], BF16))
    zt_a = e(nc.sbuf_tensor("a2_zt", [128, 1024], F32))
    oc_a = e(nc.sbuf_tensor("a2_oc", [128, 1024], F32))
    oi_a = e(nc.sbuf_tensor("a2_oi", [128, 1024], F32))
    o1_a = e(nc.sbuf_tensor("a2_o1", [128, 1024], F32))
    su_a = e(nc.sbuf_tensor("a2_su", [128, 1024], F32))
    Sst_a = e(nc.sbuf_tensor("a2_S", [128, 1024], BF16))
    S32_a = [e(nc.sbuf_tensor(f"a2_S32{i}", [128, 1024], F32)) for i in range(2)]
    o_out = [e(nc.sbuf_tensor(f"a2_oo{i}", [128, D], BF16)) for i in range(2)]

    aff = e(nc.sbuf_tensor("a2_aff", [128, 64], F32))
    afft = e(nc.sbuf_tensor("a2_afft", [128, 64], F32))
    m8t = e(nc.sbuf_tensor("a2_m8", [128, 8], F32))
    rem = e(nc.sbuf_tensor("a2_rem", [128, 64], F32))
    maskf = e(nc.sbuf_tensor("a2_maskf", [128, 64], F32))
    ssum = e(nc.sbuf_tensor("a2_ssum", [128, 2], F32))
    mtout = [e(nc.sbuf_tensor(f"a2_mt{i}", [64, 128], F32)) for i in range(2)]

    PB = [e(nc.psum_tensor(f"a2_pb{i}", [128, 512], F32)) for i in range(8)]
    BT = [BankTracker(PB[i]) for i in range(8)]

    sch = Sched(sems)
    sch.entry = [w for w in entry if w is not None]
    TEN, VEC, SCL = "ten", "vec", "scl"

    def vop(fn):
        return lambda v: (v.drain(), fn(v))[1]

    def hs(h):
        return slice(h * 128, (h + 1) * 128)

    m_ld_alog = sch.dma("g", lambda g: g.dma_start(out=prow[0:1, 0:512], in_=t_in["alog_t"]),
                        sem="w")
    m_ld_dtb = sch.dma("g", lambda g: g.dma_start(out=prow[0:1, 512:1024], in_=t_in["dtb_t"]),
                       sem="w")
    m_ld_ba = sch.dma("g", lambda g: g.dma_start(
        out=ba_all[:].rearrange("p (c j) -> p c j", j=16),
        in_=t_sc["balg_sc"][:, 0:16].rearrange("(c p) j -> p c j", p=128)), sem="w")
    m_expal = sch.add(SCL, lambda s: s.activation(out=prow[0:1, 0:512], in_=prow[0:1, 0:512],
                                                  func=AF.Exp),
                      waits=[m_ld_ba], sem="scl")
    m_bc1 = sch.add(TEN, lambda t: t.matmul(PB[0][:, 0:512], t_in["onesrow_sb"],
                                            prow[0:1, 0:512], start=True, stop=True),
                    waits=[m_expal, m_ld_ba] + BT[0].write_waits(), sem="ten")
    m_bc2 = sch.add(TEN, lambda t: t.matmul(PB[1][:, 0:512], t_in["onesrow_sb"],
                                            prow[0:1, 512:1024], start=True, stop=True),
                    waits=[m_ld_ba] + BT[1].write_waits(), sem="ten")
    m_nea = sch.add(VEC, vop(lambda v: v.tensor_scalar(pbc[:, 0:512], PB[0][:, 0:512], -1.0,
                                                       None, op0=ALU.mult)),
                    waits=[m_bc1], sem="vec")
    BT[0].add_use(m_nea)
    m_dtb = sch.add(VEC, vop(lambda v: v.tensor_copy(pbc[:, 512:1024], PB[1][:, 0:512])),
                    waits=[m_bc2], sem="vec")
    BT[1].add_use(m_dtb)
    m_eb = sch.add(SCL, lambda s: s.activation(
        out=ptmp[0][:].rearrange("p (c j) -> p c j", j=8),
        in_=ba_all[:].rearrange("p (c j) -> p c j", j=16)[:, :, 0:8],
        func=AF.Exp, scale=-1.0), waits=[m_ld_ba], sem="scl")
    m_eb1 = sch.add(VEC, vop(lambda v: v.tensor_scalar(ptmp[0][:], ptmp[0][:], 1.0, None,
                                                       op0=ALU.add)),
                    waits=[m_eb], sem="vec")
    m_beta = sch.add(VEC, vop(lambda v: v.reciprocal(beta_all[:], ptmp[0][:])),
                     waits=[m_eb1], sem="vec")
    m_nbeta = sch.add(VEC, vop(lambda v: v.tensor_scalar(nbeta_all[:], beta_all[:], -1.0,
                                                         None, op0=ALU.mult)),
                      waits=[m_beta], sem="vec")
    dtb_rep = pbc[:, 512:512 + NCH * 8].rearrange("p (c j) -> p c j", j=8, c=NCH)
    m_ain = sch.add(VEC, vop(lambda v: v.tensor_tensor(
        ptmp[0][:].rearrange("p (c j) -> p c j", j=8),
        ba_all[:].rearrange("p (c j) -> p c j", j=16)[:, :, 8:16],
        dtb_rep, op=ALU.add)), waits=[m_ld_ba, m_dtb], sem="vec")
    m_absx = sch.add(SCL, lambda s: s.activation(out=ptmp[1][:], in_=ptmp[0][:], func=AF.Abs),
                     waits=[m_ain], sem="scl")
    m_emx = sch.add(SCL, lambda s: s.activation(out=ptmp[1][:], in_=ptmp[1][:], func=AF.Exp,
                                                scale=-1.0),
                    waits=[m_absx], sem="scl")
    m_lp = sch.add(VEC, vop(lambda v: v.tensor_scalar(ptmp[1][:], ptmp[1][:], 1.0, None,
                                                      op0=ALU.add)),
                   waits=[m_emx], sem="vec")
    m_ln = sch.add(SCL, lambda s: s.activation(out=ptmp[1][:], in_=ptmp[1][:], func=AF.Ln),
                   waits=[m_lp], sem="scl")
    m_mx = sch.add(VEC, vop(lambda v: v.tensor_scalar(ptmp[0][:], ptmp[0][:], 0.0, None,
                                                      op0=ALU.max)),
                   waits=[m_ln], sem="vec")
    m_sp = sch.add(VEC, vop(lambda v: v.tensor_tensor(ptmp[0][:], ptmp[0][:], ptmp[1][:],
                                                      op=ALU.add)),
                   waits=[m_mx], sem="vec")
    negea_rep = pbc[:, 0:NCH * 8].rearrange("p (c j) -> p c j", j=8, c=NCH)
    m_gall = sch.add(VEC, vop(lambda v: v.tensor_tensor(
        g_all[:, 0:NCH * 8].rearrange("p (c j) -> p c j", j=8),
        ptmp[0][:].rearrange("p (c j) -> p c j", j=8),
        negea_rep, op=ALU.mult)), waits=[m_sp, m_nea], sem="vec")
    m_sinit = sch.add(VEC, vop(lambda v: v.memset(Sst_a[:], 0.0)), sem="vec")
    m_sinit = sch.add(VEC, vop(lambda v: v.memset(S32_a[0][:], 0.0)), waits=[m_sinit],
                      sem="vec")
    m_ms0 = sch.add(VEC, vop(lambda v: v.memset(PB[0][:, 0:128], 0.0)),
                    waits=BT[0].write_waits(), sem="vec")
    BT[0].add_use(m_ms0)
    m_ms1 = sch.add(VEC, vop(lambda v: v.memset(PB[0][0:64, 128:256], 0.0)),
                    sem="vec")
    BT[0].add_use(m_ms1)
    m_msg = sch.add(VEC, vop(lambda v: v.memset(g_all[:, NCH * 8:NCH * 8 + 128], 0.0)),
                    sem="vec")

    luse = {"q": [[], []], "k": [[], []], "v": [[], []], "ba": [[], []],
            "oo": [[], []], "mt": [[], []]}
    pu = {}
    last_S = m_sinit
    pu_lbc_rd = None
    do_router = "router" not in A2_SKIP

    for ci in range(NCH):
        ib = ci % 2
        ldsem = "dma_a" if ib == 0 else "dma_b"
        spsem = "spill" if ib == 0 else "spill2"
        r = slice(ci * 128, (ci + 1) * 128)
        m_lq = sch.dma("y", lambda y, r=r, ib=ib: y.dma_start(out=qld[ib][:],
                                                              in_=t_sc["q_sc"][r, :]),
                       waits=luse["q"][ib], sem=ldsem)
        m_lk = sch.dma("y", lambda y, r=r, ib=ib: y.dma_start(out=kld[ib][:],
                                                              in_=t_sc["k_sc"][r, :]),
                       waits=luse["k"][ib], sem=ldsem)
        m_lv = sch.dma("y", lambda y, r=r, ib=ib: y.dma_start(out=vld[ib][:],
                                                              in_=t_sc["v_sc"][r, :]),
                       waits=luse["v"][ib], sem=ldsem)
        m_lba = sch.dma("y", lambda y, r=r, ib=ib: y.dma_start(out=bald[ib][:],
                                                               in_=t_sc["balg_sc"][r, :]),
                        waits=luse["ba"][ib], sem=ldsem)
        for kk_ in ["q", "k", "v", "ba"]:
            luse[kk_][ib] = []

        m_lmm = sch.add(TEN, lambda t, ci=ci: t.matmul(PB[0][:, 0:128], cumtri,
                                                       g_all[:, ci * 8:ci * 8 + 128],
                                                       start=True, stop=True),
                        waits=[m_gall, m_msg] + BT[0].write_waits(), sem="ten")
        m_lsb = sch.add(VEC, vop(lambda v: v.tensor_copy(l_sb[:], PB[0][:, 0:128])),
                        waits=[m_lmm, pu.get("l_sb")], sem="vec")
        BT[0].add_use(m_lsb)
        m_nl = sch.add(VEC, vop(lambda v: v.tensor_scalar(nl[:], l_sb[:, 0:8], -1.0, None,
                                                          op0=ALU.mult)),
                       waits=[m_lsb, pu.get("nl")], sem="vec")
        m_A = sch.add(SCL, lambda s: s.activation(out=A_all[:], in_=l_sb[:, 0:8],
                                                  func=AF.Exp),
                      waits=[m_lsb, pu.get("A_all")], sem="scl")
        m_lt = sch.add(TEN, lambda t: t.transpose(PB[0][0:64, 128:256], l_sb[:, 0:64],
                                                  identf),
                       waits=[m_lsb] + BT[0].write_waits(), sem="ten")
        m_lr8 = sch.add(SCL, lambda s: s.copy(out=lrow8[:], in_=PB[0][0:64, 128:256]),
                        waits=[m_lt, pu.get("lrow8")], sem="scl")
        BT[0].add_use(m_lr8)
        m_lw_ = sch.dma("g", lambda g: g.dma_start(out=lrow_dram, in_=lrow8[0:8, :]),
                        waits=[m_lr8, pu_lbc_rd], sem="mka")
        m_lbc = sch.dma("g", lambda g: g.dma_start(
            out=lbc[:], in_=lrow_dram.rearrange("h j -> (h j)")[None, :]
            .broadcast_to([128, 1024])), waits=[m_lw_] + pu.get("lbc", []), sem="mka")
        pu_lbc_rd = m_lbc
        m_nx = None
        for h in range(H):
            m_nx = sch.add(VEC, vop(lambda v, h=h: v.tensor_scalar(
                nlexp[:, hs(h)], ones128, nl[:, h:h + 1], None, op0=ALU.mult)),
                waits=[m_nl] + ([pu.get("nlexp")] if h == 0 else []), sem="vec")
        m_bx = None
        for h in range(H):
            m_bx = sch.add(VEC, vop(lambda v, h=h, ci=ci: v.tensor_scalar(
                bexp[:, hs(h)], ones128, nbeta_all[:, ci * 8 + h:ci * 8 + h + 1], None,
                op0=ALU.mult)),
                waits=[m_nbeta] + ([pu.get("bexp")] if h == 0 else []), sem="vec")
        m_ecl = sch.add(VEC, vop(lambda v: v.tensor_tensor(E_all[:], lbc[:], nlexp[:],
                                                           op=ALU.add)),
                        waits=[m_lbc, m_nx, pu.get("E_all")], sem="vec")
        m_ecl2 = sch.add(VEC, vop(lambda v: v.tensor_scalar(E_all[:], E_all[:], 0.0, None,
                                                            op0=ALU.min)),
                         waits=[m_ecl], sem="vec")
        m_E = sch.add(SCL, lambda s: s.activation(out=E_all[:], in_=E_all[:], func=AF.Exp),
                      waits=[m_ecl2], sem="scl")
        m_Eb = sch.add(VEC, vop(lambda v: v.tensor_tensor(Eb_all[:], E_all[:], bexp[:],
                                                          op=ALU.mult)),
                       waits=[m_E, m_bx, pu.get("Eb_all")], sem="vec")

        mm = None
        for h in range(H):
            bt = BT[1 + h // 2]
            w0 = ([m_lba] + bt.write_waits() + [pu.get("kT_all") if h == 0 else None,
                                               pu.get("qT_all") if h == 0 else None]) \
                if h % 2 == 0 else []
            sch.add(TEN, lambda t, h=h, ib=ib, bt=bt: t.matmul(
                bt.ap[:, (h % 2) * 128:(h % 2) * 128 + 128], kld[ib][:, hs(h)], identb,
                start=True, stop=True), waits=w0, sem="ten")
            mm = sch.add(TEN, lambda t, h=h, ib=ib, bt=bt: t.matmul(
                bt.ap[:, 256 + (h % 2) * 128:256 + (h % 2) * 128 + 128],
                qld[ib][:, hs(h)], identb, start=True, stop=True), sem="ten")
        luse["q"][ib].append(mm)
        luse["k"][ib].append(mm)
        m_kqev = []
        for bi in range(4):
            ev1 = sch.add(SCL, lambda s, bi=bi: s.copy(out=kT_all[:, bi * 256:(bi + 1) * 256],
                                                       in_=PB[1 + bi][:, 0:256]),
                          waits=[mm], sem="scl")
            ev2 = sch.add(SCL, lambda s, bi=bi: s.copy(out=qT_all[:, bi * 256:(bi + 1) * 256],
                                                       in_=PB[1 + bi][:, 256:512]),
                          sem="scl")
            BT[1 + bi].add_use(ev2)
            m_kqev.append(ev2)
        for h in range(H):
            bt = BT[1 + h // 2]
            w0 = ([m_kqev[3]] + bt.write_waits() + [pu.get("KK_all") if h == 0 else None,
                                                    pu.get("QK_all") if h == 0 else None]) \
                if h % 2 == 0 else []
            sch.add(TEN, lambda t, h=h, bt=bt: t.matmul(
                bt.ap[:, (h % 2) * 128:(h % 2) * 128 + 128], kT_all[:, hs(h)],
                kT_all[:, hs(h)], start=True, stop=True), waits=w0, sem="ten")
            mm = sch.add(TEN, lambda t, h=h, bt=bt: t.matmul(
                bt.ap[:, 256 + (h % 2) * 128:256 + (h % 2) * 128 + 128],
                kT_all[:, hs(h)], qT_all[:, hs(h)], start=True, stop=True), sem="ten")
        m_kkev = []
        for bi in range(4):
            ev1 = sch.add(SCL, lambda s, bi=bi: s.copy(out=KK_all[:, bi * 256:(bi + 1) * 256],
                                                       in_=PB[1 + bi][:, 0:256]),
                          waits=[mm], sem="scl")
            ev2 = sch.add(SCL, lambda s, bi=bi: s.copy(out=QK_all[:, bi * 256:(bi + 1) * 256],
                                                       in_=PB[1 + bi][:, 256:512]),
                          sem="scl")
            BT[1 + bi].add_use(ev2)
            m_kkev.append(ev2)
        m_n1 = sch.add(VEC, vop(lambda v: v.tensor_tensor(zt_a[:], Eb_all[:], KK_all[:],
                                                          op=ALU.mult)),
                       waits=[m_Eb, m_kkev[3], pu.get("zt_a")], sem="vec")
        m_n2 = sch.add(VEC, vop(lambda v: v.tensor_tensor(
            Ntil_a[:], zt_a[:], mstrict8, op=ALU.mult)),
            waits=[m_n1, pu.get("Ntil_a")], sem="vec")
        m_g1 = sch.add(VEC, vop(lambda v: v.tensor_tensor(zt_a[:], E_all[:], QK_all[:],
                                                          op=ALU.mult)),
                       waits=[m_n1], sem="vec")
        m_g2 = sch.add(VEC, vop(lambda v: v.tensor_tensor(G_a[:], zt_a[:], mincl8,
                                                          op=ALU.mult)),
                       waits=[m_g1, pu.get("G_a")], sem="vec")
        for h in range(H):
            bt = BT[5] if h < 4 else BT[6]
            w0 = ([m_n2] + bt.write_waits() + [pu.get("NtilT_a") if h == 0 else None]) \
                if h % 4 == 0 else []
            mm = sch.add(TEN, lambda t, h=h, bt=bt: t.matmul(
                bt.ap[:, (h % 4) * 128:(h % 4) * 128 + 128], Ntil_a[:, hs(h)], identb,
                start=True, stop=True), waits=w0, sem="ten")
        m_ntev = []
        for bi in range(2):
            evv = sch.add(SCL, lambda s, bi=bi: s.copy(
                out=NtilT_a[:, bi * 512:(bi + 1) * 512], in_=PB[5 + bi][:]),
                waits=[mm], sem="scl")
            BT[5 + bi].add_use(evv)
            m_ntev.append(evv)
        m_sqev = None
        for j in range(5):
            aU = Ntil_a if j == 0 else curU_a
            aT = NtilT_a if j == 0 else NTp_a[j - 1]
            dep = m_ntev[1] if j == 0 else m_sqev
            for h in range(H):
                btu = BT[1] if h < 4 else BT[2]
                btt = BT[3] if h < 4 else BT[4]
                w0 = []
                if h == 0:
                    w0 = [dep] + btu.write_waits() + btt.write_waits() + \
                        BT[2].write_waits() + BT[4].write_waits()
                    if j == 0:
                        w0 += [pu.get("curU_a"), pu.get("NTp0")]
                sch.add(TEN, lambda t, h=h, aU=aU, aT=aT, btu=btu: t.matmul(
                    btu.ap[:, (h % 4) * 128:(h % 4) * 128 + 128], aT[:, hs(h)], aU[:, hs(h)],
                    start=True, stop=True), waits=w0, sem="ten")
                mm = sch.add(TEN, lambda t, h=h, aU=aU, aT=aT, btt=btt: t.matmul(
                    btt.ap[:, (h % 4) * 128:(h % 4) * 128 + 128], aU[:, hs(h)], aT[:, hs(h)],
                    start=True, stop=True), sem="ten")
            ev1 = sch.add(SCL, lambda s: s.copy(out=curU_a[:, 0:512], in_=PB[1][:]),
                          waits=[mm], sem="scl")
            ev2 = sch.add(SCL, lambda s: s.copy(out=curU_a[:, 512:1024], in_=PB[2][:]),
                          sem="scl")
            ev3 = sch.add(SCL, lambda s, j=j: s.copy(out=NTp_a[j][:, 0:512], in_=PB[3][:]),
                          sem="scl")
            ev4 = sch.add(SCL, lambda s, j=j: s.copy(out=NTp_a[j][:, 512:1024], in_=PB[4][:]),
                          sem="scl")
            BT[1].add_use(ev1)
            BT[2].add_use(ev2)
            BT[3].add_use(ev3)
            BT[4].add_use(ev4)
            m_sqev = ev4
        for h in range(H):
            bt = BT[5] if h < 4 else BT[6]
            w0 = ([m_sqev] + bt.write_waits()) if h % 4 == 0 else []
            mm = sch.add(TEN, lambda t, h=h, bt=bt: t.matmul(
                bt.ap[:, (h % 4) * 128:(h % 4) * 128 + 128], NTp_a[4][:, hs(h)],
                curU_a[:, hs(h)], start=True, stop=True), waits=w0, sem="ten")
        m_q = sch.add(VEC, vop(lambda v: v.tensor_tensor(Qb_a[0][:, 0:512], PB[5][:],
                                                         ident4, op=ALU.add)),
                      waits=[mm, pu.get("Qb0")], sem="vec")
        m_q = sch.add(VEC, vop(lambda v: v.tensor_tensor(Qb_a[0][:, 512:1024], PB[6][:],
                                                         ident4, op=ALU.add)),
                      waits=[m_q], sem="vec")
        BT[5].add_use(m_q)
        BT[6].add_use(m_q)
        for lev in range(6):
            src_ = NTp_a[4 - lev] if lev < 5 else NtilT_a
            qin = Qb_a[lev % 2]
            qout = Qb_a[(lev + 1) % 2]
            for h in range(H):
                bt = BT[5] if h < 4 else BT[6]
                w0 = []
                if h == 0:
                    w0 = [m_q] + bt.write_waits() + BT[6].write_waits()
                    if lev == 0:
                        w0.append(pu.get("Qb1"))
                mm = sch.add(TEN, lambda t, h=h, src_=src_, qin=qin, bt=bt: t.matmul(
                    bt.ap[:, (h % 4) * 128:(h % 4) * 128 + 128], src_[:, hs(h)],
                    qin[:, hs(h)], start=True, stop=True), waits=w0, sem="ten")
            mq1 = sch.add(VEC, vop(lambda v, qin=qin, qout=qout: v.tensor_tensor(
                qout[:, 0:512], PB[5][:], qin[:, 0:512], op=ALU.add)),
                waits=[mm], sem="vec")
            m_q = sch.add(VEC, vop(lambda v, qin=qin, qout=qout: v.tensor_tensor(
                qout[:, 512:1024], PB[6][:], qin[:, 512:1024], op=ALU.add)),
                waits=[mq1], sem="vec")
            BT[5].add_use(m_q)
            BT[6].add_use(m_q)
        m_T = m_q
        m_negA8 = sch.add(VEC, vop(lambda v: v.tensor_scalar(nl[:], A_all[:], -1.0, None,
                                                             op0=ALU.mult)),
                          waits=[m_A, m_nx], sem="vec")
        m_akl = None
        for h in range(H):
            m_akl = sch.add(VEC, vop(lambda v, h=h, ib=ib: v.tensor_scalar(
                ak_a[:, hs(h)], kld[ib][:, hs(h)], nl[:, h:h + 1], None, op0=ALU.mult)),
                waits=[m_negA8] + ([pu.get("ak_a"), m_lba] if h == 0 else []), sem="vec")
        m_kpl = None
        for h in range(H):
            m_kpl = sch.add(VEC, vop(lambda v, h=h, ib=ib: v.tensor_scalar(
                kp_a[:, hs(h)], kld[ib][:, hs(h)],
                E_all[:, h * 128 + 127:h * 128 + 128], None, op0=ALU.mult)),
                waits=[m_akl] + ([pu.get("kp_a")] if h == 0 else []), sem="vec")
        luse["k"][ib].append(m_kpl)
        for h in range(H):
            btz = BT[1 + h // 2]
            w0 = ([m_T, m_lba, m_akl] + btz.write_waits()) if h % 2 == 0 else []
            sch.add(TEN, lambda t, h=h, ib=ib, btz=btz: t.matmul(
                btz.ap[:, (h % 2) * 128:(h % 2) * 128 + 128], Qb_a[0][:, hs(h)],
                vld[ib][:, hs(h)], start=True, stop=True), waits=w0, sem="ten")
            mm = sch.add(TEN, lambda t, h=h, btz=btz: t.matmul(
                btz.ap[:, 256 + (h % 2) * 128:256 + (h % 2) * 128 + 128], ak_a[:, hs(h)],
                Qb_a[0][:, hs(h)], start=True, stop=True), sem="ten")
        luse["v"][ib].append(mm)
        m_zwev = []
        for bi in range(4):
            ev1 = sch.add(SCL, lambda s, bi=bi: s.copy(out=Zv_a[:, bi * 256:(bi + 1) * 256],
                                                       in_=PB[1 + bi][:, 0:256]),
                          waits=[mm, pu.get("Zv_a") if bi == 0 else None], sem="scl")
            ev2 = sch.add(SCL, lambda s, bi=bi: s.copy(out=W2T_a[:, bi * 256:(bi + 1) * 256],
                                                       in_=PB[1 + bi][:, 256:512]),
                          waits=[pu.get("W2T_a") if bi == 0 else None], sem="scl")
            BT[1 + bi].add_use(ev2)
            m_zwev.append(ev2)
        for h in range(H):
            bt = BT[5] if h < 4 else BT[6]
            w0 = ([m_zwev[3], last_S] + bt.write_waits()) if h % 4 == 0 else []
            sch.add(TEN, lambda t, h=h, bt=bt: t.matmul(
                bt.ap[:, (h % 4) * 128:(h % 4) * 128 + 128], W2T_a[:, hs(h)],
                Sst_a[:, hs(h)], start=True, stop=False), waits=w0, sem="ten")
            mm = sch.add(TEN, lambda t, h=h, bt=bt: t.matmul(
                bt.ap[:, (h % 4) * 128:(h % 4) * 128 + 128], identb, Zv_a[:, hs(h)],
                start=False, stop=True), sem="ten")
        ev1 = sch.add(SCL, lambda s: s.copy(out=zt_a[:, 0:512], in_=PB[5][:]),
                      waits=[mm, m_g2], sem="scl")
        ev2 = sch.add(SCL, lambda s: s.copy(out=zt_a[:, 512:1024], in_=PB[6][:]),
                      sem="scl")
        BT[5].add_use(ev1)
        BT[6].add_use(ev2)
        m_dl = None
        for h in range(H):
            m_dl = sch.add(VEC, vop(lambda v, h=h, ci=ci: v.tensor_scalar(
                delta_a[:, hs(h)], zt_a[:, hs(h)],
                beta_all[:, ci * 8 + h:ci * 8 + h + 1], None, op0=ALU.mult)),
                waits=[ev2 if h == 0 else None, pu.get("delta_a") if h == 0 else None],
                sem="vec")
        for h in range(H):
            bto = BT[1 + h // 2]
            w0 = ([m_dl] + bto.write_waits()) if h % 2 == 0 else []
            sch.add(TEN, lambda t, h=h, bto=bto: t.matmul(
                bto.ap[:, (h % 2) * 128:(h % 2) * 128 + 128], qT_all[:, hs(h)],
                Sst_a[:, hs(h)], start=True, stop=True), waits=w0, sem="ten")
            mm = sch.add(TEN, lambda t, h=h, bto=bto: t.matmul(
                bto.ap[:, 256 + (h % 2) * 128:256 + (h % 2) * 128 + 128], G_a[:, hs(h)],
                delta_a[:, hs(h)], start=True, stop=True), sem="ten")
        m_ocv = []
        for bi in range(4):
            ev1 = sch.add(SCL, lambda s, bi=bi: s.copy(out=oc_a[:, bi * 256:(bi + 1) * 256],
                                                       in_=PB[1 + bi][:, 0:256]),
                          waits=[mm, pu.get("oc_a") if bi == 0 else None], sem="scl")
            ev2 = sch.add(SCL, lambda s, bi=bi: s.copy(out=oi_a[:, bi * 256:(bi + 1) * 256],
                                                       in_=PB[1 + bi][:, 256:512]),
                          waits=[pu.get("oi_a") if bi == 0 else None], sem="scl")
            BT[1 + bi].add_use(ev2)
            m_ocv.append(ev2)
        m_o1 = None
        for h in range(H):
            m_o1 = sch.add(VEC, vop(lambda v, h=h: v.tensor_scalar(
                o1_a[:, hs(h)], oc_a[:, hs(h)], A_all[:, h:h + 1], None, op0=ALU.mult)),
                waits=[m_ocv[3] if h == 0 else None, pu.get("o1_a") if h == 0 else None],
                sem="vec")
        m_ov = sch.add(VEC, vop(lambda v, ib=ib: v.tensor_tensor(o_out[ib][:], o1_a[:],
                                                                 oi_a[:], op=ALU.add)),
                       waits=[m_o1] + luse["oo"][ib], sem="vec")
        for h in range(H):
            bt = BT[5] if h < 4 else BT[6]
            w0 = ([m_dl, m_kpl] + bt.write_waits()) if h % 4 == 0 else []
            mm = sch.add(TEN, lambda t, h=h, bt=bt: t.matmul(
                bt.ap[:, (h % 4) * 128:(h % 4) * 128 + 128], kp_a[:, hs(h)],
                delta_a[:, hs(h)], start=True, stop=True), waits=w0, sem="ten")
        ev1 = sch.add(SCL, lambda s: s.copy(out=su_a[:, 0:512], in_=PB[5][:]),
                      waits=[mm, pu.get("su_a")], sem="scl")
        ev2 = sch.add(SCL, lambda s: s.copy(out=su_a[:, 512:1024], in_=PB[6][:]),
                      sem="scl")
        BT[5].add_use(ev1)
        BT[6].add_use(ev2)
        m_sup = None
        for h in range(H):
            m_sup = sch.add(VEC, vop(lambda v, h=h, ci=ci: v.scalar_tensor_tensor(
                out=S32_a[(ci + 1) % 2][:, hs(h)], in0=S32_a[ci % 2][:, hs(h)],
                scalar=E_all[:, h * 128 + 127:h * 128 + 128], in1=su_a[:, hs(h)],
                op0=ALU.mult, op1=ALU.add)),
                waits=[ev2 if h == 0 else None, m_sinit if ci == 0 and h == 0 else None],
                sem="vec")
        last_S = sch.add(VEC, vop(lambda v, ci=ci: v.tensor_copy(Sst_a[:],
                                                                 S32_a[(ci + 1) % 2][:])),
                         waits=[m_sup], sem="vec")
        pu["l_sb"] = m_lt
        pu["nl"] = m_akl
        pu["A_all"] = m_o1
        pu["lrow8"] = m_lw_
        pu["lbc"] = [m_ecl]
        pu["nlexp"] = m_ecl
        pu["bexp"] = m_Eb
        pu["E_all"] = m_sup
        pu["Eb_all"] = m_n1
        pu["kT_all"] = mm
        pu["qT_all"] = mm
        pu["KK_all"] = m_n1
        pu["QK_all"] = m_g1
        pu["Ntil_a"] = m_sqev
        pu["NtilT_a"] = mm
        pu["G_a"] = mm
        pu["curU_a"] = mm
        pu["NTp0"] = mm
        pu["Qb0"] = mm
        pu["Qb1"] = mm
        pu["Zv_a"] = mm
        pu["W2T_a"] = mm
        pu["ak_a"] = mm
        pu["kp_a"] = mm
        pu["delta_a"] = mm
        pu["zt_a"] = m_g2 if False else m_dl
        pu["oc_a"] = m_o1
        pu["oi_a"] = m_ov
        pu["o1_a"] = m_ov
        pu["su_a"] = m_sup

        if do_router:
            m_elg = sch.add(SCL, lambda s, ib=ib: s.activation(
                out=afft[:], in_=bald[ib][:, 16:80], func=AF.Exp, scale=-1.0),
                waits=[m_lba, pu.get("afft")], sem="scl")
            luse["ba"][ib].append(m_elg)
            mr1 = sch.add(VEC, vop(lambda v: v.tensor_scalar(afft[:], afft[:], 1.0, None,
                                                             op0=ALU.add)),
                          waits=[m_elg, pu.get("maskf_rd")], sem="vec")
            mr2 = sch.add(VEC, vop(lambda v: v.reciprocal(aff[:], afft[:])),
                          waits=[mr1], sem="vec")
            mr3 = sch.add(VEC, vop(lambda v: v.max(out=m8t[:], in_=aff[:])),
                          waits=[mr2], sem="vec")
            mr4 = sch.add(VEC, vop(lambda v: v.match_replace(out=rem[:], in_to_replace=m8t[:],
                                                             in_values=aff[:],
                                                             imm_value=0.0)),
                          waits=[mr3], sem="vec")
            mr5 = sch.add(VEC, vop(lambda v: v.tensor_tensor(rem[:], aff[:], rem[:],
                                                             op=ALU.subtract)),
                          waits=[mr4], sem="vec")
            mr6 = sch.add(VEC, vop(lambda v: v.reduce_sum(ssum[:, 0:1], m8t[:], axis=AX.X)),
                          waits=[mr4], sem="vec")
            mr7 = sch.add(VEC, vop(lambda v: v.tensor_scalar(ssum[:, 0:1], ssum[:, 0:1],
                                                             1e-9, None, op0=ALU.add)),
                          waits=[mr6], sem="vec")
            mr8 = sch.add(VEC, vop(lambda v: v.reciprocal(ssum[:, 1:2], ssum[:, 0:1])),
                          waits=[mr7], sem="vec")
            m_mask = sch.add(VEC, vop(lambda v: v.tensor_scalar(maskf[:], rem[:],
                                                                ssum[:, 1:2], None,
                                                                op0=ALU.mult)),
                             waits=[mr5, mr8], sem="vec")
            pu["afft"] = mr2
            m_mt = sch.add(TEN, lambda t: t.matmul(BT[7].ap[0:64, 0:128], maskf[:], identf,
                                                   start=True, stop=True),
                           waits=[m_mask] + BT[7].write_waits(), sem="ten")
            pu["maskf_rd"] = m_mt
            m_mtev = sch.add(SCL, lambda s, ib=ib: s.copy(out=mtout[ib][:],
                                                          in_=BT[7].ap[0:64, 0:128]),
                             waits=[m_mt] + luse["mt"][ib], sem="scl")
            BT[7].add_use(m_mtev)
            luse["mt"][ib] = []

        mo_sp = sch.dma("y", lambda y, r=r, ib=ib: y.dma_start(out=t_sc["o_sc"][r, :],
                                                               in_=o_out[ib][:]),
                        waits=[m_ov], sem=spsem)
        if do_router:
            mt_sp = sch.dma("y", lambda y, ci=ci, ib=ib: y.dma_start(
                out=t_sc["maskT_sc"][:, ci * 128:(ci + 1) * 128], in_=mtout[ib][:]),
                waits=[m_mtev], sem=spsem)
            luse["oo"][ib] = [mt_sp]
            luse["mt"][ib] = [mt_sp]
        else:
            luse["oo"][ib] = [mo_sp]

    sch.plan()

    @block.tensor
    def _(t):
        sch.emit_engine(TEN, t)

    @block.vector
    def _(v):
        sch.emit_engine(VEC, v)

    @block.scalar
    def _(s):
        sch.emit_engine(SCL, s)

    @block.sync
    def _(y):
        sch.emit_engine("y", y)

    @block.gpsimd
    def _(g):
        sch.emit_engine("g", g)

    return [Ev(sems[s], sch.final_counts[s]) for s in ["spill", "spill2"]]



def pass_r(nc, block, ctx, t_in, t_sc, sems, entry=()):
    e = ctx.enter_context
    identf = t_in["identf_sb"]
    bal = [e(nc.sbuf_tensor(f"r_ba{i}", [128, 80], F32)) for i in range(2)]
    lg = e(nc.sbuf_tensor("r_lg", [128, 64], F32))
    sil = e(nc.sbuf_tensor("r_sil", [128, 64], F32))
    sig = e(nc.sbuf_tensor("r_sig", [128, 64], F32))
    rcp = e(nc.sbuf_tensor("r_rcp", [128, 64], F32))
    m8t = e(nc.sbuf_tensor("r_m8", [128, 8], F32))
    rem = e(nc.sbuf_tensor("r_rem", [128, 64], F32))
    selm = e(nc.sbuf_tensor("r_sel", [128, 64], F32))
    sqt = e(nc.sbuf_tensor("r_sq", [128, 64], F32))
    cond = e(nc.sbuf_tensor("r_cond", [128, 64], mybir.dt.uint8))
    halfc = e(nc.sbuf_tensor("r_half", [128, 64], F32))
    ssum = e(nc.sbuf_tensor("r_ssum", [128, 2], F32))
    maskf = e(nc.sbuf_tensor("r_maskf", [128, 64], F32))
    mtout = [e(nc.sbuf_tensor(f"r_mt{i}", [64, 128], F32)) for i in range(2)]
    hsel = e(nc.sbuf_tensor("r_hsel", [128, 2], F32))
    mlo = e(nc.sbuf_tensor("r_mlo", [64, 512], F32))
    mhi = e(nc.sbuf_tensor("r_mhi", [64, 512], F32))
    msel = e(nc.sbuf_tensor("r_msel", [64, 512], F32))
    PBm = e(nc.psum_tensor("r_pb", [128, 512], F32))
    BTm = BankTracker(PBm)

    sch = Sched(sems)
    sch.entry = [w for w in entry if w is not None]

    def vop(fn):
        return lambda v: (v.drain(), fn(v))[1]

    m_hs = sch.dma("g", lambda g: g.dma_start(out=hsel[:], in_=t_in["hselr"]), sem="w")
    m_half = sch.add("vec", vop(lambda v: v.memset(halfc[:], 0.5)), sem="vec")
    pu = {}
    luse_ba = [[], []]
    luse_mt = [[], []]
    for ci in range(NCH):
        ib = ci % 2
        ldsem = "dma_a" if ib == 0 else "dma_b"
        spsem = "spill" if ib == 0 else "spill2"
        r = slice(ci * 128, (ci + 1) * 128)
        m_lb = sch.dma("y", lambda y, r=r, ib=ib: y.dma_start(out=bal[ib][:],
                                                              in_=t_sc["balg_sc"][r, :]),
                       waits=luse_ba[ib], sem=ldsem)
        luse_ba[ib] = []
        m1 = sch.add("vec", vop(lambda v, ib=ib: v.tensor_copy(lg[:], bal[ib][:, 16:80])),
                     waits=[m_lb, pu.get("lg")], sem="vec")
        luse_ba[ib].append(m1)
        m2 = sch.add("scl", lambda s: s.activation(out=sil[:], in_=lg[:], func=AF.Silu),
                     waits=[m1, pu.get("sil")], sem="scl")
        m3r = sch.add("vec", vop(lambda v: v.reciprocal(rcp[:], lg[:])),
                      waits=[m1, pu.get("sig")], sem="vec")
        m3 = sch.add("vec", vop(lambda v: v.tensor_tensor(sig[:], sil[:], rcp[:],
                                                          op=ALU.mult)),
                     waits=[m2, m3r], sem="vec")
        m4 = sch.add("vec", vop(lambda v: v.tensor_tensor(sqt[:], lg[:], lg[:], op=ALU.mult)),
                     waits=[m3], sem="vec")
        m5 = sch.add("vec", vop(lambda v: v.tensor_scalar(cond[:], sqt[:], 1e-38, None,
                                                          op0=ALU.is_lt)),
                     waits=[m4], sem="vec")
        m6 = sch.add("vec", vop(lambda v: v.copy_predicated(sig[:], cond[:], halfc[:])),
                     waits=[m5, m_half], sem="vec")
        m7 = sch.add("vec", vop(lambda v: v.max(out=m8t[:], in_=lg[:])),
                     waits=[m6], sem="vec")
        m8_ = sch.add("vec", vop(lambda v: v.match_replace(out=rem[:], in_to_replace=m8t[:],
                                                           in_values=lg[:],
                                                           imm_value=-3e38)),
                      waits=[m7], sem="vec")
        m9 = sch.add("vec", vop(lambda v: v.tensor_tensor(selm[:], lg[:], rem[:],
                                                          op=ALU.not_equal)),
                     waits=[m8_], sem="vec")
        m10 = sch.add("vec", vop(lambda v: v.tensor_tensor(selm[:], selm[:], sig[:],
                                                           op=ALU.mult)),
                      waits=[m9], sem="vec")
        m11 = sch.add("vec", vop(lambda v: v.reduce_sum(ssum[:, 0:1], selm[:], axis=AX.X)),
                      waits=[m10], sem="vec")
        m12 = sch.add("vec", vop(lambda v: v.tensor_scalar(ssum[:, 0:1], ssum[:, 0:1], 1e-9,
                                                           None, op0=ALU.add)),
                      waits=[m11], sem="vec")
        m13 = sch.add("vec", vop(lambda v: v.reciprocal(ssum[:, 1:2], ssum[:, 0:1])),
                      waits=[m12], sem="vec")
        m14 = sch.add("vec", vop(lambda v: v.tensor_scalar(maskf[:], selm[:], ssum[:, 1:2],
                                                           None, op0=ALU.mult)),
                      waits=[m13, pu.get("maskf")], sem="vec")
        pu["lg"] = m10
        pu["sil"] = m3
        pu["sig"] = m10
        m_mt = sch.add("ten", lambda t: t.matmul(PBm[0:64, 0:128], maskf[:], identf,
                                                 start=True, stop=True),
                       waits=[m14] + BTm.write_waits(), sem="ten")
        pu["maskf"] = m_mt
        m_ev = sch.add("scl", lambda s, ib=ib: s.copy(out=mtout[ib][:],
                                                      in_=PBm[0:64, 0:128]),
                       waits=[m_mt] + luse_mt[ib], sem="scl")
        BTm.add_use(m_ev)
        m_sp = sch.dma("y", lambda y, ci=ci, ib=ib: y.dma_start(
            out=t_sc["maskT_sc"][:, ci * 128:(ci + 1) * 128], in_=mtout[ib][:]),
            waits=[m_ev], sem=spsem)
        luse_mt[ib] = [m_sp]
    NTG2 = HALF // 512
    prevm = None
    for tg in range(NTG2):
        cl = slice(tg * 512, (tg + 1) * 512)
        ch_ = slice(HALF + tg * 512, HALF + (tg + 1) * 512)
        w0 = [luse_mt[0][0], luse_mt[1][0]] if tg == 0 else [prevm]
        ma = sch.dma("y", lambda y, cl=cl: y.dma_start(out=mlo[:],
                                                       in_=t_sc["maskT_sc"][:, cl]),
                     waits=w0, sem="dma_a")
        mb = sch.dma("y", lambda y, ch_=ch_: y.dma_start(out=mhi[:],
                                                         in_=t_sc["maskT_sc"][:, ch_]),
                     sem="dma_a")
        mc = sch.add("vec", vop(lambda v: v.tensor_scalar(msel[:], mlo[:], hsel[0:64, 0:1],
                                                          None, op0=ALU.mult)),
                     waits=[mb, m_hs], sem="vec")
        md = sch.add("vec", vop(lambda v: v.scalar_tensor_tensor(
            out=msel[:], in0=mhi[:], scalar=hsel[0:64, 1:2], in1=msel[:],
            op0=ALU.mult, op1=ALU.add)), waits=[mc], sem="vec")
        prevm = sch.dma("y", lambda y, cl=cl: y.dma_start(out=t_sc["maskH_sc"][:, cl],
                                                          in_=msel[:]),
                        waits=[md], sem="spill")

    sch.plan()

    @block.tensor
    def _(t):
        sch.emit_engine("ten", t)

    @block.vector
    def _(v):
        sch.emit_engine("vec", v)

    @block.scalar
    def _(s):
        sch.emit_engine("scl", s)

    @block.sync
    def _(y):
        sch.emit_engine("y", y)

    @block.gpsimd
    def _(g):
        sch.emit_engine("g", g)

    return [Ev(sems[s], sch.final_counts[s]) for s in ["spill", "spill2"]]


A2_CONST_SHAPES = {"onesrow": [1, 128], "cumtri": [128, 128], "mstrict": [128, 128],
                   "mincl": [128, 128], "ones128": [128, 128], "ident4": [128, 512],
                   "mstrict8": [128, 1024], "mincl8": [128, 1024]}


def a2_consts():
    tri1 = np.triu(np.ones((128, 128), dtype=np.float32), 1)
    tri0 = np.triu(np.ones((128, 128), dtype=np.float32))
    return {
        "onesrow": np.ones((1, 128), dtype=np.float32),
        "cumtri": tri0.copy(),
        "mstrict": tri1,
        "mincl": tri0,
        "ones128": np.ones((128, 128), dtype=np.float32),
        "ident4": np.tile(np.eye(128, dtype=np.float32), (1, 4)),
        "mstrict8": np.tile(tri1, (1, 8)),
        "mincl8": np.tile(tri0, (1, 8)),
    }


def load_a2_consts(nc, block, ctx, t_in, sems, skip=()):
    e = ctx.enter_context
    names = [(n, s) for n, s in A2_CONST_SHAPES.items() if n not in skip]
    s_w = sems["w"]

    @block.gpsimd
    def _(g):
        for name, shp in names:
            sb = e(nc.sbuf_tensor(name + "_sb", shp, F32))
            t_in[name + "_sb"] = sb[:]
            s_w.mark(g.dma_start(out=sb[:], in_=t_in[name]), 16)


def build_a2_standalone():
    nc = bass.Bass("TRN2", target_bir_lowering=False, debug=False)
    t_in = {
        "alog_t": nc.dram_tensor("alog_t", [1, 512], F32, kind="ExternalInput").ap(),
        "dtb_t": nc.dram_tensor("dtb_t", [1, 512], F32, kind="ExternalInput").ap(),
        "identf": nc.dram_tensor("identf", [128, 128], F32, kind="ExternalInput").ap(),
    }
    for name, shp in A2_CONST_SHAPES.items():
        t_in[name] = nc.dram_tensor(name, shp, F32, kind="ExternalInput").ap()
    t_sc = {
        "q_sc": nc.dram_tensor("q_sc", [S, D], BF16, kind="ExternalInput").ap(),
        "k_sc": nc.dram_tensor("k_sc", [S, D], BF16, kind="ExternalInput").ap(),
        "v_sc": nc.dram_tensor("v_sc", [S, D], BF16, kind="ExternalInput").ap(),
        "balg_sc": nc.dram_tensor("balg_sc", [S, 80], F32, kind="ExternalInput").ap(),
        "o_sc": nc.dram_tensor("o_sc", [S, D], BF16, kind="ExternalOutput").ap(),
        "maskT_sc": nc.dram_tensor("maskT_sc", [64, S], F32, kind="ExternalOutput").ap(),
    }
    ctx = ExitStack()
    with nc.Block() as block:
        sems = make_sems(nc, ctx)
        load_ident(nc, block, ctx, t_in, sems)
        load_a2_consts(nc, block, ctx, t_in, sems)
        evs = pass_a2(nc, block, ctx, t_in, t_sc, sems)

        @block.gpsimd
        def _(g):
            wait_all(g, *evs)
    ctx.close()
    return nc


def pass_b1(nc, block, ctx, t_in, t_sc, sems, entry=()):
    e = ctx.enter_context
    identb = t_in["identb_sb"]

    wg = e(nc.sbuf_tensor("b1_wg", [128, 8 * D], BF16))
    wo = e(nc.sbuf_tensor("b1_wo", [128, 8 * D], BF16))
    onw = e(nc.sbuf_tensor("b1_onw", [128, 128], F32))

    xb = [e(nc.sbuf_tensor(f"b1_x{i}", [128, D], F32)) for i in range(2)]
    olo = [e(nc.sbuf_tensor(f"b1_olo{i}", [128, D], BF16)) for i in range(2)]
    ohi = [e(nc.sbuf_tensor(f"b1_ohi{i}", [128, D], BF16)) for i in range(2)]
    hsel = e(nc.sbuf_tensor("b1_hsel", [128, 2], F32))
    xn = [e(nc.sbuf_tensor(f"b1_xn{i}", [128, D], F32)) for i in range(2)]
    xnb = [e(nc.sbuf_tensor(f"b1_xnb{i}", [128, D], BF16)) for i in range(2)]
    xrb = [e(nc.sbuf_tensor(f"b1_xrb{i}", [128, D], BF16)) for i in range(2)]
    xnT = [e(nc.sbuf_tensor(f"b1_xnT{i}", [128, D], BF16)) for i in range(2)]
    xrT = [e(nc.sbuf_tensor(f"b1_xrT{i}", [128, D], BF16)) for i in range(2)]
    gsil = [e(nc.sbuf_tensor(f"b1_gs{i}", [128, D], BF16)) for i in range(2)]
    osel = [e(nc.sbuf_tensor(f"b1_os{i}", [128, D], F32)) for i in range(2)]
    og = [e(nc.sbuf_tensor(f"b1_og{i}", [128, D], BF16)) for i in range(2)]
    ogT = [e(nc.sbuf_tensor(f"b1_ogT{i}", [128, D], BF16)) for i in range(2)]
    xattn = [e(nc.sbuf_tensor(f"b1_xa{i}", [128, D], F32)) for i in range(2)]
    mb_ = [e(nc.sbuf_tensor(f"b1_mb{i}", [128, D], BF16)) for i in range(2)]
    mT = [e(nc.sbuf_tensor(f"b1_mT{i}", [128, D], BF16)) for i in range(2)]
    sq = [e(nc.sbuf_tensor(f"b1_sq{i}", [128, D], F32)) for i in range(2)]
    red = e(nc.sbuf_tensor("b1_red", [128, 32], F32))
    rs = e(nc.sbuf_tensor("b1_rs", [128, 16], F32))
    ntf = e(nc.sbuf_tensor("b1_ntf", [128, 16], F32))
    nti = e(nc.sbuf_tensor("b1_nti", [128, 16], I32))
    ut = e(nc.sbuf_tensor("b1_ut", [128, 128], F32))

    PB = [e(nc.psum_tensor(f"b1_pb{i}", [128, 512], F32)) for i in range(8)]
    BT = [BankTracker(PB[i]) for i in range(8)]

    sch = Sched(sems)
    sch.entry = [w for w in entry if w is not None]
    try:
        sch.entry = [w for w in entry if w is not None]
    except NameError:
        sch.entry = []
    TEN, VEC, SCL = "ten", "vec", "scl"

    def vop(fn):
        return lambda v: (v.drain(), fn(v))[1]

    m_wg = sch.dma("g", lambda g: g.dma_start(out=wg[:].rearrange("p (a j) -> p a j", a=8),
                                              in_=t_in["Wg"].rearrange("(a p) j -> p a j", p=128)),
                   sem="w")
    m_wo = sch.dma("g", lambda g: g.dma_start(out=wo[:].rearrange("p (a j) -> p a j", a=8),
                                              in_=t_in["Wo"].rearrange("(a p) j -> p a j", p=128)),
                   sem="w")
    m_hs = sch.dma("g", lambda g: g.dma_start(out=hsel[:], in_=t_in["hsel"]), sem="w")
    m_onwr = sch.dma("g", lambda g: g.dma_start(out=ut[0:1, 0:128], in_=t_in["o_norm_w"]),
                     sem="w")
    m_onb = sch.add(TEN, lambda t: t.matmul(PB[0][:, 0:128], t_in["onesrow_sb"],
                                            ut[0:1, 0:128], start=True, stop=True),
                    waits=[m_onwr] + BT[0].write_waits(), sem="ten")
    m_onw = sch.add(VEC, vop(lambda v: v.tensor_copy(onw[:], PB[0][:, 0:128])),
                    waits=[m_onb], sem="vec")
    BT[0].add_use(m_onw)

    luse = {"x": [[], []], "olo": [[], []], "ohi": [[], []], "sp": [[], []]}
    prev = {}

    for ti in range(NHT):
        ib = ti % 2
        ldsem = "dma_a" if ib == 0 else "dma_b"
        spsem = "spill" if ib == 0 else "spill2"
        r = slice(ti * 128, (ti + 1) * 128)
        m_lx = sch.dma("y", lambda y, r=r, ib=ib: y.dma_start(out=xb[ib][:],
                                                              in_=t_in["x_half"][r, :]),
                       waits=luse["x"][ib], sem=ldsem)
        m_lol = sch.dma("y", lambda y, r=r, ib=ib: y.dma_start(out=olo[ib][:],
                                                               in_=t_sc["o_sc"][r, :]),
                        waits=luse["olo"][ib], sem=ldsem)
        m_loh = sch.dma("y", lambda y, ti=ti, ib=ib: y.dma_start(
            out=ohi[ib][:], in_=t_sc["o_sc"][HALF + ti * 128:HALF + (ti + 1) * 128, :]),
            waits=luse["ohi"][ib], sem=ldsem)
        for kk in ["x", "olo", "ohi"]:
            luse[kk][ib] = []
        m_sqx = sch.add(SCL, lambda s, ib=ib: s.activation(out=sq[ib][:], in_=xb[ib][:],
                                                           func=AF.Square),
                        waits=[m_loh, prev.get(("s2red", ib))], sem="scl")
        m_s1 = sch.add(VEC, vop(lambda v, ib=ib: v.reduce_sum(red[:, 0:1], xb[ib][:],
                                                              axis=AX.X)),
                       waits=[m_loh], sem="vec")
        m_s2 = sch.add(VEC, vop(lambda v, ib=ib: v.reduce_sum(red[:, 1:2], sq[ib][:],
                                                              axis=AX.X)),
                       waits=[m_sqx], sem="vec")
        prev[("s2red", ib)] = m_s2

        def stats(v):
            v.drain()
            v.tensor_scalar(red[:, 2:3], red[:, 0:1], 1.0 / D, None, op0=ALU.mult)
            v.drain()
            v.tensor_tensor(red[:, 3:4], red[:, 2:3], red[:, 2:3], op=ALU.mult)
            v.drain()
            v.tensor_scalar(red[:, 4:5], red[:, 1:2], 1.0 / D, None, op0=ALU.mult)
            v.drain()
            v.tensor_tensor(red[:, 3:4], red[:, 4:5], red[:, 3:4], op=ALU.subtract)
            v.drain()
            v.tensor_scalar(red[:, 4:5], red[:, 3:4], LN_EPS, None, op0=ALU.add)
            v.drain()
            v.reciprocal(red[:, 5:6], red[:, 4:5])
            v.drain()
            v.tensor_tensor(red[:, 6:7], red[:, 3:4], red[:, 5:6], op=ALU.mult)
            v.drain()
            return v.tensor_scalar(red[:, 5:6], red[:, 6:7], EPS, None, op0=ALU.add)
        m_st = sch.add(VEC, stats, waits=[m_s2], sem="vec")

        def newt(v):
            newton_rsqrt(v, rs[:, 0:2], red[:, 4:6], ntf[:, 0:2], nti[:, 0:2])
            return v.tensor_copy(ntf[:, 0:1], rs[:, 0:1])
        m_nw = sch.add(VEC, newt, waits=[m_st], sem="vec")
        m_xn = sch.add(VEC, vop(lambda v, ib=ib: v.tensor_scalar(
            xn[ib][:], xb[ib][:], red[:, 2:3], rs[:, 0:1], op0=ALU.subtract, op1=ALU.mult)),
            waits=[m_nw, prev.get(("xnrd", ib))], sem="vec")
        luse["x"][ib].append(m_xn)
        m_xnb = sch.add(VEC, vop(lambda v, ib=ib: v.tensor_copy(xnb[ib][:], xn[ib][:])),
                        waits=[m_xn, prev.get(("xnbrd", ib))], sem="vec")
        m_xrb = sch.add(VEC, vop(lambda v, ib=ib: v.tensor_scalar(
            xrb[ib][:], xn[ib][:], rs[:, 1:2], None, op0=ALU.mult)),
            waits=[m_xn, prev.get(("xrbrd", ib))], sem="vec")
        mtx = None
        for a in range(8):
            w0 = ([m_xnb] + BT[0].write_waits()) if a == 0 else []
            mtx = sch.add(TEN, lambda t, a=a, ib=ib: t.matmul(
                PB[0][:, (a % 4) * 128:(a % 4 + 1) * 128], xnb[ib][:, a * 128:(a + 1) * 128],
                identb, start=True, stop=True),
                waits=w0 + ([prev.get(("xnT1", ib))] if a == 4 else []), sem="ten")
            if a == 3:
                m_ev1 = sch.add(SCL, lambda s, ib=ib: s.copy(out=xnT[ib][:, 0:512],
                                                             in_=PB[0][:, 0:512]),
                                waits=[mtx, prev.get(("xnTrd", ib))], sem="scl")
                prev[("xnT1", ib)] = m_ev1
        m_ev2 = sch.add(SCL, lambda s, ib=ib: s.copy(out=xnT[ib][:, 512:1024],
                                                     in_=PB[0][:, 0:512]),
                        waits=[mtx], sem="scl")
        BT[0].add_use(m_ev2)
        mtx2 = None
        for a in range(8):
            w0 = ([m_xrb] + BT[1].write_waits()) if a == 0 else []
            mtx2 = sch.add(TEN, lambda t, a=a, ib=ib: t.matmul(
                PB[1][:, (a % 4) * 128:(a % 4 + 1) * 128], xrb[ib][:, a * 128:(a + 1) * 128],
                identb, start=True, stop=True),
                waits=w0 + ([prev.get(("xrT1", ib))] if a == 4 else []), sem="ten")
            if a == 3:
                m_ev3 = sch.add(SCL, lambda s, ib=ib: s.copy(out=xrT[ib][:, 0:512],
                                                             in_=PB[1][:, 0:512]),
                                waits=[mtx2, prev.get(("xrTrd", ib))], sem="scl")
                prev[("xrT1", ib)] = m_ev3
        m_ev4 = sch.add(SCL, lambda s, ib=ib: s.copy(out=xrT[ib][:, 512:1024],
                                                     in_=PB[1][:, 0:512]),
                        waits=[mtx2], sem="scl")
        BT[1].add_use(m_ev4)
        prev[("xnbrd", ib)] = m_ev2
        prev[("xrbrd", ib)] = m_ev4
        mg = None
        for half in range(2):
            bt = BT[2 + half]
            for a in range(8):
                w0 = ([m_ev3, m_ev4] + bt.write_waits()) if a == 0 else []
                mg = sch.add(TEN, lambda t, a=a, half=half, ib=ib, bt=bt: t.matmul(
                    bt.ap[:], xrT[ib][:, a * 128:(a + 1) * 128],
                    wg[:, a * 1024 + half * 512: a * 1024 + (half + 1) * 512],
                    start=(a == 0), stop=(a == 7)),
                    waits=w0 + ([m_onwr] if ti == 0 and half == 0 and a == 0 else []),
                    sem="ten")
            m_gs = sch.add(SCL, lambda s, half=half, ib=ib, bt=bt: s.activation(
                out=gsil[ib][:, half * 512:(half + 1) * 512], in_=bt.ap[:], func=AF.Silu),
                waits=[mg, prev.get(("gsrd", ib)) if half == 0 else None], sem="scl")
            bt.add_use(m_gs)
            if half == 1:
                m_gs_last = m_gs
        prev[("xrTrd", ib)] = mg
        m_os1 = sch.add(VEC, vop(lambda v, ib=ib: v.tensor_scalar(
            osel[ib][:], olo[ib][:], hsel[:, 0:1], None, op0=ALU.mult)),
            waits=[m_loh, m_onwr, prev.get(("osrd", ib))], sem="vec")
        m_os2 = sch.add(VEC, vop(lambda v, ib=ib: v.scalar_tensor_tensor(
            out=osel[ib][:], in0=ohi[ib][:], scalar=hsel[:, 1:2], in1=osel[ib][:],
            op0=ALU.mult, op1=ALU.add)),
            waits=[m_os1], sem="vec")
        luse["olo"][ib].append(m_os2)
        luse["ohi"][ib].append(m_os2)
        m_sqo = sch.add(SCL, lambda s, ib=ib: s.activation(out=sq[ib][:], in_=osel[ib][:],
                                                           func=AF.Square),
                        waits=[m_os2, m_s2], sem="scl")
        m_ro = sch.add(VEC, vop(lambda v, ib=ib: v.reduce_sum(
            red[:, 8:16], sq[ib][:].rearrange("p (h d) -> p h d", h=8), axis=AX.X)),
            waits=[m_sqo], sem="vec")
        prev[("s2red", ib)] = m_ro

        def onorm(v):
            v.drain()
            v.tensor_scalar(red[:, 8:16], red[:, 8:16], 1.0 / DV, EPS, op0=ALU.mult,
                            op1=ALU.add)
            v.drain()
            newton_rsqrt(v, rs[:, 8:16], red[:, 8:16], ntf[:, 8:16], nti[:, 8:16])
            return v.tensor_copy(ntf[:, 8:9], rs[:, 8:9])
        m_on = sch.add(VEC, onorm, waits=[m_ro], sem="vec")
        m_ogl = None
        for h in range(H):
            hs = slice(h * 128, (h + 1) * 128)
            mu = sch.add(VEC, vop(lambda v, h=h, hs=hs, ib=ib: v.scalar_tensor_tensor(
                out=ut[:], in0=gsil[ib][:, hs], scalar=rs[:, 8 + h:9 + h], in1=onw[:],
                op0=ALU.mult, op1=ALU.mult)),
                waits=[m_on, m_gs_last, m_onw], sem="vec")
            m_ogl = sch.add(VEC, vop(lambda v, h=h, hs=hs, ib=ib: v.tensor_tensor(
                og[ib][:, hs], ut[:], osel[ib][:, hs], op=ALU.mult)),
                waits=[mu, prev.get(("ogrd", ib)) if h == 0 else None], sem="vec")
        prev[("gsrd", ib)] = m_ogl
        prev[("osrd", ib)] = m_ogl
        mto = None
        for a in range(8):
            bt = BT[6 + a // 4]
            w0 = ([m_ogl] + bt.write_waits()) if a % 4 == 0 else []
            mto = sch.add(TEN, lambda t, a=a, ib=ib, bt=bt: t.matmul(
                bt.ap[:, (a % 4) * 128:(a % 4 + 1) * 128], og[ib][:, a * 128:(a + 1) * 128],
                identb, start=True, stop=True), waits=w0, sem="ten")
            if a == 3:
                mto3 = mto
        m_ev5 = sch.add(SCL, lambda s, ib=ib: s.copy(out=ogT[ib][:, 0:512], in_=PB[6][:]),
                        waits=[mto3, prev.get(("ogTrd", ib))], sem="scl")
        BT[6].add_use(m_ev5)
        m_ev6 = sch.add(SCL, lambda s, ib=ib: s.copy(out=ogT[ib][:, 512:1024], in_=PB[7][:]),
                        waits=[mto, prev.get(("ogTrd", ib))], sem="scl")
        BT[7].add_use(m_ev6)
        prev[("ogrd", ib)] = m_ev6
        mwo = None
        for half in range(2):
            bt = BT[4 + half]
            for a in range(8):
                w0 = ([m_ev5, m_ev6] + bt.write_waits()) if a == 0 else []
                mwo = sch.add(TEN, lambda t, a=a, half=half, ib=ib, bt=bt: t.matmul(
                    bt.ap[:], ogT[ib][:, a * 128:(a + 1) * 128],
                    wo[:, a * 1024 + half * 512: a * 1024 + (half + 1) * 512],
                    start=(a == 0), stop=(a == 7)),
                    waits=w0 + ([m_onwr] if ti == 0 and half == 0 and a == 0 else []),
                    sem="ten")
            if half == 0:
                mwo0 = mwo
        prev[("ogTrd", ib)] = mwo
        m_xa1 = sch.add(VEC, vop(lambda v, ib=ib: v.tensor_add(
            xattn[ib][:, 0:512], xn[ib][:, 0:512], PB[4][:])),
            waits=[mwo0, m_xn, prev.get(("xard", ib))], sem="vec")
        BT[4].add_use(m_xa1)
        m_xa2 = sch.add(VEC, vop(lambda v, ib=ib: v.tensor_add(
            xattn[ib][:, 512:1024], xn[ib][:, 512:1024], PB[5][:])),
            waits=[mwo], sem="vec")
        BT[5].add_use(m_xa2)
        prev[("xnrd", ib)] = m_xa2
        m_sqa = sch.add(SCL, lambda s, ib=ib: s.activation(out=sq[ib][:], in_=xattn[ib][:],
                                                           func=AF.Square),
                        waits=[m_xa2, m_ro], sem="scl")
        m_ra = sch.add(VEC, vop(lambda v, ib=ib: v.reduce_sum(red[:, 16:17], sq[ib][:],
                                                              axis=AX.X)),
                       waits=[m_sqa], sem="vec")
        prev[("s2red", ib)] = m_ra

        def mnorm(v):
            v.drain()
            v.tensor_scalar(red[:, 16:17], red[:, 16:17], 1.0 / D, EPS, op0=ALU.mult,
                            op1=ALU.add)
            v.drain()
            newton_rsqrt(v, rs[:, 2:3], red[:, 16:17], ntf[:, 2:3], nti[:, 2:3])
            return v.tensor_copy(ntf[:, 2:3], rs[:, 2:3])
        m_mn = sch.add(VEC, mnorm, waits=[m_ra], sem="vec")
        m_mb = sch.add(VEC, vop(lambda v, ib=ib: v.tensor_scalar(
            mb_[ib][:], xattn[ib][:], rs[:, 2:3], None, op0=ALU.mult)),
            waits=[m_mn, prev.get(("mbrd", ib))], sem="vec")
        mtm = None
        for a in range(8):
            bt = BT[6 + a // 4]
            w0 = ([m_mb] + bt.write_waits()) if a % 4 == 0 else []
            mtm = sch.add(TEN, lambda t, a=a, ib=ib, bt=bt: t.matmul(
                bt.ap[:, (a % 4) * 128:(a % 4 + 1) * 128], mb_[ib][:, a * 128:(a + 1) * 128],
                identb, start=True, stop=True), waits=w0, sem="ten")
            if a == 3:
                mtm3 = mtm
        m_ev7 = sch.add(SCL, lambda s, ib=ib: s.copy(out=mT[ib][:, 0:512], in_=PB[6][:]),
                        waits=[mtm3, prev.get(("mTrd", ib))], sem="scl")
        BT[6].add_use(m_ev7)
        m_ev8 = sch.add(SCL, lambda s, ib=ib: s.copy(out=mT[ib][:, 512:1024], in_=PB[7][:]),
                        waits=[mtm, prev.get(("mTrd", ib))], sem="scl")
        BT[7].add_use(m_ev8)
        prev[("mbrd", ib)] = m_ev8
        wsp = [m_ev2, m_ev8, m_xa2] + luse["sp"][ib]
        sch.dma("y", lambda y, ti=ti, ib=ib: y.dma_start(
            out=t_sc["xnT_sc"].rearrange("(a p) t -> p a t", p=128)[:, :, ti * 128:(ti + 1) * 128],
            in_=xnT[ib][:].rearrange("p (a t) -> p a t", a=8)), waits=wsp, sem=spsem)
        sch.dma("y", lambda y, ti=ti, ib=ib: y.dma_start(
            out=t_sc["mT_sc"].rearrange("(a p) t -> p a t", p=128)[:, :, ti * 128:(ti + 1) * 128],
            in_=mT[ib][:].rearrange("p (a t) -> p a t", a=8)), sem=spsem)
        m_sp3 = sch.dma("y", lambda y, r=r, ib=ib: y.dma_start(
            out=t_sc["xattn_sc"][r, :], in_=xattn[ib][:]), sem=spsem)
        luse["sp"][ib] = [m_sp3]
        prev[("xnTrd", ib)] = m_sp3
        prev[("mTrd", ib)] = m_sp3
        prev[("xard", ib)] = m_sp3

    sch.plan()

    @block.tensor
    def _(t):
        sch.emit_engine(TEN, t)

    @block.vector
    def _(v):
        sch.emit_engine(VEC, v)

    @block.scalar
    def _(s):
        sch.emit_engine(SCL, s)

    @block.sync
    def _(y):
        sch.emit_engine("y", y)

    @block.gpsimd
    def _(g):
        sch.emit_engine("g", g)

    return [Ev(sems[s], sch.final_counts[s]) for s in ["spill", "spill2"]]


def pass_b2(nc, block, ctx, t_in, t_sc, sems, entry=(), wd_cast=False):
    e = ctx.enter_context
    wfc = e(nc.sbuf_tensor("b2_wfc", [128, 8 * DFF], BF16))
    wpj = e(nc.sbuf_tensor("b2_wpj", [128, 32 * D], BF16))
    mTt = [e(nc.sbuf_tensor(f"b2_mT{i}", [128, D], BF16)) for i in range(2)]
    xat = [e(nc.sbuf_tensor(f"b2_xa{i}", [128, D], F32)) for i in range(2)]
    rlt = e(nc.sbuf_tensor("b2_rl", [128, 512], F32))
    actT = [e(nc.sbuf_tensor(f"b2_act{i}", [128, DFF], BF16)) for i in range(2)]
    st = [e(nc.sbuf_tensor(f"b2_s{i}", [128, D], F32)) for i in range(2)]

    PB = [e(nc.psum_tensor(f"b2_pb{i}", [128, 512], F32)) for i in range(8)]
    BT = [BankTracker(PB[i]) for i in range(8)]

    sch = Sched(sems)
    try:
        sch.entry = [w for w in entry if w is not None]
    except NameError:
        sch.entry = []
    TEN, VEC, SCL = "ten", "vec", "scl"

    def vop(fn):
        return lambda v: (v.drain(), fn(v))[1]

    sch.dma("g", lambda g: g.dma_start(out=wfc[:].rearrange("p (a j) -> p a j", a=8),
                                       in_=t_in["W_fc"].rearrange("(a p) j -> p a j", p=128)),
            sem="w")
    m_w = sch.dma("g", lambda g: g.dma_start(
        out=wpj[:].rearrange("p (a j) -> p a j", a=32),
        in_=t_in["W_proj"].rearrange("(a p) j -> p a j", p=128)), sem="w")

    if wd_cast:
        wdt = [e(nc.sbuf_tensor(f"b2_wdt{i}", [128, 4 * D], BF16)) for i in range(2)]
        luse_wd = [[], []]
        for ei in range(NE):
            eb = ei % 2
            m_wd = sch.dma("g", lambda g, ei=ei, eb=eb: g.dma_start(
                out=wdt[eb][:].rearrange("p (a j) -> p a j", a=4),
                in_=t_in["Wed"][ei].rearrange("(a p) j -> p a j", p=128)),
                waits=luse_wd[eb], sem="mka" if eb == 0 else "mkb")
            m_wdo = sch.dma("y", lambda y, ei=ei, eb=eb: y.dma_start(
                out=t_sc["wdb_sc"][ei].rearrange("(a p) j -> p a j", p=128),
                in_=wdt[eb][:].rearrange("p (a j) -> p a j", a=4)),
                waits=[m_wd], sem="ama" if eb == 0 else "amb")
            luse_wd[eb] = [m_wdo]

    luse = {"mT": [[], []], "xa": [[], []], "sp": [[], []]}
    prev = {}
    for ti in range(NHT):
        ib = ti % 2
        ldsem = "dma_a" if ib == 0 else "dma_b"
        spsem = "spill" if ib == 0 else "spill2"
        r = slice(ti * 128, (ti + 1) * 128)
        sch.dma("y", lambda y, ti=ti, ib=ib: y.dma_start(
            out=mTt[ib][:].rearrange("p (a t) -> p a t", a=8),
            in_=t_sc["mT_sc"].rearrange("(a p) t -> p a t", p=128)[:, :, ti * 128:(ti + 1) * 128]),
            waits=luse["mT"][ib], sem=ldsem)
        m_lx = sch.dma("y", lambda y, r=r, ib=ib: y.dma_start(out=xat[ib][:],
                                                              in_=t_sc["xattn_sc"][r, :]),
                       waits=luse["xa"][ib], sem=ldsem)
        luse["mT"][ib] = []
        luse["xa"][ib] = []
        m_act = None
        for grp in range(8):
            bt = BT[grp % 4]
            mm = None
            for dblk in range(4):
                d = grp * 4 + dblk
                for a in range(8):
                    w0 = []
                    if dblk == 0 and a == 0:
                        w0 = [m_lx, m_w] + bt.write_waits()
                    mm = sch.add(TEN, lambda t, a=a, d=d, dblk=dblk, ib=ib, bt=bt: t.matmul(
                        bt.ap[:, dblk * 128:(dblk + 1) * 128],
                        wfc[:, a * DFF + d * 128: a * DFF + (d + 1) * 128],
                        mTt[ib][:, a * 128:(a + 1) * 128],
                        start=(a == 0), stop=(a == 7)), waits=w0, sem="ten")
            m_rl = sch.add(SCL, lambda s, bt=bt: s.activation(out=rlt[:], in_=bt.ap[:],
                                                              func=AF.Relu),
                           waits=[mm, prev.get("rl")], sem="scl")
            bt.add_use(m_rl)
            m_act = sch.add(VEC, vop(lambda v, grp=grp, ib=ib: v.tensor_tensor(
                actT[ib][:, grp * 512:(grp + 1) * 512], rlt[:], rlt[:], op=ALU.mult)),
                waits=[m_rl, prev.get(("actrd", ib)) if grp == 0 else None], sem="vec")
            prev["rl"] = m_act
        luse["mT"][ib].append(m_act)
        my = None
        for half in range(2):
            bt = BT[4 + half]
            for d in range(32):
                w0 = ([m_act] + bt.write_waits()) if d == 0 else []
                my = sch.add(TEN, lambda t, d=d, half=half, ib=ib, bt=bt: t.matmul(
                    bt.ap[:], actT[ib][:, d * 128:(d + 1) * 128],
                    wpj[:, d * 1024 + half * 512: d * 1024 + (half + 1) * 512],
                    start=(d == 0), stop=(d == 31)), waits=w0, sem="ten")
            if half == 0:
                my0 = my
        prev[("actrd", ib)] = my
        m_s1 = sch.add(VEC, vop(lambda v, ib=ib: v.tensor_add(st[ib][:, 0:512],
                                                              xat[ib][:, 0:512], PB[4][:])),
                       waits=[my0, m_lx, prev.get(("srd", ib))] + luse["sp"][ib], sem="vec")
        BT[4].add_use(m_s1)
        m_s2 = sch.add(VEC, vop(lambda v, ib=ib: v.tensor_add(st[ib][:, 512:1024],
                                                              xat[ib][:, 512:1024], PB[5][:])),
                       waits=[my], sem="vec")
        BT[5].add_use(m_s2)
        luse["xa"][ib].append(m_s2)
        m_sp = sch.dma("y", lambda y, r=r, ib=ib: y.dma_start(out=t_sc["s_sc"][r, :],
                                                              in_=st[ib][:]),
                       waits=[m_s2], sem=spsem)
        luse["sp"][ib] = [m_sp]
        prev[("srd", ib)] = m_sp

    sch.plan()

    @block.tensor
    def _(t):
        sch.emit_engine(TEN, t)

    @block.vector
    def _(v):
        sch.emit_engine(VEC, v)

    @block.scalar
    def _(s):
        sch.emit_engine(SCL, s)

    @block.sync
    def _(y):
        sch.emit_engine("y", y)

    @block.gpsimd
    def _(g):
        sch.emit_engine("g", g)

    return [Ev(sems[s], sch.final_counts[s]) for s in
            (["spill", "spill2", "ama", "amb"] if wd_cast else ["spill", "spill2"])]


def pass_b3(nc, block, ctx, t_in, t_sc, sems, entry=()):
    e = ctx.enter_context
    ctx_a = ExitStack()
    ea = ctx_a.enter_context
    NTG = HALF // 512

    xnT = e(nc.sbuf_tensor("b3_xnT", [128, 8 * HALF], BF16))
    wsg = e(nc.sbuf_tensor("b3_wsg", [128, 8 * NI], BF16))
    wsu = e(nc.sbuf_tensor("b3_wsu", [128, 8 * NI], BF16))
    wsd = e(nc.sbuf_tensor("b3_wsd", [128, 4 * D], BF16))
    wge = [ea(nc.sbuf_tensor(f"b3_wg{i}", [128, 8 * NI], BF16)) for i in range(2)]
    wue = [ea(nc.sbuf_tensor(f"b3_wu{i}", [128, 8 * NI], BF16)) for i in range(2)]
    wdt = [ea(nc.sbuf_tensor(f"b3_wdt{i}", [128, 4 * D], BF16)) for i in range(2)]
    mkb = [ea(nc.sbuf_tensor(f"b3_mkb{i}", [128, 512], F32)) for i in range(2)]
    gs = [ea(nc.sbuf_tensor(f"b3_gs{i}", [128, 512], BF16)) for i in range(2)]
    um = [ea(nc.sbuf_tensor(f"b3_um{i}", [128, 512], F32)) for i in range(2)]
    am = [ea(nc.sbuf_tensor(f"b3_am{i}", [128, 512], BF16)) for i in range(4)]

    PB = [e(nc.psum_tensor(f"b3_pb{i}", [128, 512], F32)) for i in range(8)]
    BT = [BankTracker(PB[i]) for i in range(8)]

    sch = Sched(sems)
    try:
        sch.entry = [w for w in entry if w is not None]
    except NameError:
        sch.entry = []
    TEN, VEC, SCL = "ten", "vec", "scl"

    def vop(fn):
        return lambda v: (v.drain(), fn(v))[1]

    m_xnT = sch.dma("g", lambda g: g.dma_start(
        out=xnT[:].rearrange("p (a t) -> p a t", a=8),
        in_=t_sc["xnT_sc"].rearrange("(a p) t -> p a t", p=128)), waits=[], sem="w")
    m_ws1 = sch.dma("g", lambda g: g.dma_start(
        out=wsg[:].rearrange("p (a j) -> p a j", a=8),
        in_=t_in["Wsg"].rearrange("(a p) j -> p a j", p=128)), sem="w")
    m_ws2 = sch.dma("g", lambda g: g.dma_start(
        out=wsu[:].rearrange("p (a j) -> p a j", a=8),
        in_=t_in["Wsu"].rearrange("(a p) j -> p a j", p=128)), sem="w")
    m_ws3 = sch.dma("g", lambda g: g.dma_start(
        out=wsd[:].rearrange("p (a j) -> p a j", a=4),
        in_=t_in["Wsd"].rearrange("(a p) j -> p a j", p=128)), sem="w")

    luse_w = [[], []]
    luse_m = [[], []]
    luse_am = [[] for _ in range(4)]
    last_am = [None, None]
    for ei in range(NE):
        eb = ei % 2
        wsem = "ewa" if eb == 0 else "ewb"
        m_wg = sch.dma("g", lambda g, ei=ei, eb=eb: g.dma_start(
            out=wge[eb][:].rearrange("p (a j) -> p a j", a=8),
            in_=t_in["Weg"][ei].rearrange("(a p) j -> p a j", p=128)),
            waits=luse_w[eb], sem=wsem)
        m_wu = sch.dma("g", lambda g, ei=ei, eb=eb: g.dma_start(
            out=wue[eb][:].rearrange("p (a j) -> p a j", a=8),
            in_=t_in["Weu"][ei].rearrange("(a p) j -> p a j", p=128)), sem=wsem)
        m_wd = sch.dma("g", lambda g, ei=ei, eb=eb: g.dma_start(
            out=wdt[eb][:].rearrange("p (a j) -> p a j", a=4),
            in_=t_in["Wed"][ei].rearrange("(a p) j -> p a j", p=128)), sem=wsem)
        m_wdo = sch.dma("y", lambda y, ei=ei, eb=eb: y.dma_start(
            out=t_sc["wdb_sc"][ei].rearrange("(a p) j -> p a j", p=128),
            in_=wdt[eb][:].rearrange("p (a j) -> p a j", a=4)),
            waits=[m_wd], sem="spill2")
        luse_w[eb] = [m_wdo]
        for tg in range(NTG):
            tb = tg % 2
            m_mk = sch.dma("g", lambda g, ei=ei, tg=tg, tb=tb: g.dma_start(
                out=mkb[tb][:], in_=t_sc["maskH_sc"][ei:ei + 1,
                                                     tg * 512:(tg + 1) * 512]
                .broadcast_to([128, 512])),
                waits=luse_m[tb], sem="mka" if tb == 0 else "mkb")
            luse_m[tb] = []
            for iblk in range(4):
                pb_g = BT[iblk % 2]
                pb_u = BT[2 + iblk % 2]
                mmg = mmu = None
                for a in range(8):
                    wg0 = ([m_wdo, m_ws3] + pb_g.write_waits()) if a == 0 else []
                    mmg = sch.add(TEN, lambda t, a=a, iblk=iblk, tg=tg, eb=eb, pb_g=pb_g:
                                  t.matmul(pb_g.ap[:],
                                           wge[eb][:, a * NI + iblk * 128: a * NI + (iblk + 1) * 128],
                                           xnT[:, a * HALF + tg * 512: a * HALF + tg * 512 + 512],
                                           start=(a == 0), stop=(a == 7)),
                                  waits=wg0, sem="ten")
                for a in range(8):
                    wu0 = (pb_u.write_waits()) if a == 0 else []
                    mmu = sch.add(TEN, lambda t, a=a, iblk=iblk, tg=tg, eb=eb, pb_u=pb_u:
                                  t.matmul(pb_u.ap[:],
                                           wue[eb][:, a * NI + iblk * 128: a * NI + (iblk + 1) * 128],
                                           xnT[:, a * HALF + tg * 512: a * HALF + tg * 512 + 512],
                                           start=(a == 0), stop=(a == 7)),
                                  waits=wu0, sem="ten")
                ab = iblk % 4
                m_gs = sch.add(SCL, lambda s, pb_g=pb_g, tb=tb: s.activation(
                    out=gs[tb][:], in_=pb_g.ap[:], func=AF.Silu),
                    waits=[mmg, last_am[tb],
                           luse_am[ab][0] if luse_am[ab] else None], sem="scl")
                pb_g.add_use(m_gs)
                m_um = sch.add(VEC, vop(lambda v, pb_u=pb_u, tb=tb: v.tensor_tensor(
                    um[tb][:], pb_u.ap[:], mkb[tb][:], op=ALU.mult)),
                    waits=[mmu, m_mk, luse_am[ab][0] if luse_am[ab] else None], sem="vec")
                pb_u.add_use(m_um)
                if iblk == 3:
                    luse_m[tb].append(m_um)
                m_am = sch.add(VEC, vop(lambda v, ab=ab, tb=tb: v.tensor_tensor(
                    am[ab][:], um[tb][:], gs[tb][:], op=ALU.mult)),
                    waits=[m_gs, m_um], sem="vec")
                last_am[tb] = m_am
                m_ao = sch.dma("y", lambda y, ei=ei, tg=tg, iblk=iblk, ab=ab: y.dma_start(
                    out=t_sc["actm_sc"][ei, iblk * 128:(iblk + 1) * 128,
                                        tg * 512:(tg + 1) * 512], in_=am[ab][:]),
                    waits=[m_am], sem=["ama", "amb", "amc", "amd"][ab])
                luse_am[ab] = [m_ao]

    ev_a_done = [luse_am[i][0] for i in range(4)] + \
        [luse_w[1][0] if luse_w[1] else luse_w[0][0], luse_w[0][0]]
    amt = [ea(nc.sbuf_tensor(f"b3_amt{i}", [128, 4 * 512], BF16)) for i in range(2)]
    wdb = [ea(nc.sbuf_tensor(f"b3_wdb{i}", [128, 4 * D], BF16)) for i in range(2)]
    rout = ea(nc.sbuf_tensor("b3_rout", [128, D], F32))
    luse_amt = [[], []]
    luse_wdb = [[], []]
    m_rv = None
    for tg in range(NTG):
        mmacc = [[None, None] for _ in range(4)]
        for ei in range(NE):
            eb = ei % 2
            wsem2 = "dma_a" if eb == 0 else "dma_b"
            m_la = sch.dma("y", lambda y, ei=ei, tg=tg, eb=eb: y.dma_start(
                out=amt[eb][:].rearrange("p (b t) -> p b t", b=4),
                in_=t_sc["actm_sc"][ei, :, tg * 512:(tg + 1) * 512]
                .rearrange("(b p) t -> p b t", p=128)),
                waits=luse_amt[eb] + (ev_a_done if tg == 0 and ei == 0 else []), sem=wsem2)
            m_lw = sch.dma("y", lambda y, ei=ei, eb=eb: y.dma_start(
                out=wdb[eb][:].rearrange("p (a j) -> p a j", a=4),
                in_=t_sc["wdb_sc"][ei].rearrange("(a p) j -> p a j", p=128)),
                waits=luse_wdb[eb], sem=wsem2)
            luse_amt[eb] = []
            luse_wdb[eb] = []
            mm = None
            for iblk in range(4):
                for tt in range(4):
                    for half in range(2):
                        pb = BT[tt * 2 + half]
                        w0 = []
                        if ei == 0 and iblk == 0:
                            w0 = pb.write_waits() + [m_lw]
                        elif iblk == 0 and tt == 0 and half == 0:
                            w0 = [m_lw]
                        st_ = (ei == 0 and iblk == 0)
                        sp_ = (ei == NE - 1 and iblk == 3)
                        mm = sch.add(TEN, lambda t, iblk=iblk, tt=tt, half=half, eb=eb, pb=pb,
                                     st_=st_, sp_=sp_:
                                     t.matmul(pb.ap[:],
                                              amt[eb][:, iblk * 512 + tt * 128: iblk * 512 + (tt + 1) * 128],
                                              wdb[eb][:, iblk * 1024 + half * 512: iblk * 1024 + (half + 1) * 512],
                                              start=st_, stop=sp_,
                                              skip_group_check=True),
                                     waits=w0, sem="ten")
            luse_amt[eb].append(mm)
            luse_wdb[eb].append(mm)
        for tt in range(4):
            for half in range(2):
                pb = BT[tt * 2 + half]
                m_rv = sch.add(VEC, vop(lambda v, tt=tt, half=half, pb=pb: v.tensor_copy(
                    rout[:, half * 512:(half + 1) * 512], pb.ap[:])),
                    waits=[mm] + ([m_rv] if m_rv else []), sem="vec")
                pb.add_use(m_rv)
            m_rv = sch.dma("y", lambda y, tg=tg, tt=tt: y.dma_start(
                out=t_sc["routed_sc"][tg * 512 + tt * 128: tg * 512 + (tt + 1) * 128, :],
                in_=rout[:]), waits=[m_rv], sem="spill")

    ctx_a.close()
    sgs = e(nc.sbuf_tensor("b3_sgs", [128, 512], BF16))
    ash = e(nc.sbuf_tensor("b3_ash", [128, 512], BF16))
    rl2 = [e(nc.sbuf_tensor(f"b3_r{i}", [128, D], F32)) for i in range(2)]
    sl2 = [e(nc.sbuf_tensor(f"b3_sl{i}", [128, D], F32)) for i in range(2)]
    xl2 = [e(nc.sbuf_tensor(f"b3_xl{i}", [128, D], F32)) for i in range(2)]
    gf = e(nc.sbuf_tensor("b3_gf", [128, D], F32))
    ot = [e(nc.sbuf_tensor(f"b3_ot{i}", [128, D], F32)) for i in range(2)]


    luse3 = {"r": [[], []], "s": [[], []], "x": [[], []], "o": [[], []]}
    prevc = {}
    for ti in range(NHT):
        ib = ti % 2
        ldsem = "dma_a" if ib == 0 else "dma_b"
        spsem = "spill" if ib == 0 else "spill2"
        r = slice(ti * 128, (ti + 1) * 128)
        m_lr = sch.dma("y", lambda y, r=r, ib=ib: y.dma_start(out=rl2[ib][:],
                                                              in_=t_sc["routed_sc"][r, :]),
                       waits=luse3["r"][ib] + ([m_rv] if ti == 0 else []), sem=ldsem)
        m_ls = sch.dma("y", lambda y, r=r, ib=ib: y.dma_start(out=sl2[ib][:],
                                                              in_=t_sc["s_sc"][r, :]),
                       waits=luse3["s"][ib], sem=ldsem)
        m_lx2 = sch.dma("y", lambda y, r=r, ib=ib: y.dma_start(out=xl2[ib][:],
                                                               in_=t_in["x_half"][r, :]),
                        waits=luse3["x"][ib], sem=ldsem)
        for kk in ["r", "s", "x"]:
            luse3[kk][ib] = []
        msg = msu_ = None
        for iblk in range(4):
            pb_g = BT[iblk % 2]
            pb_u = BT[2 + iblk % 2]
            for a in range(8):
                w0 = ([m_ws3, m_lx2] + pb_g.write_waits()) if a == 0 else []
                msg = sch.add(TEN, lambda t, a=a, iblk=iblk, ti=ti, pb_g=pb_g: t.matmul(
                    pb_g.ap[:, 0:128],
                    wsg[:, a * NI + iblk * 128: a * NI + (iblk + 1) * 128],
                    xnT[:, a * HALF + ti * 128: a * HALF + (ti + 1) * 128],
                    start=(a == 0), stop=(a == 7)), waits=w0, sem="ten")
            for a in range(8):
                w0 = (pb_u.write_waits()) if a == 0 else []
                msu_ = sch.add(TEN, lambda t, a=a, iblk=iblk, ti=ti, pb_u=pb_u: t.matmul(
                    pb_u.ap[:, 0:128],
                    wsu[:, a * NI + iblk * 128: a * NI + (iblk + 1) * 128],
                    xnT[:, a * HALF + ti * 128: a * HALF + (ti + 1) * 128],
                    start=(a == 0), stop=(a == 7)), waits=w0, sem="ten")
            m_sg = sch.add(SCL, lambda s, iblk=iblk, pb_g=pb_g: s.activation(
                out=sgs[:, iblk * 128:(iblk + 1) * 128], in_=pb_g.ap[:, 0:128],
                func=AF.Silu), waits=[msg, prevc.get("sgrd")], sem="scl")
            pb_g.add_use(m_sg)
            m_as = sch.add(VEC, vop(lambda v, iblk=iblk, pb_u=pb_u: v.tensor_tensor(
                ash[:, iblk * 128:(iblk + 1) * 128], pb_u.ap[:, 0:128],
                sgs[:, iblk * 128:(iblk + 1) * 128], op=ALU.mult)),
                waits=[msu_, m_sg, prevc.get("ashrd") if iblk == 0 else None], sem="vec")
            pb_u.add_use(m_as)
            prevc["sgrd"] = m_as
        msd = None
        for half in range(2):
            pb = BT[4 + half]
            for iblk in range(4):
                w0 = ([m_as] + pb.write_waits()) if iblk == 0 else []
                msd = sch.add(TEN, lambda t, iblk=iblk, half=half, pb=pb: t.matmul(
                    pb.ap[:], ash[:, iblk * 128:(iblk + 1) * 128],
                    wsd[:, iblk * 1024 + half * 512: iblk * 1024 + (half + 1) * 512],
                    start=(iblk == 0), stop=(iblk == 3)), waits=w0, sem="ten")
            if half == 0:
                msd0 = msd
        prevc["ashrd"] = msd
        m_g1 = sch.add(VEC, vop(lambda v, ib=ib: v.tensor_add(gf[:, 0:512], rl2[ib][:, 0:512],
                                                              PB[4][:])),
                       waits=[msd0, m_lx2, prevc.get("gfrd")], sem="vec")
        BT[4].add_use(m_g1)
        m_g2 = sch.add(VEC, vop(lambda v, ib=ib: v.tensor_add(gf[:, 512:1024],
                                                              rl2[ib][:, 512:1024], PB[5][:])),
                       waits=[msd], sem="vec")
        BT[5].add_use(m_g2)
        luse3["r"][ib].append(m_g2)
        m_gs2 = sch.add(SCL, lambda s: s.activation(out=gf[:], in_=gf[:], func=AF.Silu),
                        waits=[m_g2], sem="scl")
        prevc["gfrd"] = m_gs2
        m_o1 = sch.add(VEC, vop(lambda v, ib=ib: v.tensor_tensor(ot[ib][:], sl2[ib][:],
                                                                 gf[:], op=ALU.mult)),
                       waits=[m_gs2, m_lx2] + luse3["o"][ib], sem="vec")
        m_o2 = sch.add(VEC, vop(lambda v, ib=ib: v.tensor_tensor(ot[ib][:], ot[ib][:],
                                                                 xl2[ib][:], op=ALU.add)),
                       waits=[m_o1, m_lx2], sem="vec")
        luse3["s"][ib].append(m_o2)
        luse3["x"][ib].append(m_o2)
        m_oo = sch.dma("y", lambda y, r=r, ib=ib: y.dma_start(out=t_sc["out"][r, :],
                                                              in_=ot[ib][:]),
                       waits=[m_o2], sem=spsem)
        luse3["o"][ib] = [m_oo]

    sch.plan()

    @block.tensor
    def _(t):
        sch.emit_engine(TEN, t)

    @block.vector
    def _(v):
        sch.emit_engine(VEC, v)

    @block.scalar
    def _(s):
        sch.emit_engine(SCL, s)

    @block.sync
    def _(y):
        sch.emit_engine("y", y)

    @block.gpsimd
    def _(g):
        sch.emit_engine("g", g)

    return [Ev(sems[s], sch.final_counts[s]) for s in ["spill", "spill2"]]


def b_input_tensors(nc, skip=()):
    t_in = {
        "x_half": nc.dram_tensor("x_half", [HALF, D], F32, kind="ExternalInput").ap(),
        "hsel": nc.dram_tensor("hsel", [128, 2], F32, kind="ExternalInput").ap(),
        "Wg": nc.dram_tensor("Wg", [D, D], F32, kind="ExternalInput").ap(),
        "Wo": nc.dram_tensor("Wo", [D, D], F32, kind="ExternalInput").ap(),
        "o_norm_w": nc.dram_tensor("o_norm_w", [1, 128], F32, kind="ExternalInput").ap(),
        "W_fc": nc.dram_tensor("W_fc", [D, DFF], F32, kind="ExternalInput").ap(),
        "W_proj": nc.dram_tensor("W_proj", [DFF, D], F32, kind="ExternalInput").ap(),
        "Weg": nc.dram_tensor("Weg", [NE, D, NI], F32, kind="ExternalInput").ap(),
        "Weu": nc.dram_tensor("Weu", [NE, D, NI], F32, kind="ExternalInput").ap(),
        "Wed": nc.dram_tensor("Wed", [NE, NI, D], F32, kind="ExternalInput").ap(),
        "Wsg": nc.dram_tensor("Wsg", [D, NI], F32, kind="ExternalInput").ap(),
        "Wsu": nc.dram_tensor("Wsu", [D, NI], F32, kind="ExternalInput").ap(),
        "Wsd": nc.dram_tensor("Wsd", [NI, D], F32, kind="ExternalInput").ap(),
    }
    if "identf" not in skip:
        t_in["identf"] = nc.dram_tensor("identf", [128, 128], F32,
                                        kind="ExternalInput").ap()
        t_in["onesrow"] = nc.dram_tensor("onesrow", [1, 128], F32,
                                         kind="ExternalInput").ap()
    return t_in


def b_scratch_tensors(nc, io_kind, skip_ab=False):
    kind_in = "ExternalInput" if io_kind else "Internal"
    t_sc = {}
    if not skip_ab:
        t_sc["o_sc"] = nc.dram_tensor("o_sc", [S, D], BF16, kind=kind_in).ap()
        t_sc["maskT_sc"] = nc.dram_tensor("maskT_sc", [64, S], F32, kind=kind_in).ap()
    if not skip_ab:
        t_sc["maskH_sc"] = nc.dram_tensor("maskH_sc", [64, HALF], F32,
                                          kind=kind_in).ap()
    t_sc.update({
        "wge_sc": nc.dram_tensor("wge_sc", [NE, D, NI], BF16, kind="Internal").ap(),
        "wue_sc": nc.dram_tensor("wue_sc", [NE, D, NI], BF16, kind="Internal").ap(),
        "xnT_sc": nc.dram_tensor("xnT_sc", [D, HALF], BF16, kind="Internal").ap(),
        "xn_sc": nc.dram_tensor("xn_sc", [HALF, D], F32, kind="Internal").ap(),
        "gate_sc": nc.dram_tensor("gate_sc", [HALF, D], BF16, kind="Internal").ap(),
        "mT_sc": nc.dram_tensor("mT_sc", [D, HALF], BF16, kind="Internal").ap(),
        "xattn_sc": nc.dram_tensor("xattn_sc", [HALF, D], F32, kind="Internal").ap(),
        "s_sc": nc.dram_tensor("s_sc", [HALF, D], F32, kind="Internal").ap(),
        "actm_sc": nc.dram_tensor("actm_sc", [NE, NI, HALF], BF16, kind="Internal").ap(),
        "wdb_sc": nc.dram_tensor("wdb_sc", [NE, NI, D], BF16, kind="Internal").ap(),
        "routed_sc": nc.dram_tensor("routed_sc", [HALF, D], F32, kind="Internal").ap(),
        "out": nc.dram_tensor("out", [HALF, D], F32, kind="ExternalOutput").ap(),
    })
    return t_sc


def build_b_standalone():
    nc = bass.Bass("TRN2", target_bir_lowering=False, debug=False)
    t_in = b_input_tensors(nc)
    t_sc = b_scratch_tensors(nc, io_kind=True)
    ctx = ExitStack()
    with nc.Block() as block:
        sems = make_sems(nc, ctx)
        load_ident(nc, block, ctx, t_in, sems)
        e = ctx.enter_context
        onesrow_sb = e(nc.sbuf_tensor("onesrow_sb", [1, 128], F32))
        s_w = sems["w"]

        @block.gpsimd
        def _(g):
            s_w.mark(g.dma_start(out=onesrow_sb[:], in_=t_in["onesrow"]), 16)
        t_in["onesrow_sb"] = onesrow_sb[:]

        ctx1a = ExitStack()
        ev1a = pass_b1a(nc, block, ctx1a, t_in, t_sc, sems)
        ctx1a.close()
        ctx1 = ExitStack()
        ev1 = pass_b1b(nc, block, ctx1, t_in, t_sc, sems, entry=ev1a)
        ctx1.close()
        ctx2 = ExitStack()
        ev2 = pass_b2(nc, block, ctx2, t_in, t_sc, sems, entry=ev1)
        ctx2.close()
        ctx3 = ExitStack()
        ev3 = pass_b3(nc, block, ctx3, t_in, t_sc, sems, entry=ev1 + ev2)
        ctx3.close()

        @block.gpsimd
        def _(g):
            wait_all(g, *ev3)
    ctx.close()
    return nc


def build_a_prog(fused=False, nc=None, block=None, ctx=None, sems=None, t_sc_override=None,
                 between=None):
    own = nc is None
    if own:
        nc = bass.Bass("TRN2", target_bir_lowering=False, debug=False)
    t_in = {
        "x_full": nc.dram_tensor("x_full", [S, D], F32, kind="ExternalInput").ap(),
        "Wq": nc.dram_tensor("Wq", [D, D], F32, kind="ExternalInput").ap(),
        "Wk": nc.dram_tensor("Wk", [D, D], F32, kind="ExternalInput").ap(),
        "Wv": nc.dram_tensor("Wv", [D, D], F32, kind="ExternalInput").ap(),
        "wbar": nc.dram_tensor("wbar", [D, 80], F32, kind="ExternalInput").ap(),
        "alog_t": nc.dram_tensor("alog_t", [1, 512], F32, kind="ExternalInput").ap(),
        "dtb_t": nc.dram_tensor("dtb_t", [1, 512], F32, kind="ExternalInput").ap(),
        "identf": nc.dram_tensor("identf", [128, 128], F32, kind="ExternalInput").ap(),
        "hselr": nc.dram_tensor("hselr", [128, 2], F32, kind="ExternalInput").ap(),
    }
    for name, shp in A2_CONST_SHAPES.items():
        t_in[name] = nc.dram_tensor(name, shp, F32, kind="ExternalInput").ap()
    sck = "ExternalOutput" if own else "Internal"
    t_sc = {
        "q_sc": nc.dram_tensor("q_sc", [S, D], BF16, kind="Internal").ap(),
        "k_sc": nc.dram_tensor("k_sc", [S, D], BF16, kind="Internal").ap(),
        "v_sc": nc.dram_tensor("v_sc", [S, D], BF16, kind="Internal").ap(),
        "balg_sc": nc.dram_tensor("balg_sc", [S, 80], F32, kind="Internal").ap(),
    }
    if t_sc_override is not None:
        t_sc.update(t_sc_override)
    else:
        t_sc["o_sc"] = nc.dram_tensor("o_sc", [S, D], BF16, kind=sck).ap()
        t_sc["maskT_sc"] = nc.dram_tensor("maskT_sc", [64, S], F32, kind=sck).ap()
        t_sc["maskH_sc"] = nc.dram_tensor("maskH_sc", [64, HALF], F32, kind=sck).ap()

    def emit(block, ctx, sems):
        load_ident(nc, block, ctx, t_in, sems)
        load_a2_consts(nc, block, ctx, t_in, sems)
        ctx1 = ExitStack()
        ev1 = pass_a1(nc, block, ctx1, t_in, t_sc, sems)
        ctx1.close()
        ctxr = ExitStack()
        evr = pass_r(nc, block, ctxr, t_in, t_sc, sems, entry=ev1)
        ctxr.close()
        extra = between(ev1 + evr, t_in, t_sc) if between else []
        ctx2 = ExitStack()
        ev2 = pass_a2(nc, block, ctx2, t_in, t_sc, sems, entry=ev1 + evr + list(extra))
        ctx2.close()
        return ev2

    if own:
        ctx = ExitStack()
        with nc.Block() as block:
            sems = make_sems(nc, ctx)
            ev2 = emit(block, ctx, sems)

            @block.gpsimd
            def _(g):
                wait_all(g, *ev2)
        ctx.close()
        return nc, t_in, t_sc
    ev2 = emit(block, ctx, sems)
    return t_in, t_sc, ev2


def build_b_prog(nc=None, block=None, ctx=None, sems=None, t_sc_shared=None, entry=(),
                 t_in_extra=None):
    own = nc is None
    if own:
        nc = bass.Bass("TRN2", target_bir_lowering=False, debug=False)
    t_in = b_input_tensors(nc, skip=() if t_in_extra is None else ("identf",))
    if t_in_extra:
        t_in.update(t_in_extra)
    t_sc = b_scratch_tensors(nc, io_kind=own) if t_sc_shared is None else t_sc_shared

    def emit(block, ctx, sems, entry):
        e = ctx.enter_context
        if "onesrow_sb" not in t_in:
            onesrow_sb = e(nc.sbuf_tensor("onesrow_sb2", [1, 128], F32))
            s_w = sems["w"]

            @block.gpsimd
            def _(g):
                s_w.mark(g.dma_start(out=onesrow_sb[:], in_=t_in["onesrow"]), 16)
            t_in["onesrow_sb"] = onesrow_sb[:]
        ctx1a = ExitStack()
        ev1a = pass_b1a(nc, block, ctx1a, t_in, t_sc, sems, entry=entry)
        ctx1a.close()
        ctx1 = ExitStack()
        ev1 = pass_b1b(nc, block, ctx1, t_in, t_sc, sems, entry=list(entry) + ev1a)
        ctx1.close()
        ctx2 = ExitStack()
        ev2 = pass_b2(nc, block, ctx2, t_in, t_sc, sems, entry=list(entry) + ev1)
        ctx2.close()
        ctx3 = ExitStack()
        ev3 = pass_b3(nc, block, ctx3, t_in, t_sc, sems, entry=list(entry) + ev1 + ev2)
        ctx3.close()
        return ev3

    if own:
        ctx = ExitStack()
        with nc.Block() as block:
            sems = make_sems(nc, ctx)
            load_ident(nc, block, ctx, t_in, sems)
            ev3 = emit(block, ctx, sems, ())

            @block.gpsimd
            def _(g):
                wait_all(g, *ev3)
        ctx.close()
        return nc, t_in, t_sc
    return emit(block, ctx, sems, entry)


def build_fused():
    nc = bass.Bass("TRN2", target_bir_lowering=False, debug=False)
    ctx = ExitStack()
    with nc.Block() as block:
        sems = make_sems(nc, ctx)
        t_sc_b = b_scratch_tensors(nc, io_kind=False, skip_ab=True)
        holder = {}

        def between(evs, t_in_a, t_sc_a):
            t_in_b = b_input_tensors(nc, skip=("identf",))
            for k2 in ["identf", "onesrow", "onesrow_sb", "identb_sb", "identf_sb"]:
                t_in_b[k2] = t_in_a[k2]
            holder["t_in_b"] = t_in_b
            ctx1a = ExitStack()
            ev1a = pass_b1a(nc, block, ctx1a, t_in_b, t_sc_b, sems, entry=evs)
            ctx1a.close()
            holder["ev1a"] = ev1a
            return ev1a

        t_in_a, t_sc_a, ev_a = build_a_prog(fused=True, nc=nc, block=block, ctx=ctx,
                                            sems=sems, between=between)
        t_sc_b["o_sc"] = t_sc_a["o_sc"]
        t_sc_b["maskT_sc"] = t_sc_a["maskT_sc"]
        t_sc_b["maskH_sc"] = t_sc_a["maskH_sc"]
        t_in_b = holder["t_in_b"]
        ctx1 = ExitStack()
        ev1 = pass_b1b(nc, block, ctx1, t_in_b, t_sc_b, sems,
                       entry=list(ev_a) + holder["ev1a"])
        ctx1.close()
        ctx2 = ExitStack()
        ev2 = pass_b2(nc, block, ctx2, t_in_b, t_sc_b, sems, entry=ev1)
        ctx2.close()
        ctx3 = ExitStack()
        ev3 = pass_b3(nc, block, ctx3, t_in_b, t_sc_b, sems, entry=ev1 + ev2)
        ctx3.close()

        @block.gpsimd
        def _(g):
            wait_all(g, *ev3)
    ctx.close()
    return nc


_B1_EXTRA = None
KERNEL_MODE = "fused2"


def _in_maps(inputs):
    xw = np.ascontiguousarray(inputs["L0_attn_x"].astype(np.float32))
    Wb, Wa = inputs["L0_attn_Wb"], inputs["L0_attn_Wa"]
    wbar = np.ascontiguousarray(
        np.concatenate([Wb, Wa, inputs["L0_ffn_router_w"]], axis=1).astype(np.float32))
    alog_t = np.tile(np.asarray(inputs["L0_attn_A_log"], np.float32), 64)[None, :]
    dtb_t = np.tile(np.asarray(inputs["L0_attn_dt_bias"], np.float32), 64)[None, :]
    consts = {**host_consts(), **a2_consts()}
    a_maps, b_maps = [], []
    for c in range(8):
        b, hf = c // 2, c % 2
        hsel = np.zeros((128, 2), np.float32)
        hsel[:, 0] = 1.0 - hf
        hsel[:, 1] = float(hf)
        am = {
            "x_full": xw[b],
            "Wq": inputs["L0_attn_Wq"], "Wk": inputs["L0_attn_Wk"],
            "Wv": inputs["L0_attn_Wv"], "wbar": wbar,
            "alog_t": alog_t, "dtb_t": dtb_t,
            "hselr": hsel,
        }
        for cname in A2_CONST_SHAPES:
            am[cname] = consts[cname]
        am["identf"] = consts["identf"]
        bm = {
            "x_half": np.ascontiguousarray(xw[b, hf * HALF:(hf + 1) * HALF]),
            "hsel": hsel,
            "Wg": inputs["L0_attn_Wg"], "Wo": inputs["L0_attn_Wo"],
            "o_norm_w": np.asarray(inputs["L0_attn_o_norm_w"], np.float32)[None, :],
            "W_fc": inputs["L0_attn_W_fc"], "W_proj": inputs["L0_attn_W_proj"],
            "Weg": inputs["L0_ffn_expert_gate_k"], "Weu": inputs["L0_ffn_expert_up_k"],
            "Wed": inputs["L0_ffn_expert_down_k"],
            "Wsg": inputs["L0_ffn_shared_gate_w"], "Wsu": inputs["L0_ffn_shared_up_w"],
            "Wsd": inputs["L0_ffn_shared_down_w"],
            "identf": consts["identf"], "onesrow": consts["onesrow"],
        }
        a_maps.append(am)
        b_maps.append(bm)
    return a_maps, b_maps


def kernel(**inputs):
    a_maps, b_maps = _in_maps(inputs)
    out = np.empty((B, S, D), np.float32)
    if KERNEL_MODE == "2launch":
        nc_a, _, _ = build_a_prog()
        res_a = run_bass_kernel_spmd(nc_a, a_maps, core_ids=list(range(8)))
        for c in range(8):
            b_maps[c]["o_sc"] = res_a.results[c]["o_sc"]
            b_maps[c]["maskT_sc"] = res_a.results[c]["maskT_sc"]
            b_maps[c]["maskH_sc"] = res_a.results[c]["maskH_sc"]
        nc_b, _, _ = build_b_prog()
        res_b = run_bass_kernel_spmd(nc_b, b_maps, core_ids=list(range(8)))
        for c in range(8):
            b, hf = c // 2, c % 2
            out[b, hf * HALF:(hf + 1) * HALF] = res_b.results[c]["out"]
    else:
        nc = build_fused2() if KERNEL_MODE == "fused2" else build_fused()
        maps = [{**a_maps[c], **b_maps[c]} for c in range(8)]
        res = run_bass_kernel_spmd(nc, maps, core_ids=list(range(8)))
        for c in range(8):
            b, hf = c // 2, c % 2
            out[b, hf * HALF:(hf + 1) * HALF] = res.results[c]["out"]
    return out


def pass_b1a(nc, block, ctx, t_in, t_sc, sems, entry=(), banks=(0, 1, 2, 3, 4, 5, 6, 7)):
    e = ctx.enter_context
    identb = t_in["identb_sb"]
    wg = e(nc.sbuf_tensor("b1a_wg", [128, 8 * D], BF16))
    xb = [e(nc.sbuf_tensor(f"b1a_x{i}", [128, D], F32)) for i in range(2)]
    xn = [e(nc.sbuf_tensor(f"b1a_xn{i}", [128, D], F32)) for i in range(2)]
    xnb = [e(nc.sbuf_tensor(f"b1a_xnb{i}", [128, D], BF16)) for i in range(2)]
    xrb = [e(nc.sbuf_tensor(f"b1a_xrb{i}", [128, D], BF16)) for i in range(2)]
    xnT = [e(nc.sbuf_tensor(f"b1a_xnT{i}", [128, D], BF16)) for i in range(2)]
    xrT = [e(nc.sbuf_tensor(f"b1a_xrT{i}", [128, D], BF16)) for i in range(2)]
    gsil = [e(nc.sbuf_tensor(f"b1a_gs{i}", [128, D], BF16)) for i in range(2)]
    sq = [e(nc.sbuf_tensor(f"b1a_sq{i}", [128, D], F32)) for i in range(2)]
    red = e(nc.sbuf_tensor("b1a_red", [128, 8], F32))
    rs = e(nc.sbuf_tensor("b1a_rs", [128, 4], F32))
    ntf = e(nc.sbuf_tensor("b1a_ntf", [128, 4], F32))
    nti = e(nc.sbuf_tensor("b1a_nti", [128, 4], I32))

    PBx = [e(nc.psum_tensor(f"b1a_pb{i}", [128, 512], F32)) for i in range(len(banks))]
    BTx = [BankTracker(p) for p in PBx]
    bTR = [BTx[0], BTx[1]]
    bGT = [BTx[2], BTx[3]]

    sch = Sched(sems)
    sch.entry = [w for w in entry if w is not None]

    def vop(fn):
        return lambda v: (v.drain(), fn(v))[1]

    m_wg = sch.dma("g", lambda g: g.dma_start(out=wg[:].rearrange("p (a j) -> p a j", a=8),
                                              in_=t_in["Wg"].rearrange("(a p) j -> p a j", p=128)),
                   sem="w")
    luse = {"x": [[], []], "sp": [[], []]}
    prev = {}
    for ti in range(NHT):
        ib = ti % 2
        ldsem = "dma_a" if ib == 0 else "dma_b"
        spsem = "spill" if ib == 0 else "spill2"
        r = slice(ti * 128, (ti + 1) * 128)
        m_lx = sch.dma("y", lambda y, r=r, ib=ib: y.dma_start(out=xb[ib][:],
                                                              in_=t_in["x_half"][r, :]),
                       waits=luse["x"][ib], sem=ldsem)
        luse["x"][ib] = []
        m_sqx = sch.add("scl", lambda s, ib=ib: s.activation(out=sq[ib][:], in_=xb[ib][:],
                                                             func=AF.Square),
                        waits=[m_lx, prev.get(("s2red", ib))], sem="scl")
        m_s1 = sch.add("vec", vop(lambda v, ib=ib: v.reduce_sum(red[:, 0:1], xb[ib][:],
                                                                axis=AX.X)),
                       waits=[m_lx, prev.get("red")], sem="vec")
        m_s2 = sch.add("vec", vop(lambda v, ib=ib: v.reduce_sum(red[:, 1:2], sq[ib][:],
                                                                axis=AX.X)),
                       waits=[m_sqx], sem="vec")
        prev[("s2red", ib)] = m_s2

        def stats(v):
            v.drain()
            v.tensor_scalar(red[:, 2:3], red[:, 0:1], 1.0 / D, None, op0=ALU.mult)
            v.drain()
            v.tensor_tensor(red[:, 3:4], red[:, 2:3], red[:, 2:3], op=ALU.mult)
            v.drain()
            v.tensor_scalar(red[:, 4:5], red[:, 1:2], 1.0 / D, None, op0=ALU.mult)
            v.drain()
            v.tensor_tensor(red[:, 3:4], red[:, 4:5], red[:, 3:4], op=ALU.subtract)
            v.drain()
            v.tensor_scalar(red[:, 4:5], red[:, 3:4], LN_EPS, None, op0=ALU.add)
            v.drain()
            v.reciprocal(red[:, 5:6], red[:, 4:5])
            v.drain()
            v.tensor_tensor(red[:, 6:7], red[:, 3:4], red[:, 5:6], op=ALU.mult)
            v.drain()
            v.tensor_scalar(red[:, 5:6], red[:, 6:7], EPS, None, op0=ALU.add)
            v.drain()
            newton_rsqrt(v, rs[:, 0:2], red[:, 4:6], ntf[:, 0:2], nti[:, 0:2])
            return v.tensor_copy(ntf[:, 0:1], rs[:, 0:1])
        m_nw = sch.add("vec", stats, waits=[m_s2], sem="vec")
        m_xn = sch.add("vec", vop(lambda v, ib=ib: v.tensor_scalar(
            xn[ib][:], xb[ib][:], red[:, 2:3], rs[:, 0:1], op0=ALU.subtract, op1=ALU.mult)),
            waits=[m_nw, prev.get(("xnrd", ib))], sem="vec")
        prev["red"] = m_xn
        luse["x"][ib].append(m_xn)
        m_xnb = sch.add("vec", vop(lambda v, ib=ib: v.tensor_copy(xnb[ib][:], xn[ib][:])),
                        waits=[m_xn, prev.get(("xnbrd", ib))], sem="vec")
        m_xrb = sch.add("vec", vop(lambda v, ib=ib: v.tensor_scalar(
            xrb[ib][:], xn[ib][:], rs[:, 1:2], None, op0=ALU.mult)),
            waits=[m_xn, prev.get(("xrbrd", ib))], sem="vec")
        mtx = None
        for a in range(8):
            bt = bTR[0]
            w0 = ([m_xnb] + bt.write_waits() + ([m_wg] if ti == 0 else [])) if a == 0 \
                else ([prev.get(("xnT1", ib))] if a == 4 else [])
            mtx = sch.add("ten", lambda t, a=a, ib=ib, bt=bt: t.matmul(
                bt.ap[:, (a % 4) * 128:(a % 4 + 1) * 128], xnb[ib][:, a * 128:(a + 1) * 128],
                identb, start=True, stop=True), waits=w0, sem="ten")
            if a == 3:
                m_ev1 = sch.add("scl", lambda s, ib=ib, bt=bt: s.copy(
                    out=xnT[ib][:, 0:512], in_=bt.ap[:]),
                    waits=[mtx, prev.get(("xnTrd", ib))], sem="scl")
                prev[("xnT1", ib)] = m_ev1
        m_ev2 = sch.add("scl", lambda s, ib=ib: s.copy(out=xnT[ib][:, 512:1024],
                                                       in_=bTR[0].ap[:]),
                        waits=[mtx], sem="scl")
        bTR[0].add_use(m_ev2)
        prev[("xnbrd", ib)] = m_ev2
        mtx2 = None
        for a in range(8):
            bt = bTR[1]
            w0 = ([m_xrb] + bt.write_waits()) if a == 0 else \
                ([prev.get(("xrT1", ib))] if a == 4 else [])
            mtx2 = sch.add("ten", lambda t, a=a, ib=ib, bt=bt: t.matmul(
                bt.ap[:, (a % 4) * 128:(a % 4 + 1) * 128], xrb[ib][:, a * 128:(a + 1) * 128],
                identb, start=True, stop=True), waits=w0, sem="ten")
            if a == 3:
                m_ev3 = sch.add("scl", lambda s, ib=ib, bt=bt: s.copy(
                    out=xrT[ib][:, 0:512], in_=bt.ap[:]),
                    waits=[mtx2, prev.get(("xrTrd", ib))], sem="scl")
                prev[("xrT1", ib)] = m_ev3
        m_ev4 = sch.add("scl", lambda s, ib=ib: s.copy(out=xrT[ib][:, 512:1024],
                                                       in_=bTR[1].ap[:]),
                        waits=[mtx2], sem="scl")
        bTR[1].add_use(m_ev4)
        prev[("xrbrd", ib)] = m_ev4
        mg = None
        m_gs_last = None
        for half in range(2):
            bt = bGT[half]
            for a in range(8):
                w0 = ([m_ev3, m_ev4] + bt.write_waits()) if a == 0 else []
                mg = sch.add("ten", lambda t, a=a, half=half, ib=ib, bt=bt: t.matmul(
                    bt.ap[:], xrT[ib][:, a * 128:(a + 1) * 128],
                    wg[:, a * 1024 + half * 512: a * 1024 + (half + 1) * 512],
                    start=(a == 0), stop=(a == 7)),
                    waits=w0 + ([m_wg] if ti == 0 and half == 0 and a == 0 else []),
                    sem="ten")
            m_gs_last = sch.add("scl", lambda s, half=half, ib=ib, bt=bt: s.activation(
                out=gsil[ib][:, half * 512:(half + 1) * 512], in_=bt.ap[:], func=AF.Silu),
                waits=[mg, prev.get(("gsrd", ib)) if half == 0 else None], sem="scl")
            bt.add_use(m_gs_last)
        prev[("xrTrd", ib)] = mg
        sch.dma("y", lambda y, ti=ti, ib=ib: y.dma_start(
            out=t_sc["xnT_sc"].rearrange("(a p) t -> p a t", p=128)[:, :, ti * 128:(ti + 1) * 128],
            in_=xnT[ib][:].rearrange("p (a t) -> p a t", a=8)),
            waits=[m_ev2] + luse["sp"][ib], sem=spsem)
        sch.dma("y", lambda y, r=r, ib=ib: y.dma_start(out=t_sc["xn_sc"][r, :],
                                                       in_=xn[ib][:]), sem=spsem)
        m_sp = sch.dma("y", lambda y, r=r, ib=ib: y.dma_start(out=t_sc["gate_sc"][r, :],
                                                              in_=gsil[ib][:]),
                       waits=[m_gs_last], sem=spsem)
        luse["sp"][ib] = [m_sp]
        prev[("xnTrd", ib)] = m_sp
        prev[("xnrd", ib)] = m_sp
        prev[("gsrd", ib)] = m_sp

    sch.plan()
    _b1a_extra = []

    @block.tensor
    def _(t):
        sch.emit_engine("ten", t)

    @block.vector
    def _(v):
        sch.emit_engine("vec", v)

    @block.scalar
    def _(s):
        sch.emit_engine("scl", s)

    @block.sync
    def _(y):
        sch.emit_engine("y", y)

    @block.gpsimd
    def _(g):
        sch.emit_engine("g", g)

    return [Ev(sems[s], sch.final_counts[s])
            for s in ["spill", "spill2"] + _b1a_extra]


def pass_b1b(nc, block, ctx, t_in, t_sc, sems, entry=(), banks=(0, 1, 2, 3, 4)):
    e = ctx.enter_context
    identb = t_in["identb_sb"]
    wo = e(nc.sbuf_tensor("b1b_wo", [128, 8 * D], BF16))
    onw = e(nc.sbuf_tensor("b1b_onw", [128, 128], F32))
    hsel = e(nc.sbuf_tensor("b1b_hsel", [128, 2], F32))
    ut = e(nc.sbuf_tensor("b1b_ut", [128, 128], F32))
    olo = [e(nc.sbuf_tensor(f"b1b_olo{i}", [128, D], BF16)) for i in range(2)]
    ohi = [e(nc.sbuf_tensor(f"b1b_ohi{i}", [128, D], BF16)) for i in range(2)]
    xnl = [e(nc.sbuf_tensor(f"b1b_xn{i}", [128, D], F32)) for i in range(2)]
    gsl = [e(nc.sbuf_tensor(f"b1b_gs{i}", [128, D], BF16)) for i in range(2)]
    osel = [e(nc.sbuf_tensor(f"b1b_os{i}", [128, D], F32)) for i in range(2)]
    og = [e(nc.sbuf_tensor(f"b1b_og{i}", [128, D], BF16)) for i in range(2)]
    ogT = [e(nc.sbuf_tensor(f"b1b_ogT{i}", [128, D], BF16)) for i in range(2)]
    xattn = [e(nc.sbuf_tensor(f"b1b_xa{i}", [128, D], F32)) for i in range(2)]
    mb_ = [e(nc.sbuf_tensor(f"b1b_mb{i}", [128, D], BF16)) for i in range(2)]
    mT = [e(nc.sbuf_tensor(f"b1b_mT{i}", [128, D], BF16)) for i in range(2)]
    sq = [e(nc.sbuf_tensor(f"b1b_sq{i}", [128, D], F32)) for i in range(2)]
    red = e(nc.sbuf_tensor("b1b_red", [128, 18], F32))
    rs = e(nc.sbuf_tensor("b1b_rs", [128, 9], F32))
    ntf = e(nc.sbuf_tensor("b1b_ntf", [128, 9], F32))
    nti = e(nc.sbuf_tensor("b1b_nti", [128, 9], I32))

    PBx = [e(nc.psum_tensor(f"b1b_pb{i}", [128, 512], F32)) for i in range(5)]
    BTx = [BankTracker(p) for p in PBx]
    bTR = [BTx[0], BTx[1]]
    bWO = [BTx[2], BTx[3]]
    bON = BTx[4]

    sch = Sched(sems)
    sch.entry = [w for w in entry if w is not None]

    def vop(fn):
        return lambda v: (v.drain(), fn(v))[1]

    m_wo = sch.dma("g", lambda g: g.dma_start(out=wo[:].rearrange("p (a j) -> p a j", a=8),
                                              in_=t_in["Wo"].rearrange("(a p) j -> p a j", p=128)),
                   sem="w")
    m_hs = sch.dma("g", lambda g: g.dma_start(out=hsel[:], in_=t_in["hsel"]), sem="w")
    m_onr = sch.dma("g", lambda g: g.dma_start(out=ut[0:1, 0:128], in_=t_in["o_norm_w"]),
                    sem="w")
    m_onb = sch.add("ten", lambda t: t.matmul(bON.ap[:, 0:128], t_in["onesrow_sb"],
                                              ut[0:1, 0:128], start=True, stop=True),
                    waits=[m_onr] + bON.write_waits(), sem="ten")
    m_onw = sch.add("vec", vop(lambda v: v.tensor_copy(onw[:], bON.ap[:, 0:128])),
                    waits=[m_onb], sem="vec")
    bON.add_use(m_onw)

    luse = {"olo": [[], []], "ohi": [[], []], "xn": [[], []], "gs": [[], []],
            "sp": [[], []]}
    prev = {}
    for ti in range(NHT):
        ib = ti % 2
        ldsem = "dma_a" if ib == 0 else "dma_b"
        spsem = "spill" if ib == 0 else "spill2"
        r = slice(ti * 128, (ti + 1) * 128)
        sch.dma("y", lambda y, r=r, ib=ib: y.dma_start(out=olo[ib][:], in_=t_sc["o_sc"][r, :]),
                waits=luse["olo"][ib], sem=ldsem)
        sch.dma("y", lambda y, ti=ti, ib=ib: y.dma_start(
            out=ohi[ib][:], in_=t_sc["o_sc"][HALF + ti * 128:HALF + (ti + 1) * 128, :]),
            waits=luse["ohi"][ib], sem=ldsem)
        sch.dma("y", lambda y, r=r, ib=ib: y.dma_start(out=xnl[ib][:], in_=t_sc["xn_sc"][r, :]),
                waits=luse["xn"][ib], sem=ldsem)
        m_lg = sch.dma("y", lambda y, r=r, ib=ib: y.dma_start(out=gsl[ib][:],
                                                              in_=t_sc["gate_sc"][r, :]),
                       waits=luse["gs"][ib], sem=ldsem)
        for kk_ in ["olo", "ohi", "xn", "gs"]:
            luse[kk_][ib] = []
        m_os1 = sch.add("vec", vop(lambda v, ib=ib: v.tensor_scalar(
            osel[ib][:], olo[ib][:], hsel[:, 0:1], None, op0=ALU.mult)),
            waits=[m_lg, m_onr, prev.get(("osrd", ib))], sem="vec")
        m_os2 = sch.add("vec", vop(lambda v, ib=ib: v.scalar_tensor_tensor(
            out=osel[ib][:], in0=ohi[ib][:], scalar=hsel[:, 1:2], in1=osel[ib][:],
            op0=ALU.mult, op1=ALU.add)), waits=[m_os1], sem="vec")
        luse["olo"][ib].append(m_os2)
        luse["ohi"][ib].append(m_os2)
        m_sqo = sch.add("scl", lambda s, ib=ib: s.activation(out=sq[ib][:], in_=osel[ib][:],
                                                             func=AF.Square),
                        waits=[m_os2, prev.get(("s2red", ib))], sem="scl")
        m_ro = sch.add("vec", vop(lambda v, ib=ib: v.reduce_sum(
            red[:, 8:16], sq[ib][:].rearrange("p (h d) -> p h d", h=8), axis=AX.X)),
            waits=[m_sqo, prev.get("red")], sem="vec")

        def onorm(v):
            v.drain()
            v.tensor_scalar(red[:, 8:16], red[:, 8:16], 1.0 / DV, EPS, op0=ALU.mult,
                            op1=ALU.add)
            v.drain()
            newton_rsqrt(v, rs[:, 0:8], red[:, 8:16], ntf[:, 0:8], nti[:, 0:8])
            return v.tensor_copy(ntf[:, 0:1], rs[:, 0:1])
        m_on = sch.add("vec", onorm, waits=[m_ro], sem="vec")
        m_ogl = None
        for h in range(H):
            hh = slice(h * 128, (h + 1) * 128)
            mu = sch.add("vec", vop(lambda v, h=h, hh=hh, ib=ib: v.scalar_tensor_tensor(
                out=ut[:], in0=gsl[ib][:, hh], scalar=rs[:, h:h + 1], in1=onw[:],
                op0=ALU.mult, op1=ALU.mult)),
                waits=[m_on, m_onw] + ([prev.get("ut")] if h == 0 else []), sem="vec")
            m_ogl = sch.add("vec", vop(lambda v, h=h, hh=hh, ib=ib: v.tensor_tensor(
                og[ib][:, hh], ut[:], osel[ib][:, hh], op=ALU.mult)),
                waits=[mu, prev.get(("ogrd", ib)) if h == 0 else None], sem="vec")
        prev["ut"] = m_ogl
        prev[("osrd", ib)] = m_ogl
        luse["gs"][ib].append(m_ogl)
        mto = None
        for a in range(8):
            bt = bTR[a // 4]
            w0 = ([m_ogl] + bt.write_waits()) if a % 4 == 0 else []
            mto = sch.add("ten", lambda t, a=a, ib=ib, bt=bt: t.matmul(
                bt.ap[:, (a % 4) * 128:(a % 4 + 1) * 128], og[ib][:, a * 128:(a + 1) * 128],
                identb, start=True, stop=True), waits=w0, sem="ten")
            if a == 3:
                mto3 = mto
        m_ev5 = sch.add("scl", lambda s, ib=ib: s.copy(out=ogT[ib][:, 0:512], in_=bTR[0].ap[:]),
                        waits=[mto3, prev.get(("ogTrd", ib))], sem="scl")
        bTR[0].add_use(m_ev5)
        m_ev6 = sch.add("scl", lambda s, ib=ib: s.copy(out=ogT[ib][:, 512:1024],
                                                       in_=bTR[1].ap[:]),
                        waits=[mto, prev.get(("ogTrd", ib))], sem="scl")
        bTR[1].add_use(m_ev6)
        prev[("ogrd", ib)] = m_ev6
        mwo = None
        for half in range(2):
            bt = bWO[half]
            for a in range(8):
                w0 = ([m_ev5, m_ev6] + bt.write_waits()) if a == 0 else []
                mwo = sch.add("ten", lambda t, a=a, half=half, ib=ib, bt=bt: t.matmul(
                    bt.ap[:], ogT[ib][:, a * 128:(a + 1) * 128],
                    wo[:, a * 1024 + half * 512: a * 1024 + (half + 1) * 512],
                    start=(a == 0), stop=(a == 7)),
                    waits=w0 + ([m_onr] if ti == 0 and half == 0 and a == 0 else []),
                    sem="ten")
            if half == 0:
                mwo0 = mwo
        prev[("ogTrd", ib)] = mwo
        m_xa1 = sch.add("vec", vop(lambda v, ib=ib: v.tensor_add(
            xattn[ib][:, 0:512], xnl[ib][:, 0:512], bWO[0].ap[:])),
            waits=[mwo0, prev.get(("xard", ib))], sem="vec")
        bWO[0].add_use(m_xa1)
        m_xa2 = sch.add("vec", vop(lambda v, ib=ib: v.tensor_add(
            xattn[ib][:, 512:1024], xnl[ib][:, 512:1024], bWO[1].ap[:])),
            waits=[mwo], sem="vec")
        bWO[1].add_use(m_xa2)
        luse["xn"][ib].append(m_xa2)
        m_sqa = sch.add("scl", lambda s, ib=ib: s.activation(out=sq[ib][:], in_=xattn[ib][:],
                                                             func=AF.Square),
                        waits=[m_xa2, m_ro], sem="scl")
        prev[("s2red", ib)] = None
        m_ra = sch.add("vec", vop(lambda v, ib=ib: v.reduce_sum(red[:, 16:17], sq[ib][:],
                                                                axis=AX.X)),
                       waits=[m_sqa], sem="vec")
        prev["red"] = m_ra
        prev[("s2red", ib)] = m_ra

        def mnorm(v):
            v.drain()
            v.tensor_scalar(red[:, 16:17], red[:, 16:17], 1.0 / D, EPS, op0=ALU.mult,
                            op1=ALU.add)
            v.drain()
            newton_rsqrt(v, rs[:, 8:9], red[:, 16:17], ntf[:, 8:9], nti[:, 8:9])
            return v.tensor_copy(ntf[:, 8:9], rs[:, 8:9])
        m_mn = sch.add("vec", mnorm, waits=[m_ra], sem="vec")
        m_mb = sch.add("vec", vop(lambda v, ib=ib: v.tensor_scalar(
            mb_[ib][:], xattn[ib][:], rs[:, 8:9], None, op0=ALU.mult)),
            waits=[m_mn, prev.get(("mbrd", ib))], sem="vec")
        mtm = None
        for a in range(8):
            bt = bTR[a // 4]
            w0 = ([m_mb] + bt.write_waits()) if a % 4 == 0 else []
            mtm = sch.add("ten", lambda t, a=a, ib=ib, bt=bt: t.matmul(
                bt.ap[:, (a % 4) * 128:(a % 4 + 1) * 128], mb_[ib][:, a * 128:(a + 1) * 128],
                identb, start=True, stop=True), waits=w0, sem="ten")
            if a == 3:
                mtm3 = mtm
        m_ev7 = sch.add("scl", lambda s, ib=ib: s.copy(out=mT[ib][:, 0:512], in_=bTR[0].ap[:]),
                        waits=[mtm3, prev.get(("mTrd", ib))], sem="scl")
        bTR[0].add_use(m_ev7)
        m_ev8 = sch.add("scl", lambda s, ib=ib: s.copy(out=mT[ib][:, 512:1024],
                                                       in_=bTR[1].ap[:]),
                        waits=[mtm, prev.get(("mTrd", ib))], sem="scl")
        bTR[1].add_use(m_ev8)
        prev[("mbrd", ib)] = m_ev8
        wsp = [m_ev8, m_xa2] + luse["sp"][ib]
        sch.dma("y", lambda y, ti=ti, ib=ib: y.dma_start(
            out=t_sc["mT_sc"].rearrange("(a p) t -> p a t", p=128)[:, :, ti * 128:(ti + 1) * 128],
            in_=mT[ib][:].rearrange("p (a t) -> p a t", a=8)), waits=wsp, sem=spsem)
        m_sp3 = sch.dma("y", lambda y, r=r, ib=ib: y.dma_start(
            out=t_sc["xattn_sc"][r, :], in_=xattn[ib][:]), sem=spsem)
        luse["sp"][ib] = [m_sp3]
        prev[("mTrd", ib)] = m_sp3
        prev[("xard", ib)] = m_sp3

    sch.plan()

    @block.tensor
    def _(t):
        sch.emit_engine("ten", t)

    @block.vector
    def _(v):
        sch.emit_engine("vec", v)

    @block.scalar
    def _(s):
        sch.emit_engine("scl", s)

    @block.sync
    def _(y):
        sch.emit_engine("y", y)

    @block.gpsimd
    def _(g):
        sch.emit_engine("g", g)

    return [Ev(sems[s], sch.final_counts[s]) for s in ["spill", "spill2"]]


def pass_mega(nc, block, ctx, t_in, t_sc, sems, entry=()):
    e = ctx.enter_context
    identb = t_in["identb_sb"]
    identf = t_in["identf_sb"]
    cumtri = t_in["cumtri_sb"]
    mstrict = t_in["mstrict_sb"]
    mincl = t_in["mincl_sb"]
    ones128 = t_in["ones128_sb"]
    ident4 = t_in["ident4_sb"]

    qld = [e(nc.sbuf_tensor(f"mg_q{i}", [128, D], BF16)) for i in range(2)]
    kld = [e(nc.sbuf_tensor(f"mg_k{i}", [128, D], BF16)) for i in range(2)]
    vld = [e(nc.sbuf_tensor(f"mg_v{i}", [128, D], BF16)) for i in range(2)]
    g_all = e(nc.sbuf_tensor("mg_g_all", [128, NCH * 8 + 128], F32))
    beta_all = e(nc.sbuf_tensor("mg_beta", [128, NCH * 8], F32))
    nbeta_all = e(nc.sbuf_tensor("mg_nbeta", [128, NCH * 8], F32))
    prow = e(nc.sbuf_tensor("mg_prow", [1, 1024], F32))
    l_sb = e(nc.sbuf_tensor("mg_l", [128, 128], F32))
    nl = e(nc.sbuf_tensor("mg_nl", [128, 8], F32))
    A_all = e(nc.sbuf_tensor("mg_A", [128, 8], F32))
    lrow8 = e(nc.sbuf_tensor("mg_lrow8", [64, 128], F32))
    lrow_dram = nc.dram_tensor("mg_lrow_dram", [8, 128], F32, kind="Internal").ap()
    E_all = e(nc.sbuf_tensor("mg_E", [128, 1024], F32))
    kT_all = e(nc.sbuf_tensor("mg_kT", [128, 1024], BF16))
    qT_all = e(nc.sbuf_tensor("mg_qT", [128, 1024], BF16))
    KK_all = e(nc.sbuf_tensor("mg_KK", [128, 1024], BF16))
    QK_all = e(nc.sbuf_tensor("mg_QK", [128, 1024], BF16))
    Ntil_a = e(nc.sbuf_tensor("mg_N", [128, 1024], BF16))
    NtilT_a = e(nc.sbuf_tensor("mg_NT", [128, 1024], BF16))
    G_a = e(nc.sbuf_tensor("mg_G", [128, 1024], BF16))
    curU_a = e(nc.sbuf_tensor("mg_cu", [128, 1024], BF16))
    NTp_a = [e(nc.sbuf_tensor(f"mg_np{j}", [128, 1024], BF16)) for j in range(5)]
    Qb_a = [e(nc.sbuf_tensor(f"mg_Qb{i}", [128, 1024], BF16)) for i in range(2)]
    Zv_a = e(nc.sbuf_tensor("mg_Zv", [128, 1024], BF16))
    W2T_a = e(nc.sbuf_tensor("mg_W2T", [128, 1024], BF16))
    ak_a = e(nc.sbuf_tensor("mg_ak", [128, 1024], BF16))
    kp_a = e(nc.sbuf_tensor("mg_kp", [128, 1024], BF16))
    delta_a = e(nc.sbuf_tensor("mg_dl", [128, 1024], BF16))
    zt_a = e(nc.sbuf_tensor("mg_zt", [128, 1024], F32))
    oc_a = e(nc.sbuf_tensor("mg_oc", [128, 1024], F32))
    oi_a = e(nc.sbuf_tensor("mg_oi", [128, 1024], F32))
    Sst_a = e(nc.sbuf_tensor("mg_S", [128, 1024], BF16))
    S32_a = [e(nc.sbuf_tensor(f"mg_S32{i}", [128, 1024], F32)) for i in range(2)]
    o_out = [e(nc.sbuf_tensor(f"mg_oo{i}", [128, D], BF16)) for i in range(2)]

    PB = [e(nc.psum_tensor(f"mg_pb{i}", [128, 512], F32)) for i in range(5)]
    BT = [BankTracker(PB[i]) for i in range(5)]
    PB3 = [e(nc.psum_tensor(f"mg_p3{i}", [128, 512], F32)) for i in range(3)]
    BT3 = [BankTracker(PB3[i]) for i in range(3)]

    sch = Sched(sems)
    sch.entry = [w for w in entry if w is not None]
    TEN, VEC, SCL = "ten", "vec", "scl"

    def vop(fn):
        return lambda v: (v.drain(), fn(v))[1]

    def hs(h):
        return slice(h * 128, (h + 1) * 128)

    ctx_p = ExitStack()
    ep = ctx_p.enter_context
    ba_all = ep(nc.sbuf_tensor("mg_ba_all", [128, NCH * 16], F32))
    pbc = ep(nc.sbuf_tensor("mg_pbc", [128, 1024], F32))
    ptmp = [ep(nc.sbuf_tensor(f"mg_pt{i}", [128, NCH * 8], F32)) for i in range(2)]

    m_ld_alog = sch.dma("g", lambda g: g.dma_start(out=prow[0:1, 0:512], in_=t_in["alog_t"]),
                        sem="w")
    m_ld_dtb = sch.dma("g", lambda g: g.dma_start(out=prow[0:1, 512:1024], in_=t_in["dtb_t"]),
                       sem="w")
    m_ld_ba = sch.dma("g", lambda g: g.dma_start(
        out=ba_all[:].rearrange("p (c j) -> p c j", j=16),
        in_=t_sc["balg_sc"][:, 0:16].rearrange("(c p) j -> p c j", p=128)), sem="w")
    m_expal = sch.add(SCL, lambda s: s.activation(out=prow[0:1, 0:512], in_=prow[0:1, 0:512],
                                                  func=AF.Exp),
                      waits=[m_ld_ba], sem="scl")
    m_bc1 = sch.add(TEN, lambda t: t.matmul(PB[0][:, 0:512], t_in["onesrow_sb"],
                                            prow[0:1, 0:512], start=True, stop=True),
                    waits=[m_expal, m_ld_ba] + BT[0].write_waits(), sem="ten")
    m_bc2 = sch.add(TEN, lambda t: t.matmul(PB[1][:, 0:512], t_in["onesrow_sb"],
                                            prow[0:1, 512:1024], start=True, stop=True),
                    waits=[m_ld_ba] + BT[1].write_waits(), sem="ten")
    m_nea = sch.add(VEC, vop(lambda v: v.tensor_scalar(pbc[:, 0:512], PB[0][:, 0:512], -1.0,
                                                       None, op0=ALU.mult)),
                    waits=[m_bc1], sem="vec")
    BT[0].add_use(m_nea)
    m_dtb = sch.add(VEC, vop(lambda v: v.tensor_copy(pbc[:, 512:1024], PB[1][:, 0:512])),
                    waits=[m_bc2], sem="vec")
    BT[1].add_use(m_dtb)
    m_eb = sch.add(SCL, lambda s: s.activation(
        out=ptmp[0][:].rearrange("p (c j) -> p c j", j=8),
        in_=ba_all[:].rearrange("p (c j) -> p c j", j=16)[:, :, 0:8],
        func=AF.Exp, scale=-1.0), waits=[m_ld_ba], sem="scl")
    m_eb1 = sch.add(VEC, vop(lambda v: v.tensor_scalar(ptmp[0][:], ptmp[0][:], 1.0, None,
                                                       op0=ALU.add)),
                    waits=[m_eb], sem="vec")
    m_beta = sch.add(VEC, vop(lambda v: v.reciprocal(beta_all[:], ptmp[0][:])),
                     waits=[m_eb1], sem="vec")
    m_nbeta = sch.add(VEC, vop(lambda v: v.tensor_scalar(nbeta_all[:], beta_all[:], -1.0,
                                                         None, op0=ALU.mult)),
                      waits=[m_beta], sem="vec")
    dtb_rep = pbc[:, 512:512 + NCH * 8].rearrange("p (c j) -> p c j", j=8, c=NCH)
    m_ain = sch.add(VEC, vop(lambda v: v.tensor_tensor(
        ptmp[0][:].rearrange("p (c j) -> p c j", j=8),
        ba_all[:].rearrange("p (c j) -> p c j", j=16)[:, :, 8:16],
        dtb_rep, op=ALU.add)), waits=[m_ld_ba, m_dtb], sem="vec")
    m_absx = sch.add(SCL, lambda s: s.activation(out=ptmp[1][:], in_=ptmp[0][:], func=AF.Abs),
                     waits=[m_ain], sem="scl")
    m_emx = sch.add(SCL, lambda s: s.activation(out=ptmp[1][:], in_=ptmp[1][:], func=AF.Exp,
                                                scale=-1.0),
                    waits=[m_absx], sem="scl")
    m_lp = sch.add(VEC, vop(lambda v: v.tensor_scalar(ptmp[1][:], ptmp[1][:], 1.0, None,
                                                      op0=ALU.add)),
                   waits=[m_emx], sem="vec")
    m_ln = sch.add(SCL, lambda s: s.activation(out=ptmp[1][:], in_=ptmp[1][:], func=AF.Ln),
                   waits=[m_lp], sem="scl")
    m_mx = sch.add(VEC, vop(lambda v: v.tensor_scalar(ptmp[0][:], ptmp[0][:], 0.0, None,
                                                      op0=ALU.max)),
                   waits=[m_ln], sem="vec")
    m_sp = sch.add(VEC, vop(lambda v: v.tensor_tensor(ptmp[0][:], ptmp[0][:], ptmp[1][:],
                                                      op=ALU.add)),
                   waits=[m_mx], sem="vec")
    negea_rep = pbc[:, 0:NCH * 8].rearrange("p (c j) -> p c j", j=8, c=NCH)
    m_gall = sch.add(VEC, vop(lambda v: v.tensor_tensor(
        g_all[:, 0:NCH * 8].rearrange("p (c j) -> p c j", j=8),
        ptmp[0][:].rearrange("p (c j) -> p c j", j=8),
        negea_rep, op=ALU.mult)), waits=[m_sp, m_nea], sem="vec")
    m_sinit = sch.add(VEC, vop(lambda v: v.memset(Sst_a[:], 0.0)), sem="vec")
    m_sinit = sch.add(VEC, vop(lambda v: v.memset(S32_a[0][:], 0.0)), waits=[m_sinit],
                      sem="vec")
    m_ms0 = sch.add(VEC, vop(lambda v: v.memset(PB[0][:, 0:128], 0.0)),
                    waits=BT[0].write_waits(), sem="vec")
    BT[0].add_use(m_ms0)
    m_ms1 = sch.add(VEC, vop(lambda v: v.memset(PB[0][0:64, 128:256], 0.0)), sem="vec")
    BT[0].add_use(m_ms1)
    m_msg = sch.add(VEC, vop(lambda v: v.memset(g_all[:, NCH * 8:NCH * 8 + 128], 0.0)),
                    sem="vec")
    ev_prol = m_msg
    ctx_p.close()

    PH = HALF // 2
    NTGP = PH // 512
    PPU = NTGP * 4
    NUNITS = 2 * NE
    xnsl = e(nc.sbuf_tensor("mg_xnsl", [128, 8 * PH], BF16))
    wge = [e(nc.sbuf_tensor(f"mg_wg{i}", [128, 8 * NI], BF16)) for i in range(2)]
    wue = [e(nc.sbuf_tensor(f"mg_wu{i}", [128, 8 * NI], BF16)) for i in range(2)]
    tnh = [e(nc.sbuf_tensor(f"mg_tnh{i}", [128, 512], F32)) for i in range(4)]
    xgb = [e(nc.sbuf_tensor(f"mg_xg{i}", [128, 512], F32)) for i in range(4)]
    umf = [e(nc.sbuf_tensor(f"mg_um{i}", [128, 512], F32)) for i in range(4)]
    gsb = e(nc.sbuf_tensor("mg_gsb", [128, 512], F32))
    sgm = e(nc.sbuf_tensor("mg_sgm", [128, 512], F32))
    amb4 = [e(nc.sbuf_tensor(f"mg_am{i}", [128, 512], BF16)) for i in range(4)]
    mkb = [e(nc.sbuf_tensor(f"mg_mk{i}", [128, 512], F32)) for i in range(2)]
    AMSEM = ["ama", "amb", "amc", "amd"]
    MKSEM = ["mkb", "mkc"]

    st = {"mw": [None, None], "wfree": [[ev_prol], [ev_prol]],
          "mmk": [None, None], "mkfree": [[ev_prol], [ev_prol]],
          "amfree": [[] for _ in range(4)],
          "tnh": [ev_prol] * 4, "xg": [ev_prol] * 4, "um": [None] * 4,
          "mxn": [None, None], "mask_issued": -1, "last_ten": None,
          "umf_by_stream": {}}

    def issue_weights(u):
        if u >= NUNITS:
            return
        ei, eb = u % NE, u % 2
        wsem = "ewa" if eb == 0 else "ewb"
        sch.dma("y", lambda y, ei=ei, eb=eb: y.dma_start(
            out=wge[eb][:].rearrange("p (a j) -> p a j", a=8),
            in_=t_sc["wge_sc"][ei].rearrange("(a p) j -> p a j", p=128)),
            waits=st["wfree"][eb], sem=wsem)
        st["mw"][eb] = sch.dma("y", lambda y, ei=ei, eb=eb: y.dma_start(
            out=wue[eb][:].rearrange("p (a j) -> p a j", a=8),
            in_=t_sc["wue_sc"][ei].rearrange("(a p) j -> p a j", p=128)), sem=wsem)
        st["wfree"][eb] = []

    def issue_mask(s):
        if s >= NUNITS * NTGP:
            return
        u, tg = s // NTGP, s % NTGP
        p, ei = u // NE, u % NE
        tb = s % 2
        c0 = p * PH + tg * 512
        st["mmk"][tb] = sch.dma("g", lambda g, ei=ei, c0=c0, tb=tb: g.dma_start(
            out=mkb[tb][:], in_=t_sc["maskH_sc"][ei:ei + 1, c0:c0 + 512]
            .broadcast_to([128, 512])), waits=st["mkfree"][tb], sem=MKSEM[tb])
        st["mkfree"][tb] = []
        st["mask_issued"] = s

    def issue_xnsl(p, waits):
        st["mxn"][p] = sch.dma("g", lambda g, p=p: g.dma_start(
            out=xnsl[:].rearrange("p (a t) -> p a t", a=8),
            in_=t_sc["xnT_sc"].rearrange("(a p) t -> p a t", p=128)
            [:, :, p * PH:(p + 1) * PH]), waits=waits, sem="w")

    def b3a_gen():
        q = 0
        issue_xnsl(0, [ev_prol])
        issue_weights(0)
        issue_mask(0)
        issue_mask(1)
        rot = 0
        for u in range(NUNITS):
            p, ei, eb = u // NE, u % NE, u % 2
            issue_weights(u + 1)
            for tg in range(NTGP):
                s = u * NTGP + tg
                tb = s % 2
                if st["mask_issued"] < s + 1:
                    issue_mask(s + 1)
                for iblk in range(4):
                    pr = q % 4
                    ab = q % 4
                    first_pair = (tg == 0 and iblk == 0)
                    bg = BT3[rot % 3]; rot += 1
                    bu = BT3[rot % 3]; rot += 1
                    mmg = None
                    for a in range(8):
                        w0 = []
                        if a == 0:
                            w0 = bg.write_waits()
                            if first_pair:
                                w0 += [st["mw"][eb], st["mxn"][p]]
                        mmg = sch.add(TEN, lambda t, a=a, iblk=iblk, tg=tg, eb=eb, bg=bg:
                                      t.matmul(bg.ap[:],
                                               wge[eb][:, a * NI + iblk * 128:
                                                       a * NI + (iblk + 1) * 128],
                                               xnsl[:, a * PH + tg * 512:
                                                    a * PH + tg * 512 + 512],
                                               start=(a == 0), stop=(a == 7)),
                                      waits=w0, sem="ten")
                    m_tn = sch.add(SCL, lambda sc, pr=pr, bg=bg: sc.activation(
                        out=tnh[pr][:], in_=bg.ap[:], func=AF.Tanh, scale=0.5),
                        waits=[mmg, st["tnh"][pr]], sem="scl")
                    m_xg = sch.add(SCL, lambda sc, pr=pr, bg=bg: sc.copy(
                        out=xgb[pr][:], in_=bg.ap[:]),
                        waits=[st["xg"][pr]], sem="scl")
                    bg.add_use(m_xg)
                    mmu = None
                    for a in range(8):
                        w0 = bu.write_waits() if a == 0 else []
                        mmu = sch.add(TEN, lambda t, a=a, iblk=iblk, tg=tg, eb=eb, bu=bu:
                                      t.matmul(bu.ap[:],
                                               wue[eb][:, a * NI + iblk * 128:
                                                       a * NI + (iblk + 1) * 128],
                                               xnsl[:, a * PH + tg * 512:
                                                    a * PH + tg * 512 + 512],
                                               start=(a == 0), stop=(a == 7)),
                                      waits=w0, sem="ten")
                    st["last_ten"] = mmu
                    m_um = sch.add(VEC, vop(lambda v, pr=pr, tb=tb, bu=bu: v.tensor_tensor(
                        umf[pr][:], bu.ap[:], mkb[tb][:], op=ALU.mult)),
                        waits=[mmu, st["mmk"][tb], st["um"][pr]], sem="vec")
                    bu.add_use(m_um)
                    st["umf_by_stream"].setdefault(s, []).append(m_um)

                    def pool_ops(g, pr=pr, ab=ab):
                        g.drain()
                        g.tensor_scalar(sgm[:], tnh[pr][:], 0.5, 0.5, op0=ALU.mult,
                                        op1=ALU.add)
                        g.drain()
                        g.tensor_tensor(gsb[:], sgm[:], xgb[pr][:], op=ALU.mult)
                        g.drain()
                        return g.tensor_tensor(amb4[ab][:], umf[pr][:], gsb[:], op=ALU.mult)
                    m_am = sch.add("g", pool_ops,
                                   waits=[m_xg, m_um] + st["amfree"][ab], sem="pool")
                    st["amfree"][ab] = []
                    st["tnh"][pr] = m_am
                    st["xg"][pr] = m_am
                    st["um"][pr] = m_am
                    c0 = p * PH + tg * 512
                    m_spam = sch.dma("y", lambda y, ei=ei, iblk=iblk, c0=c0, ab=ab:
                                     y.dma_start(out=t_sc["actm_sc"][
                                         ei, iblk * 128:(iblk + 1) * 128, c0:c0 + 512],
                                         in_=amb4[ab][:]),
                                     waits=[m_am], sem=AMSEM[ab])
                    st["amfree"][ab] = [m_spam]
                    q += 1
                    yield
                st["mkfree"][tb] = [st["umf_by_stream"][s][-1]]
                del st["umf_by_stream"][s]
            st["wfree"][eb] = [m_xg, m_um]
            if u == NE - 1:
                issue_xnsl(1, [m_xg, m_um])

    gen = b3a_gen()
    due = [0]

    def b3a_step(k):
        for _ in range(k):
            if due[0] <= 0:
                return
            if next(gen, "END") == "END":
                return
            due[0] -= 1

    luse = {"q": [[], []], "k": [[], []], "v": [[], []], "oo": [[], []]}
    pu = {}
    last_S = m_sinit
    pu_lbc_rd = None

    for ci in range(NCH):
        due[0] += PPU * NUNITS // NCH
        ib = ci % 2
        ldsem = "dma_a" if ib == 0 else "dma_b"
        spsem = "spill" if ib == 0 else "spill2"
        r = slice(ci * 128, (ci + 1) * 128)
        m_lq = sch.dma("y", lambda y, r=r, ib=ib: y.dma_start(out=qld[ib][:],
                                                              in_=t_sc["q_sc"][r, :]),
                       waits=luse["q"][ib], sem=ldsem)
        m_lk = sch.dma("y", lambda y, r=r, ib=ib: y.dma_start(out=kld[ib][:],
                                                              in_=t_sc["k_sc"][r, :]),
                       waits=luse["k"][ib], sem=ldsem)
        m_lv = sch.dma("y", lambda y, r=r, ib=ib: y.dma_start(out=vld[ib][:],
                                                              in_=t_sc["v_sc"][r, :]),
                       waits=luse["v"][ib], sem=ldsem)
        for kk_ in ["q", "k", "v"]:
            luse[kk_][ib] = []

        m_lmm = sch.add(TEN, lambda t, ci=ci: t.matmul(PB[0][:, 0:128], cumtri,
                                                       g_all[:, ci * 8:ci * 8 + 128],
                                                       start=True, stop=True),
                        waits=[m_gall, m_msg] + BT[0].write_waits(), sem="ten")
        b3a_step(2)
        m_lsb = sch.add(VEC, vop(lambda v: v.tensor_copy(l_sb[:], PB[0][:, 0:128])),
                        waits=[m_lmm, pu.get("l_sb")], sem="vec")
        BT[0].add_use(m_lsb)
        m_nl = sch.add(VEC, vop(lambda v: v.tensor_scalar(nl[:], l_sb[:, 0:8], -1.0, None,
                                                          op0=ALU.mult)),
                       waits=[m_lsb, pu.get("nl")], sem="vec")
        m_A = sch.add(SCL, lambda s: s.activation(out=A_all[:], in_=l_sb[:, 0:8],
                                                  func=AF.Exp),
                      waits=[m_lsb, pu.get("A_all")], sem="scl")
        m_lt = sch.add(TEN, lambda t: t.transpose(PB[0][0:64, 128:256], l_sb[:, 0:64],
                                                  identf),
                       waits=[m_lsb] + BT[0].write_waits(), sem="ten")
        b3a_step(2)
        m_lr8 = sch.add(SCL, lambda s: s.copy(out=lrow8[:], in_=PB[0][0:64, 128:256]),
                        waits=[m_lt, pu.get("lrow8")], sem="scl")
        BT[0].add_use(m_lr8)
        m_lw_ = sch.dma("g", lambda g: g.dma_start(out=lrow_dram, in_=lrow8[0:8, :]),
                        waits=[m_lr8, pu_lbc_rd], sem="mka")
        m_lbc = sch.dma("g", lambda g: g.dma_start(
            out=E_all[:], in_=lrow_dram.rearrange("h j -> (h j)")[None, :]
            .broadcast_to([128, 1024])), waits=[m_lw_] + pu.get("E_ld", []), sem="mka")
        pu_lbc_rd = m_lbc
        m_ecl = None
        for h in range(H):
            m_ecl = sch.add(VEC, vop(lambda v, h=h: v.tensor_scalar(
                E_all[:, hs(h)], E_all[:, hs(h)], nl[:, h:h + 1], 0.0,
                op0=ALU.add, op1=ALU.min)),
                waits=[m_lbc, m_nl] if h == 0 else [], sem="vec")
        m_E = sch.add(SCL, lambda s: s.activation(out=E_all[:], in_=E_all[:], func=AF.Exp),
                      waits=[m_ecl], sem="scl")

        mm = None
        for h in range(H):
            bt = BT[1 + h // 2]
            w0 = ([m_lv] + bt.write_waits() + [pu.get("kT_all") if h == 0 else None,
                                              pu.get("qT_all") if h == 0 else None]) \
                if h % 2 == 0 else []
            sch.add(TEN, lambda t, h=h, ib=ib, bt=bt: t.matmul(
                bt.ap[:, (h % 2) * 128:(h % 2) * 128 + 128], kld[ib][:, hs(h)], identb,
                start=True, stop=True), waits=w0, sem="ten")
            mm = sch.add(TEN, lambda t, h=h, ib=ib, bt=bt: t.matmul(
                bt.ap[:, 256 + (h % 2) * 128:256 + (h % 2) * 128 + 128],
                qld[ib][:, hs(h)], identb, start=True, stop=True), sem="ten")
        luse["q"][ib].append(mm)
        b3a_step(2)
        m_kqev = []
        for bi in range(4):
            ev1 = sch.add(SCL, lambda s, bi=bi: s.copy(out=kT_all[:, bi * 256:(bi + 1) * 256],
                                                       in_=PB[1 + bi][:, 0:256]),
                          waits=[mm], sem="scl")
            ev2 = sch.add(SCL, lambda s, bi=bi: s.copy(out=qT_all[:, bi * 256:(bi + 1) * 256],
                                                       in_=PB[1 + bi][:, 256:512]),
                          sem="scl")
            BT[1 + bi].add_use(ev2)
            m_kqev.append(ev2)
        for h in range(H):
            bt = BT[1 + h // 2]
            w0 = ([m_kqev[3]] + bt.write_waits() + [pu.get("KK_all") if h == 0 else None,
                                                    pu.get("QK_all") if h == 0 else None]) \
                if h % 2 == 0 else []
            sch.add(TEN, lambda t, h=h, bt=bt: t.matmul(
                bt.ap[:, (h % 2) * 128:(h % 2) * 128 + 128], kT_all[:, hs(h)],
                kT_all[:, hs(h)], start=True, stop=True), waits=w0, sem="ten")
            mm = sch.add(TEN, lambda t, h=h, bt=bt: t.matmul(
                bt.ap[:, 256 + (h % 2) * 128:256 + (h % 2) * 128 + 128],
                kT_all[:, hs(h)], qT_all[:, hs(h)], start=True, stop=True), sem="ten")
        b3a_step(2)
        m_kkev = []
        for bi in range(4):
            ev1 = sch.add(SCL, lambda s, bi=bi: s.copy(out=KK_all[:, bi * 256:(bi + 1) * 256],
                                                       in_=PB[1 + bi][:, 0:256]),
                          waits=[mm], sem="scl")
            ev2 = sch.add(SCL, lambda s, bi=bi: s.copy(out=QK_all[:, bi * 256:(bi + 1) * 256],
                                                       in_=PB[1 + bi][:, 256:512]),
                          sem="scl")
            BT[1 + bi].add_use(ev2)
            m_kkev.append(ev2)
        m_n1 = None
        for h in range(H):
            m_n1 = sch.add(VEC, vop(lambda v, h=h, ci=ci: v.scalar_tensor_tensor(
                out=zt_a[:, hs(h)], in0=KK_all[:, hs(h)],
                scalar=nbeta_all[:, ci * 8 + h:ci * 8 + h + 1], in1=E_all[:, hs(h)],
                op0=ALU.mult, op1=ALU.mult)),
                waits=[m_E, m_kkev[3], m_nbeta, pu.get("zt_a")] if h == 0 else [],
                sem="vec")
        m_n2 = None
        for h in range(H):
            m_n2 = sch.add(VEC, vop(lambda v, h=h: v.tensor_tensor(
                Ntil_a[:, hs(h)], zt_a[:, hs(h)], mstrict, op=ALU.mult)),
                waits=[m_n1, pu.get("Ntil_a")] if h == 0 else [], sem="vec")
        m_g1 = sch.add(VEC, vop(lambda v: v.tensor_tensor(zt_a[:], E_all[:], QK_all[:],
                                                          op=ALU.mult)),
                       waits=[m_n2], sem="vec")
        m_g2 = None
        for h in range(H):
            m_g2 = sch.add(VEC, vop(lambda v, h=h: v.tensor_tensor(
                G_a[:, hs(h)], zt_a[:, hs(h)], mincl, op=ALU.mult)),
                waits=[m_g1, pu.get("G_a")] if h == 0 else [], sem="vec")
        for h in range(H):
            bt = BT[3] if h < 4 else BT[4]
            w0 = ([m_n2] + bt.write_waits() + [pu.get("NtilT_a") if h == 0 else None]) \
                if h % 4 == 0 else []
            mm = sch.add(TEN, lambda t, h=h, bt=bt: t.matmul(
                bt.ap[:, (h % 4) * 128:(h % 4) * 128 + 128], Ntil_a[:, hs(h)], identb,
                start=True, stop=True), waits=w0, sem="ten")
        b3a_step(2)
        m_ntev = []
        for bi in range(2):
            evv = sch.add(SCL, lambda s, bi=bi: s.copy(
                out=NtilT_a[:, bi * 512:(bi + 1) * 512], in_=PB[3 + bi][:]),
                waits=[mm], sem="scl")
            BT[3 + bi].add_use(evv)
            m_ntev.append(evv)
        m_sqev = None
        for j in range(5):
            aU = Ntil_a if j == 0 else curU_a
            aT = NtilT_a if j == 0 else NTp_a[j - 1]
            dep = m_ntev[1] if j == 0 else m_sqev
            for h in range(H):
                btu = BT[1] if h < 4 else BT[2]
                btt = BT[3] if h < 4 else BT[4]
                w0 = []
                if h == 0:
                    w0 = [dep] + btu.write_waits() + btt.write_waits() + \
                        BT[2].write_waits() + BT[4].write_waits()
                    if j == 0:
                        w0 += [pu.get("curU_a"), pu.get("NTp0")]
                sch.add(TEN, lambda t, h=h, aU=aU, aT=aT, btu=btu: t.matmul(
                    btu.ap[:, (h % 4) * 128:(h % 4) * 128 + 128], aT[:, hs(h)], aU[:, hs(h)],
                    start=True, stop=True), waits=w0, sem="ten")
                mm = sch.add(TEN, lambda t, h=h, aU=aU, aT=aT, btt=btt: t.matmul(
                    btt.ap[:, (h % 4) * 128:(h % 4) * 128 + 128], aU[:, hs(h)], aT[:, hs(h)],
                    start=True, stop=True), sem="ten")
            b3a_step(1)
            ev1 = sch.add(SCL, lambda s: s.copy(out=curU_a[:, 0:512], in_=PB[1][:]),
                          waits=[mm], sem="scl")
            ev2 = sch.add(SCL, lambda s: s.copy(out=curU_a[:, 512:1024], in_=PB[2][:]),
                          sem="scl")
            ev3 = sch.add(SCL, lambda s, j=j: s.copy(out=NTp_a[j][:, 0:512], in_=PB[3][:]),
                          sem="scl")
            ev4 = sch.add(SCL, lambda s, j=j: s.copy(out=NTp_a[j][:, 512:1024], in_=PB[4][:]),
                          sem="scl")
            BT[1].add_use(ev1)
            BT[2].add_use(ev2)
            BT[3].add_use(ev3)
            BT[4].add_use(ev4)
            m_sqev = ev4
        for h in range(H):
            bt = BT[1] if h < 4 else BT[2]
            w0 = ([m_sqev] + bt.write_waits()) if h % 4 == 0 else []
            mm = sch.add(TEN, lambda t, h=h, bt=bt: t.matmul(
                bt.ap[:, (h % 4) * 128:(h % 4) * 128 + 128], NTp_a[4][:, hs(h)],
                curU_a[:, hs(h)], start=True, stop=True), waits=w0, sem="ten")
        b3a_step(1)
        m_q = sch.add(VEC, vop(lambda v: v.tensor_tensor(Qb_a[0][:, 0:512], PB[1][:],
                                                         ident4, op=ALU.add)),
                      waits=[mm, pu.get("Qb0")], sem="vec")
        m_q = sch.add(VEC, vop(lambda v: v.tensor_tensor(Qb_a[0][:, 512:1024], PB[2][:],
                                                         ident4, op=ALU.add)),
                      waits=[m_q], sem="vec")
        BT[1].add_use(m_q)
        BT[2].add_use(m_q)
        for lev in range(6):
            src_ = NTp_a[4 - lev] if lev < 5 else NtilT_a
            qin = Qb_a[lev % 2]
            qout = Qb_a[(lev + 1) % 2]
            for h in range(H):
                bt = BT[1] if h < 4 else BT[2]
                w0 = []
                if h == 0:
                    w0 = [m_q] + bt.write_waits() + BT[2].write_waits()
                    if lev == 0:
                        w0.append(pu.get("Qb1"))
                mm = sch.add(TEN, lambda t, h=h, src_=src_, qin=qin, bt=bt: t.matmul(
                    bt.ap[:, (h % 4) * 128:(h % 4) * 128 + 128], src_[:, hs(h)],
                    qin[:, hs(h)], start=True, stop=True), waits=w0, sem="ten")
            if lev % 2 == 1:
                b3a_step(1)
            mq1 = sch.add(VEC, vop(lambda v, qin=qin, qout=qout: v.tensor_tensor(
                qout[:, 0:512], PB[1][:], qin[:, 0:512], op=ALU.add)),
                waits=[mm], sem="vec")
            m_q = sch.add(VEC, vop(lambda v, qin=qin, qout=qout: v.tensor_tensor(
                qout[:, 512:1024], PB[2][:], qin[:, 512:1024], op=ALU.add)),
                waits=[mq1], sem="vec")
            BT[1].add_use(m_q)
            BT[2].add_use(m_q)
        m_T = m_q
        m_negA8 = sch.add(VEC, vop(lambda v: v.tensor_scalar(nl[:], A_all[:], -1.0, None,
                                                             op0=ALU.mult)),
                          waits=[m_A, m_ecl], sem="vec")
        m_akl = None
        for h in range(H):
            m_akl = sch.add(VEC, vop(lambda v, h=h, ib=ib: v.tensor_scalar(
                ak_a[:, hs(h)], kld[ib][:, hs(h)], nl[:, h:h + 1], None, op0=ALU.mult)),
                waits=[m_negA8] + ([pu.get("ak_a"), m_lv] if h == 0 else []), sem="vec")
        m_kpl = None
        for h in range(H):
            m_kpl = sch.add(VEC, vop(lambda v, h=h, ib=ib: v.tensor_scalar(
                kp_a[:, hs(h)], kld[ib][:, hs(h)],
                E_all[:, h * 128 + 127:h * 128 + 128], None, op0=ALU.mult)),
                waits=[m_akl] + ([pu.get("kp_a")] if h == 0 else []), sem="vec")
        luse["k"][ib].append(m_kpl)
        for h in range(H):
            btz = BT[1 + h // 2]
            w0 = ([m_T, m_lv, m_akl] + btz.write_waits()) if h % 2 == 0 else []
            sch.add(TEN, lambda t, h=h, ib=ib, btz=btz: t.matmul(
                btz.ap[:, (h % 2) * 128:(h % 2) * 128 + 128], Qb_a[0][:, hs(h)],
                vld[ib][:, hs(h)], start=True, stop=True), waits=w0, sem="ten")
            mm = sch.add(TEN, lambda t, h=h, btz=btz: t.matmul(
                btz.ap[:, 256 + (h % 2) * 128:256 + (h % 2) * 128 + 128], ak_a[:, hs(h)],
                Qb_a[0][:, hs(h)], start=True, stop=True), sem="ten")
        luse["v"][ib].append(mm)
        b3a_step(2)
        m_zwev = []
        for bi in range(4):
            ev1 = sch.add(SCL, lambda s, bi=bi: s.copy(out=Zv_a[:, bi * 256:(bi + 1) * 256],
                                                       in_=PB[1 + bi][:, 0:256]),
                          waits=[mm, pu.get("Zv_a") if bi == 0 else None], sem="scl")
            ev2 = sch.add(SCL, lambda s, bi=bi: s.copy(out=W2T_a[:, bi * 256:(bi + 1) * 256],
                                                       in_=PB[1 + bi][:, 256:512]),
                          waits=[pu.get("W2T_a") if bi == 0 else None], sem="scl")
            BT[1 + bi].add_use(ev2)
            m_zwev.append(ev2)
        for h in range(H):
            bt = BT[3] if h < 4 else BT[4]
            w0 = ([m_zwev[3], last_S] + bt.write_waits()) if h % 4 == 0 else []
            sch.add(TEN, lambda t, h=h, bt=bt: t.matmul(
                bt.ap[:, (h % 4) * 128:(h % 4) * 128 + 128], W2T_a[:, hs(h)],
                Sst_a[:, hs(h)], start=True, stop=False), waits=w0, sem="ten")
            mm = sch.add(TEN, lambda t, h=h, bt=bt: t.matmul(
                bt.ap[:, (h % 4) * 128:(h % 4) * 128 + 128], identb, Zv_a[:, hs(h)],
                start=False, stop=True), sem="ten")
        b3a_step(2)
        ev1 = sch.add(SCL, lambda s: s.copy(out=zt_a[:, 0:512], in_=PB[3][:]),
                      waits=[mm, m_g2], sem="scl")
        ev2 = sch.add(SCL, lambda s: s.copy(out=zt_a[:, 512:1024], in_=PB[4][:]),
                      sem="scl")
        BT[3].add_use(ev1)
        BT[4].add_use(ev2)
        m_dl = None
        for h in range(H):
            m_dl = sch.add(VEC, vop(lambda v, h=h, ci=ci: v.tensor_scalar(
                delta_a[:, hs(h)], zt_a[:, hs(h)],
                beta_all[:, ci * 8 + h:ci * 8 + h + 1], None, op0=ALU.mult)),
                waits=[ev2 if h == 0 else None, pu.get("delta_a") if h == 0 else None],
                sem="vec")
        for h in range(H):
            bto = BT[1 + h // 2]
            w0 = ([m_dl] + bto.write_waits()) if h % 2 == 0 else []
            sch.add(TEN, lambda t, h=h, bto=bto: t.matmul(
                bto.ap[:, (h % 2) * 128:(h % 2) * 128 + 128], qT_all[:, hs(h)],
                Sst_a[:, hs(h)], start=True, stop=True), waits=w0, sem="ten")
            mm = sch.add(TEN, lambda t, h=h, bto=bto: t.matmul(
                bto.ap[:, 256 + (h % 2) * 128:256 + (h % 2) * 128 + 128], G_a[:, hs(h)],
                delta_a[:, hs(h)], start=True, stop=True), sem="ten")
        b3a_step(2)
        m_ocv = []
        for bi in range(4):
            ev1 = sch.add(SCL, lambda s, bi=bi: s.copy(out=oc_a[:, bi * 256:(bi + 1) * 256],
                                                       in_=PB[1 + bi][:, 0:256]),
                          waits=[mm, pu.get("oc_a") if bi == 0 else None], sem="scl")
            ev2 = sch.add(SCL, lambda s, bi=bi: s.copy(out=oi_a[:, bi * 256:(bi + 1) * 256],
                                                       in_=PB[1 + bi][:, 256:512]),
                          waits=[pu.get("oi_a") if bi == 0 else None], sem="scl")
            BT[1 + bi].add_use(ev2)
            m_ocv.append(ev2)
        m_o1 = None
        for h in range(H):
            m_o1 = sch.add(VEC, vop(lambda v, h=h: v.tensor_scalar(
                oc_a[:, hs(h)], oc_a[:, hs(h)], A_all[:, h:h + 1], None, op0=ALU.mult)),
                waits=[m_ocv[3] if h == 0 else None], sem="vec")
        m_ov = sch.add(VEC, vop(lambda v, ib=ib: v.tensor_tensor(o_out[ib][:], oc_a[:],
                                                                 oi_a[:], op=ALU.add)),
                       waits=[m_o1] + luse["oo"][ib], sem="vec")
        for h in range(H):
            bt = BT[3] if h < 4 else BT[4]
            w0 = ([m_dl, m_kpl] + bt.write_waits()) if h % 4 == 0 else []
            mm = sch.add(TEN, lambda t, h=h, bt=bt: t.matmul(
                bt.ap[:, (h % 4) * 128:(h % 4) * 128 + 128], kp_a[:, hs(h)],
                delta_a[:, hs(h)], start=True, stop=True), waits=w0, sem="ten")
        b3a_step(2)
        ev1 = sch.add(SCL, lambda s: s.copy(out=zt_a[:, 0:512], in_=PB[3][:]),
                      waits=[mm, m_dl], sem="scl")
        ev2 = sch.add(SCL, lambda s: s.copy(out=zt_a[:, 512:1024], in_=PB[4][:]),
                      sem="scl")
        BT[3].add_use(ev1)
        BT[4].add_use(ev2)
        m_sup = None
        for h in range(H):
            m_sup = sch.add(VEC, vop(lambda v, h=h, ci=ci: v.scalar_tensor_tensor(
                out=S32_a[(ci + 1) % 2][:, hs(h)], in0=S32_a[ci % 2][:, hs(h)],
                scalar=E_all[:, h * 128 + 127:h * 128 + 128], in1=zt_a[:, hs(h)],
                op0=ALU.mult, op1=ALU.add)),
                waits=[ev2 if h == 0 else None, m_sinit if ci == 0 and h == 0 else None],
                sem="vec")
        last_S = sch.add(VEC, vop(lambda v, ci=ci: v.tensor_copy(Sst_a[:],
                                                                 S32_a[(ci + 1) % 2][:])),
                         waits=[m_sup], sem="vec")
        pu["l_sb"] = m_lt
        pu["nl"] = m_akl
        pu["A_all"] = m_o1
        pu["lrow8"] = m_lw_
        pu["E_ld"] = [m_sup]
        pu["kT_all"] = mm
        pu["qT_all"] = mm
        pu["KK_all"] = m_n1
        pu["QK_all"] = m_g2
        pu["Ntil_a"] = m_sqev
        pu["NtilT_a"] = mm
        pu["G_a"] = mm
        pu["curU_a"] = mm
        pu["NTp0"] = mm
        pu["Qb0"] = mm
        pu["Qb1"] = mm
        pu["Zv_a"] = mm
        pu["W2T_a"] = mm
        pu["ak_a"] = mm
        pu["kp_a"] = mm
        pu["delta_a"] = mm
        pu["zt_a"] = m_sup
        pu["oc_a"] = m_ov
        pu["oi_a"] = m_ov
        mo_sp = sch.dma("y", lambda y, r=r, ib=ib: y.dma_start(out=t_sc["o_sc"][r, :],
                                                               in_=o_out[ib][:]),
                        waits=[m_ov], sem=spsem)
        luse["oo"][ib] = [mo_sp]
        b3a_step(8)

    due[0] = 10 ** 9
    while next(gen, "END") != "END":
        pass

    sch.plan()

    @block.tensor
    def _(t):
        sch.emit_engine(TEN, t)

    @block.vector
    def _(v):
        sch.emit_engine(VEC, v)

    @block.scalar
    def _(s):
        sch.emit_engine(SCL, s)

    @block.sync
    def _(y):
        sch.emit_engine("y", y)

    @block.gpsimd
    def _(g):
        sch.emit_engine("g", g)

    return [Ev(sems[s], sch.final_counts[s])
            for s in ["spill", "spill2", "ama", "amb", "amc", "amd"]]


def pass_b3bc(nc, block, ctx, t_in, t_sc, sems, entry=()):
    e = ctx.enter_context
    NTG = HALF // 512

    xnT = e(nc.sbuf_tensor("bb_xnT", [128, 8 * HALF], BF16))
    wsg = e(nc.sbuf_tensor("bb_wsg", [128, 8 * NI], BF16))
    wsu = e(nc.sbuf_tensor("bb_wsu", [128, 8 * NI], BF16))
    wsd = e(nc.sbuf_tensor("bb_wsd", [128, 4 * D], BF16))
    amt = [e(nc.sbuf_tensor(f"bb_amt{i}", [128, 4 * 512], BF16)) for i in range(4)]
    wdb = [e(nc.sbuf_tensor(f"bb_wdb{i}", [128, 4 * D], BF16)) for i in range(4)]
    rout = e(nc.sbuf_tensor("bb_rout", [128, D], F32))

    PB = [e(nc.psum_tensor(f"bb_pb{i}", [128, 512], F32)) for i in range(8)]
    BT = [BankTracker(PB[i]) for i in range(8)]

    sch = Sched(sems)
    sch.entry = [w for w in entry if w is not None]
    TEN, VEC, SCL = "ten", "vec", "scl"

    def vop(fn):
        return lambda v: (v.drain(), fn(v))[1]

    m_xnT = sch.dma("g", lambda g: g.dma_start(
        out=xnT[:].rearrange("p (a t) -> p a t", a=8),
        in_=t_sc["xnT_sc"].rearrange("(a p) t -> p a t", p=128)), sem="w")
    m_ws1 = sch.dma("g", lambda g: g.dma_start(
        out=wsg[:].rearrange("p (a j) -> p a j", a=8),
        in_=t_in["Wsg"].rearrange("(a p) j -> p a j", p=128)), sem="w")
    m_ws2 = sch.dma("g", lambda g: g.dma_start(
        out=wsu[:].rearrange("p (a j) -> p a j", a=8),
        in_=t_in["Wsu"].rearrange("(a p) j -> p a j", p=128)), sem="w")
    m_ws3 = sch.dma("g", lambda g: g.dma_start(
        out=wsd[:].rearrange("p (a j) -> p a j", a=4),
        in_=t_in["Wsd"].rearrange("(a p) j -> p a j", p=128)), sem="w")

    luse_amt = [[], [], [], []]
    luse_wdb = [[], [], [], []]
    m_rv = None
    B3SEM = ["dma_a", "dma_b", "ewa", "ewb"]
    for tg in range(NTG):
        for ei in range(NE):
            eb = ei % 4
            wsem2 = B3SEM[eb]
            m_la = sch.dma("y", lambda y, ei=ei, tg=tg, eb=eb: y.dma_start(
                out=amt[eb][:].rearrange("p (b t) -> p b t", b=4),
                in_=t_sc["actm_sc"][ei, :, tg * 512:(tg + 1) * 512]
                .rearrange("(b p) t -> p b t", p=128)),
                waits=luse_amt[eb], sem=wsem2)
            m_lw = sch.dma("y", lambda y, ei=ei, eb=eb: y.dma_start(
                out=wdb[eb][:].rearrange("p (a j) -> p a j", a=4),
                in_=t_sc["wdb_sc"][ei].rearrange("(a p) j -> p a j", p=128)),
                waits=luse_wdb[eb], sem=wsem2)
            luse_amt[eb] = []
            luse_wdb[eb] = []
            mm = None
            for iblk in range(4):
                for tt in range(4):
                    for half in range(2):
                        pb = BT[tt * 2 + half]
                        w0 = []
                        if ei == 0 and iblk == 0:
                            w0 = pb.write_waits() + [m_lw]
                        elif iblk == 0 and tt == 0 and half == 0:
                            w0 = [m_lw]
                        st_ = (ei == 0 and iblk == 0)
                        sp_ = (ei == NE - 1 and iblk == 3)
                        mm = sch.add(TEN, lambda t, iblk=iblk, tt=tt, half=half, eb=eb, pb=pb,
                                     st_=st_, sp_=sp_:
                                     t.matmul(pb.ap[:],
                                              amt[eb][:, iblk * 512 + tt * 128:
                                                      iblk * 512 + (tt + 1) * 128],
                                              wdb[eb][:, iblk * 1024 + half * 512:
                                                      iblk * 1024 + (half + 1) * 512],
                                              start=st_, stop=sp_,
                                              skip_group_check=True),
                                     waits=w0, sem="ten")
            luse_amt[eb].append(mm)
            luse_wdb[eb].append(mm)
        for tt in range(4):
            for half in range(2):
                pb = BT[tt * 2 + half]
                m_rv = sch.add(VEC, vop(lambda v, tt=tt, half=half, pb=pb: v.tensor_copy(
                    rout[:, half * 512:(half + 1) * 512], pb.ap[:])),
                    waits=[mm] + ([m_rv] if m_rv else []), sem="vec")
                pb.add_use(m_rv)
            m_rv = sch.dma("y", lambda y, tg=tg, tt=tt: y.dma_start(
                out=t_sc["routed_sc"][tg * 512 + tt * 128: tg * 512 + (tt + 1) * 128, :],
                in_=rout[:]), waits=[m_rv], sem="spill")

    sgs = e(nc.sbuf_tensor("bb_sgs", [128, 512], BF16))
    ash = e(nc.sbuf_tensor("bb_ash", [128, 512], BF16))
    rl2 = [e(nc.sbuf_tensor(f"bb_r{i}", [128, D], F32)) for i in range(2)]
    sl2 = [e(nc.sbuf_tensor(f"bb_sl{i}", [128, D], F32)) for i in range(2)]
    xl2 = [e(nc.sbuf_tensor(f"bb_xl{i}", [128, D], F32)) for i in range(2)]
    gf = e(nc.sbuf_tensor("bb_gf", [128, D], F32))
    ot = [e(nc.sbuf_tensor(f"bb_ot{i}", [128, D], F32)) for i in range(2)]

    luse3 = {"r": [[], []], "s": [[], []], "x": [[], []], "o": [[], []]}
    prevc = {}
    for ti in range(NHT):
        ib = ti % 2
        ldsem = "dma_a" if ib == 0 else "dma_b"
        spsem = "spill" if ib == 0 else "spill2"
        r = slice(ti * 128, (ti + 1) * 128)
        m_lr = sch.dma("y", lambda y, r=r, ib=ib: y.dma_start(out=rl2[ib][:],
                                                              in_=t_sc["routed_sc"][r, :]),
                       waits=luse3["r"][ib] + ([m_rv] if ti == 0 else []), sem=ldsem)
        m_ls = sch.dma("y", lambda y, r=r, ib=ib: y.dma_start(out=sl2[ib][:],
                                                              in_=t_sc["s_sc"][r, :]),
                       waits=luse3["s"][ib], sem=ldsem)
        m_lx2 = sch.dma("y", lambda y, r=r, ib=ib: y.dma_start(out=xl2[ib][:],
                                                               in_=t_in["x_half"][r, :]),
                        waits=luse3["x"][ib], sem=ldsem)
        for kk in ["r", "s", "x"]:
            luse3[kk][ib] = []
        msg = msu_ = None
        for iblk in range(4):
            pb_g = BT[iblk % 2]
            pb_u = BT[2 + iblk % 2]
            for a in range(8):
                w0 = ([m_ws3, m_lx2] + pb_g.write_waits()) if a == 0 else []
                msg = sch.add(TEN, lambda t, a=a, iblk=iblk, ti=ti, pb_g=pb_g: t.matmul(
                    pb_g.ap[:, 0:128],
                    wsg[:, a * NI + iblk * 128: a * NI + (iblk + 1) * 128],
                    xnT[:, a * HALF + ti * 128: a * HALF + (ti + 1) * 128],
                    start=(a == 0), stop=(a == 7)), waits=w0, sem="ten")
            for a in range(8):
                w0 = (pb_u.write_waits()) if a == 0 else []
                msu_ = sch.add(TEN, lambda t, a=a, iblk=iblk, ti=ti, pb_u=pb_u: t.matmul(
                    pb_u.ap[:, 0:128],
                    wsu[:, a * NI + iblk * 128: a * NI + (iblk + 1) * 128],
                    xnT[:, a * HALF + ti * 128: a * HALF + (ti + 1) * 128],
                    start=(a == 0), stop=(a == 7)), waits=w0, sem="ten")
            m_sg = sch.add(SCL, lambda s, iblk=iblk, pb_g=pb_g: s.activation(
                out=sgs[:, iblk * 128:(iblk + 1) * 128], in_=pb_g.ap[:, 0:128],
                func=AF.Silu), waits=[msg, prevc.get("sgrd")], sem="scl")
            pb_g.add_use(m_sg)
            m_as = sch.add(VEC, vop(lambda v, iblk=iblk, pb_u=pb_u: v.tensor_tensor(
                ash[:, iblk * 128:(iblk + 1) * 128], pb_u.ap[:, 0:128],
                sgs[:, iblk * 128:(iblk + 1) * 128], op=ALU.mult)),
                waits=[msu_, m_sg, prevc.get("ashrd") if iblk == 0 else None], sem="vec")
            pb_u.add_use(m_as)
            prevc["sgrd"] = m_as
        msd = None
        for half in range(2):
            pb = BT[4 + half]
            for iblk in range(4):
                w0 = ([m_as] + pb.write_waits()) if iblk == 0 else []
                msd = sch.add(TEN, lambda t, iblk=iblk, half=half, pb=pb: t.matmul(
                    pb.ap[:], ash[:, iblk * 128:(iblk + 1) * 128],
                    wsd[:, iblk * 1024 + half * 512: iblk * 1024 + (half + 1) * 512],
                    start=(iblk == 0), stop=(iblk == 3)), waits=w0, sem="ten")
            if half == 0:
                msd0 = msd
        prevc["ashrd"] = msd
        m_g1 = sch.add(VEC, vop(lambda v, ib=ib: v.tensor_add(gf[:, 0:512], rl2[ib][:, 0:512],
                                                              PB[4][:])),
                       waits=[msd0, m_lx2, prevc.get("gfrd")], sem="vec")
        BT[4].add_use(m_g1)
        m_g2 = sch.add(VEC, vop(lambda v, ib=ib: v.tensor_add(gf[:, 512:1024],
                                                              rl2[ib][:, 512:1024], PB[5][:])),
                       waits=[msd], sem="vec")
        BT[5].add_use(m_g2)
        luse3["r"][ib].append(m_g2)
        m_gs2 = sch.add(SCL, lambda s: s.activation(out=gf[:], in_=gf[:], func=AF.Silu),
                        waits=[m_g2], sem="scl")
        prevc["gfrd"] = m_gs2
        m_o1 = sch.add(VEC, vop(lambda v, ib=ib: v.tensor_tensor(ot[ib][:], sl2[ib][:],
                                                                 gf[:], op=ALU.mult)),
                       waits=[m_gs2, m_lx2] + luse3["o"][ib], sem="vec")
        m_o2 = sch.add(VEC, vop(lambda v, ib=ib: v.tensor_tensor(ot[ib][:], ot[ib][:],
                                                                 xl2[ib][:], op=ALU.add)),
                       waits=[m_o1, m_lx2], sem="vec")
        luse3["s"][ib].append(m_o2)
        luse3["x"][ib].append(m_o2)
        m_oo = sch.dma("y", lambda y, r=r, ib=ib: y.dma_start(out=t_sc["out"][r, :],
                                                              in_=ot[ib][:]),
                       waits=[m_o2], sem=spsem)
        luse3["o"][ib] = [m_oo]

    sch.plan()

    @block.tensor
    def _(t):
        sch.emit_engine(TEN, t)

    @block.vector
    def _(v):
        sch.emit_engine(VEC, v)

    @block.scalar
    def _(s):
        sch.emit_engine(SCL, s)

    @block.sync
    def _(y):
        sch.emit_engine("y", y)

    @block.gpsimd
    def _(g):
        sch.emit_engine("g", g)

    return [Ev(sems[s], sch.final_counts[s]) for s in ["spill", "spill2"]]


def build_mega_standalone():
    nc = bass.Bass("TRN2", target_bir_lowering=False, debug=False)
    t_in = {
        "alog_t": nc.dram_tensor("alog_t", [1, 512], F32, kind="ExternalInput").ap(),
        "dtb_t": nc.dram_tensor("dtb_t", [1, 512], F32, kind="ExternalInput").ap(),
        "identf": nc.dram_tensor("identf", [128, 128], F32, kind="ExternalInput").ap(),
    }
    for name, shp in A2_CONST_SHAPES.items():
        if name not in ("mstrict8", "mincl8"):
            t_in[name] = nc.dram_tensor(name, shp, F32, kind="ExternalInput").ap()
    t_sc = {
        "q_sc": nc.dram_tensor("q_sc", [S, D], BF16, kind="ExternalInput").ap(),
        "k_sc": nc.dram_tensor("k_sc", [S, D], BF16, kind="ExternalInput").ap(),
        "v_sc": nc.dram_tensor("v_sc", [S, D], BF16, kind="ExternalInput").ap(),
        "balg_sc": nc.dram_tensor("balg_sc", [S, 80], F32, kind="ExternalInput").ap(),
        "xnT_sc": nc.dram_tensor("xnT_sc", [D, HALF], BF16, kind="ExternalInput").ap(),
        "maskH_sc": nc.dram_tensor("maskH_sc", [NE, HALF], F32,
                                   kind="ExternalInput").ap(),
        "wge_sc": nc.dram_tensor("wge_sc", [NE, D, NI], BF16, kind="ExternalInput").ap(),
        "wue_sc": nc.dram_tensor("wue_sc", [NE, D, NI], BF16, kind="ExternalInput").ap(),
        "o_sc": nc.dram_tensor("o_sc", [S, D], BF16, kind="ExternalOutput").ap(),
        "actm_sc": nc.dram_tensor("actm_sc", [NE, NI, HALF], BF16,
                                  kind="ExternalOutput").ap(),
    }
    ctx = ExitStack()
    with nc.Block() as block:
        sems = make_sems(nc, ctx)
        load_ident(nc, block, ctx, t_in, sems)
        load_a2_consts(nc, block, ctx, t_in, sems, skip=("mstrict8", "mincl8"))
        evs = pass_mega2(nc, block, ctx, t_in, t_sc, sems)

        @block.gpsimd
        def _(g):
            wait_all(g, *evs)
    ctx.close()
    return nc


def pass_cast(nc, block, sems, t_in, t_sc):
    s_a, s_b = sems["mka"], sems["mkb"]
    evs = []

    @block.gpsimd
    def _(g):
        for ei in range(NE):
            evs.append(s_a.mark(g.dma_start(out=t_sc["wge_sc"][ei],
                                            in_=t_in["Weg"][ei]), 16))
            evs.append(s_b.mark(g.dma_start(out=t_sc["wue_sc"][ei],
                                            in_=t_in["Weu"][ei]), 16))
            evs.append(s_a.mark(g.dma_start(out=t_sc["wdb_sc"][ei],
                                            in_=t_in["Wed"][ei]), 16))
    return [evs[-3], evs[-2], evs[-1]]


def build_fused2():
    nc = bass.Bass("TRN2", target_bir_lowering=False, debug=False)
    ctx = ExitStack()
    with nc.Block() as block:
        sems = make_sems(nc, ctx)
        t_in_a = {
            "x_full": nc.dram_tensor("x_full", [S, D], F32, kind="ExternalInput").ap(),
            "Wq": nc.dram_tensor("Wq", [D, D], F32, kind="ExternalInput").ap(),
            "Wk": nc.dram_tensor("Wk", [D, D], F32, kind="ExternalInput").ap(),
            "Wv": nc.dram_tensor("Wv", [D, D], F32, kind="ExternalInput").ap(),
            "wbar": nc.dram_tensor("wbar", [D, 80], F32, kind="ExternalInput").ap(),
            "alog_t": nc.dram_tensor("alog_t", [1, 512], F32, kind="ExternalInput").ap(),
            "dtb_t": nc.dram_tensor("dtb_t", [1, 512], F32, kind="ExternalInput").ap(),
            "identf": nc.dram_tensor("identf", [128, 128], F32,
                                     kind="ExternalInput").ap(),
            "hselr": nc.dram_tensor("hselr", [128, 2], F32, kind="ExternalInput").ap(),
        }
        for name, shp in A2_CONST_SHAPES.items():
            t_in_a[name] = nc.dram_tensor(name, shp, F32, kind="ExternalInput").ap()
        t_sc_a = {
            "q_sc": nc.dram_tensor("q_sc", [S, D], BF16, kind="Internal").ap(),
            "k_sc": nc.dram_tensor("k_sc", [S, D], BF16, kind="Internal").ap(),
            "v_sc": nc.dram_tensor("v_sc", [S, D], BF16, kind="Internal").ap(),
            "balg_sc": nc.dram_tensor("balg_sc", [S, 80], F32, kind="Internal").ap(),
            "o_sc": nc.dram_tensor("o_sc", [S, D], BF16, kind="Internal").ap(),
            "maskT_sc": nc.dram_tensor("maskT_sc", [64, S], F32, kind="Internal").ap(),
            "maskH_sc": nc.dram_tensor("maskH_sc", [64, HALF], F32,
                                       kind="Internal").ap(),
        }
        t_sc_b = b_scratch_tensors(nc, io_kind=False, skip_ab=True)
        for k2 in ["o_sc", "maskT_sc", "maskH_sc"]:
            t_sc_b[k2] = t_sc_a[k2]
        load_ident(nc, block, ctx, t_in_a, sems)
        load_a2_consts(nc, block, ctx, t_in_a, sems, skip=("mstrict8", "mincl8"))
        t_in_b = b_input_tensors(nc, skip=("identf",))
        for k2 in ["identf", "onesrow", "onesrow_sb", "identb_sb", "identf_sb"]:
            t_in_b[k2] = t_in_a[k2]
        ctx1 = ExitStack()
        ev1 = pass_a1(nc, block, ctx1, t_in_a, t_sc_a, sems)
        ctx1.close()
        ev_cast = pass_cast(nc, block, sems, t_in_b, t_sc_b)
        ctxr = ExitStack()
        evr = pass_r(nc, block, ctxr, t_in_a, t_sc_a, sems, entry=ev1)
        ctxr.close()
        ctx1a = ExitStack()
        ev1a = pass_b1a(nc, block, ctx1a, t_in_b, t_sc_b, sems, entry=ev1 + evr)
        ctx1a.close()
        t_mg_in = dict(t_in_a)
        t_mg_sc = dict(t_sc_a)
        for k2 in ["xnT_sc", "actm_sc", "wge_sc", "wue_sc"]:
            t_mg_sc[k2] = t_sc_b[k2]
        ctxm = ExitStack()
        evm = pass_mega2(nc, block, ctxm, t_mg_in, t_mg_sc, sems,
                         entry=ev1 + evr + ev1a + ev_cast)
        ctxm.close()
        ctx12 = ExitStack()
        ev12 = pass_b12(nc, block, ctx12, t_in_b, t_sc_b, sems,
                        entry=list(evm) + ev1a)
        ctx12.close()
        ctx3 = ExitStack()
        ev3 = pass_b3bc(nc, block, ctx3, t_in_b, t_sc_b, sems,
                        entry=list(evm) + ev12 + ev_cast)
        ctx3.close()

        @block.gpsimd
        def _(g):
            wait_all(g, *ev3)
    ctx.close()
    return nc


def pass_mega2(nc, block, ctx, t_in, t_sc, sems, entry=()):
    e = ctx.enter_context
    identb = t_in["identb_sb"]
    identf = t_in["identf_sb"]
    cumtri = t_in["cumtri_sb"]
    mstrict = t_in["mstrict_sb"]
    mincl = t_in["mincl_sb"]
    ident4 = t_in["ident4_sb"]

    qld = [e(nc.sbuf_tensor(f"m2_q{i}", [128, D], BF16)) for i in range(2)]
    kld = [e(nc.sbuf_tensor(f"m2_k{i}", [128, D], BF16)) for i in range(2)]
    vld = [e(nc.sbuf_tensor(f"m2_v{i}", [128, D], BF16)) for i in range(2)]
    g_all = e(nc.sbuf_tensor("m2_g_all", [128, NCH * 8 + 128], F32))
    beta_all = e(nc.sbuf_tensor("m2_beta", [128, NCH * 8], F32))
    nbeta_all = e(nc.sbuf_tensor("m2_nbeta", [128, NCH * 8], F32))
    l_sb = e(nc.sbuf_tensor("m2_l", [128, 128], F32))
    nl = e(nc.sbuf_tensor("m2_nl", [128, 8], F32))
    acol = [e(nc.sbuf_tensor(f"m2_ac{i}", [128, 8], F32)) for i in range(2)]
    ecol = [e(nc.sbuf_tensor(f"m2_ec{i}", [128, 8], F32)) for i in range(2)]
    lrow8 = e(nc.sbuf_tensor("m2_lrow8", [64, 128], F32))
    lrow_dram = nc.dram_tensor("m2_lrow_dram", [8, 128], F32, kind="Internal").ap()
    E_all = e(nc.sbuf_tensor("m2_E", [128, 1024], F32))
    kT_all = e(nc.sbuf_tensor("m2_kT", [128, 1024], BF16))
    KK_all = e(nc.sbuf_tensor("m2_KK", [128, 1024], BF16))
    QK_all = e(nc.sbuf_tensor("m2_QK", [128, 1024], BF16))
    Ntil_a = e(nc.sbuf_tensor("m2_N", [128, 1024], BF16))
    NtilT_a = e(nc.sbuf_tensor("m2_NT", [128, 1024], BF16))
    curU_a = e(nc.sbuf_tensor("m2_cu", [128, 1024], BF16))
    NTp_a = [e(nc.sbuf_tensor(f"m2_np{j}", [128, 1024], BF16)) for j in range(5)]
    Qb_a = [e(nc.sbuf_tensor(f"m2_Qb{i}", [128, 1024], BF16)) for i in range(2)]
    ak_a = e(nc.sbuf_tensor("m2_ak", [128, 1024], BF16))
    zt_a = e(nc.sbuf_tensor("m2_zt", [128, 1024], F32))
    qT_b = [e(nc.sbuf_tensor(f"m2_qT{i}", [128, 1024], BF16)) for i in range(2)]
    G_b = [e(nc.sbuf_tensor(f"m2_G{i}", [128, 1024], BF16)) for i in range(2)]
    Zv_b = [e(nc.sbuf_tensor(f"m2_Zv{i}", [128, 1024], BF16)) for i in range(2)]
    W2T_b = [e(nc.sbuf_tensor(f"m2_W2{i}", [128, 1024], BF16)) for i in range(2)]
    kp_b = [e(nc.sbuf_tensor(f"m2_kp{i}", [128, 1024], BF16)) for i in range(2)]
    delta_a = e(nc.sbuf_tensor("m2_dl", [128, 1024], BF16))
    zt2 = e(nc.sbuf_tensor("m2_zt2", [128, 1024], F32))
    oc_a = e(nc.sbuf_tensor("m2_oc", [128, 1024], F32))
    oi_a = e(nc.sbuf_tensor("m2_oi", [128, 1024], F32))
    Sst_a = e(nc.sbuf_tensor("m2_S", [128, 1024], BF16))
    S32_a = [e(nc.sbuf_tensor(f"m2_S32{i}", [128, 1024], F32)) for i in range(2)]
    o_out = [e(nc.sbuf_tensor(f"m2_oo{i}", [128, D], BF16)) for i in range(2)]

    PB = [e(nc.psum_tensor(f"m2_pb{i}", [128, 512], F32)) for i in range(5)]
    BT = [BankTracker(PB[i]) for i in range(5)]
    PB3 = [e(nc.psum_tensor(f"m2_p3{i}", [128, 512], F32)) for i in range(3)]
    BT3 = [BankTracker(PB3[i]) for i in range(3)]

    sch = Sched(sems)
    sch.entry = [w for w in entry if w is not None]
    TEN, VEC, SCL = "ten", "vec", "scl"

    def vop(fn):
        return lambda v: (v.drain(), fn(v))[1]

    def hs(h):
        return slice(h * 128, (h + 1) * 128)

    ctx_p = ExitStack()
    ep = ctx_p.enter_context
    prow = ep(nc.sbuf_tensor("m2_prow", [1, 1024], F32))
    ba_all = ep(nc.sbuf_tensor("m2_ba_all", [128, NCH * 16], F32))
    pbc = ep(nc.sbuf_tensor("m2_pbc", [128, 1024], F32))
    ptmp = [ep(nc.sbuf_tensor(f"m2_pt{i}", [128, NCH * 8], F32)) for i in range(2)]

    m_ld_alog = sch.dma("g", lambda g: g.dma_start(out=prow[0:1, 0:512], in_=t_in["alog_t"]),
                        sem="w")
    m_ld_dtb = sch.dma("g", lambda g: g.dma_start(out=prow[0:1, 512:1024], in_=t_in["dtb_t"]),
                       sem="w")
    m_ld_ba = sch.dma("g", lambda g: g.dma_start(
        out=ba_all[:].rearrange("p (c j) -> p c j", j=16),
        in_=t_sc["balg_sc"][:, 0:16].rearrange("(c p) j -> p c j", p=128)), sem="w")
    m_expal = sch.add(SCL, lambda s: s.activation(out=prow[0:1, 0:512], in_=prow[0:1, 0:512],
                                                  func=AF.Exp),
                      waits=[m_ld_ba], sem="scl")
    m_bc1 = sch.add(TEN, lambda t: t.matmul(PB[0][:, 0:512], t_in["onesrow_sb"],
                                            prow[0:1, 0:512], start=True, stop=True),
                    waits=[m_expal, m_ld_ba] + BT[0].write_waits(), sem="ten")
    m_bc2 = sch.add(TEN, lambda t: t.matmul(PB[1][:, 0:512], t_in["onesrow_sb"],
                                            prow[0:1, 512:1024], start=True, stop=True),
                    waits=[m_ld_ba] + BT[1].write_waits(), sem="ten")
    m_nea = sch.add(VEC, vop(lambda v: v.tensor_scalar(pbc[:, 0:512], PB[0][:, 0:512], -1.0,
                                                       None, op0=ALU.mult)),
                    waits=[m_bc1], sem="vec")
    BT[0].add_use(m_nea)
    m_dtb = sch.add(VEC, vop(lambda v: v.tensor_copy(pbc[:, 512:1024], PB[1][:, 0:512])),
                    waits=[m_bc2], sem="vec")
    BT[1].add_use(m_dtb)
    m_eb = sch.add(SCL, lambda s: s.activation(
        out=ptmp[0][:].rearrange("p (c j) -> p c j", j=8),
        in_=ba_all[:].rearrange("p (c j) -> p c j", j=16)[:, :, 0:8],
        func=AF.Exp, scale=-1.0), waits=[m_ld_ba], sem="scl")
    m_eb1 = sch.add(VEC, vop(lambda v: v.tensor_scalar(ptmp[0][:], ptmp[0][:], 1.0, None,
                                                       op0=ALU.add)),
                    waits=[m_eb], sem="vec")
    m_beta = sch.add(VEC, vop(lambda v: v.reciprocal(beta_all[:], ptmp[0][:])),
                     waits=[m_eb1], sem="vec")
    m_nbeta = sch.add(VEC, vop(lambda v: v.tensor_scalar(nbeta_all[:], beta_all[:], -1.0,
                                                         None, op0=ALU.mult)),
                      waits=[m_beta], sem="vec")
    dtb_rep = pbc[:, 512:512 + NCH * 8].rearrange("p (c j) -> p c j", j=8, c=NCH)
    m_ain = sch.add(VEC, vop(lambda v: v.tensor_tensor(
        ptmp[0][:].rearrange("p (c j) -> p c j", j=8),
        ba_all[:].rearrange("p (c j) -> p c j", j=16)[:, :, 8:16],
        dtb_rep, op=ALU.add)), waits=[m_ld_ba, m_dtb], sem="vec")
    m_absx = sch.add(SCL, lambda s: s.activation(out=ptmp[1][:], in_=ptmp[0][:], func=AF.Abs),
                     waits=[m_ain], sem="scl")
    m_emx = sch.add(SCL, lambda s: s.activation(out=ptmp[1][:], in_=ptmp[1][:], func=AF.Exp,
                                                scale=-1.0),
                    waits=[m_absx], sem="scl")
    m_lp = sch.add(VEC, vop(lambda v: v.tensor_scalar(ptmp[1][:], ptmp[1][:], 1.0, None,
                                                      op0=ALU.add)),
                   waits=[m_emx], sem="vec")
    m_ln = sch.add(SCL, lambda s: s.activation(out=ptmp[1][:], in_=ptmp[1][:], func=AF.Ln),
                   waits=[m_lp], sem="scl")
    m_mx = sch.add(VEC, vop(lambda v: v.tensor_scalar(ptmp[0][:], ptmp[0][:], 0.0, None,
                                                      op0=ALU.max)),
                   waits=[m_ln], sem="vec")
    m_sp = sch.add(VEC, vop(lambda v: v.tensor_tensor(ptmp[0][:], ptmp[0][:], ptmp[1][:],
                                                      op=ALU.add)),
                   waits=[m_mx], sem="vec")
    negea_rep = pbc[:, 0:NCH * 8].rearrange("p (c j) -> p c j", j=8, c=NCH)
    m_gall = sch.add(VEC, vop(lambda v: v.tensor_tensor(
        g_all[:, 0:NCH * 8].rearrange("p (c j) -> p c j", j=8),
        ptmp[0][:].rearrange("p (c j) -> p c j", j=8),
        negea_rep, op=ALU.mult)), waits=[m_sp, m_nea], sem="vec")
    m_sinit = sch.add(VEC, vop(lambda v: v.memset(Sst_a[:], 0.0)), sem="vec")
    m_sinit = sch.add(VEC, vop(lambda v: v.memset(S32_a[0][:], 0.0)), waits=[m_sinit],
                      sem="vec")
    m_ms0 = sch.add(VEC, vop(lambda v: v.memset(PB[0][:, 0:128], 0.0)),
                    waits=BT[0].write_waits(), sem="vec")
    BT[0].add_use(m_ms0)
    m_ms1 = sch.add(VEC, vop(lambda v: v.memset(PB[0][0:64, 128:256], 0.0)), sem="vec")
    BT[0].add_use(m_ms1)
    m_msg = sch.add(VEC, vop(lambda v: v.memset(g_all[:, NCH * 8:NCH * 8 + 128], 0.0)),
                    sem="vec")
    ev_prol = m_msg
    ctx_p.close()

    PH = HALF // 2
    NTGP = PH // 512
    PPU = NTGP * 4
    NUNITS = 2 * NE
    xnsl = e(nc.sbuf_tensor("m2_xnsl", [128, 8 * PH], BF16))
    wge = [e(nc.sbuf_tensor(f"m2_wg{i}", [128, 8 * NI], BF16)) for i in range(2)]
    wue = [e(nc.sbuf_tensor(f"m2_wu{i}", [128, 8 * NI], BF16)) for i in range(2)]
    tnh = [e(nc.sbuf_tensor(f"m2_tnh{i}", [128, 512], F32)) for i in range(3)]
    xgb = [e(nc.sbuf_tensor(f"m2_xg{i}", [128, 512], F32)) for i in range(3)]
    umf = [e(nc.sbuf_tensor(f"m2_um{i}", [128, 512], F32)) for i in range(3)]
    gsb = [e(nc.sbuf_tensor(f"m2_gsb{i}", [128, 512], F32)) for i in range(3)]
    sgm = e(nc.sbuf_tensor("m2_sgm", [128, 512], F32))
    amb4 = [e(nc.sbuf_tensor(f"m2_am{i}", [128, 512], BF16)) for i in range(4)]
    mkb = [e(nc.sbuf_tensor(f"m2_mk{i}", [128, 512], F32)) for i in range(2)]
    AMSEM = ["ama", "amb", "amc", "amd"]
    MKSEM = ["mkb", "mkc"]

    st = {"mw": [None, None], "wfree": [[ev_prol], [ev_prol]],
          "mmk": [None, None], "mkfree": [[ev_prol], [ev_prol]],
          "amfree": [[] for _ in range(4)],
          "tnh": [ev_prol] * 3, "xg": [ev_prol] * 3, "um": [None] * 3,
          "gs": [ev_prol] * 3,
          "mxn": [None, None], "mask_issued": -1,
          "umf_by_stream": {}}

    def issue_weights(u):
        if u >= NUNITS:
            return
        ei, eb = u % NE, u % 2
        wsem = "ewa" if eb == 0 else "ewb"
        sch.dma("y", lambda y, ei=ei, eb=eb: y.dma_start(
            out=wge[eb][:].rearrange("p (a j) -> p a j", a=8),
            in_=t_sc["wge_sc"][ei].rearrange("(a p) j -> p a j", p=128)),
            waits=st["wfree"][eb], sem=wsem)
        st["mw"][eb] = sch.dma("y", lambda y, ei=ei, eb=eb: y.dma_start(
            out=wue[eb][:].rearrange("p (a j) -> p a j", a=8),
            in_=t_sc["wue_sc"][ei].rearrange("(a p) j -> p a j", p=128)), sem=wsem)
        st["wfree"][eb] = []

    def issue_mask(s):
        if s >= NUNITS * NTGP:
            return
        u, tg = s // NTGP, s % NTGP
        p, ei = u // NE, u % NE
        tb = s % 2
        c0 = p * PH + tg * 512
        st["mmk"][tb] = sch.dma("g", lambda g, ei=ei, c0=c0, tb=tb: g.dma_start(
            out=mkb[tb][:], in_=t_sc["maskH_sc"][ei:ei + 1, c0:c0 + 512]
            .broadcast_to([128, 512])), waits=st["mkfree"][tb], sem=MKSEM[tb])
        st["mkfree"][tb] = []
        st["mask_issued"] = s

    def issue_xnsl(p, waits):
        st["mxn"][p] = sch.dma("g", lambda g, p=p: g.dma_start(
            out=xnsl[:].rearrange("p (a t) -> p a t", a=8),
            in_=t_sc["xnT_sc"].rearrange("(a p) t -> p a t", p=128)
            [:, :, p * PH:(p + 1) * PH]), waits=waits, sem="w")

    def b3a_gen():
        q = 0
        issue_xnsl(0, [ev_prol])
        issue_weights(0)
        issue_mask(0)
        issue_mask(1)
        rot = 0
        for u in range(NUNITS):
            p, ei, eb = u // NE, u % NE, u % 2
            issue_weights(u + 1)
            for tg in range(NTGP):
                s = u * NTGP + tg
                tb = s % 2
                if st["mask_issued"] < s + 1:
                    issue_mask(s + 1)
                for iblk in range(4):
                    pr = q % 3
                    ab = q % 4
                    first_pair = (tg == 0 and iblk == 0)
                    bg = BT3[rot % 3]; rot += 1
                    bu = BT3[rot % 3]; rot += 1
                    mmg = None
                    for a in range(8):
                        w0 = []
                        if a == 0:
                            w0 = bg.write_waits()
                            if first_pair:
                                w0 += [st["mw"][eb], st["mxn"][p]]
                        mmg = sch.add(TEN, lambda t, a=a, iblk=iblk, tg=tg, eb=eb, bg=bg:
                                      t.matmul(bg.ap[:],
                                               wge[eb][:, a * NI + iblk * 128:
                                                       a * NI + (iblk + 1) * 128],
                                               xnsl[:, a * PH + tg * 512:
                                                    a * PH + tg * 512 + 512],
                                               start=(a == 0), stop=(a == 7)),
                                      waits=w0, sem="ten")
                    m_tn = sch.add(SCL, lambda sc, pr=pr, bg=bg: sc.activation(
                        out=tnh[pr][:], in_=bg.ap[:], func=AF.Tanh, scale=0.5),
                        waits=[mmg, st["tnh"][pr]], sem="scl")
                    m_xg = sch.add(SCL, lambda sc, pr=pr, bg=bg: sc.copy(
                        out=xgb[pr][:], in_=bg.ap[:]),
                        waits=[st["xg"][pr]], sem="scl")
                    bg.add_use(m_xg)
                    mmu = None
                    for a in range(8):
                        w0 = bu.write_waits() if a == 0 else []
                        mmu = sch.add(TEN, lambda t, a=a, iblk=iblk, tg=tg, eb=eb, bu=bu:
                                      t.matmul(bu.ap[:],
                                               wue[eb][:, a * NI + iblk * 128:
                                                       a * NI + (iblk + 1) * 128],
                                               xnsl[:, a * PH + tg * 512:
                                                    a * PH + tg * 512 + 512],
                                               start=(a == 0), stop=(a == 7)),
                                      waits=w0, sem="ten")
                    m_um = sch.add(VEC, vop(lambda v, pr=pr, tb=tb, bu=bu: v.tensor_tensor(
                        umf[pr][:], bu.ap[:], mkb[tb][:], op=ALU.mult)),
                        waits=[mmu, st["mmk"][tb], st["um"][pr]], sem="vec")
                    bu.add_use(m_um)
                    st["umf_by_stream"].setdefault(s, []).append(m_um)

                    def pool_ops(g, pr=pr, ab=ab):
                        g.drain()
                        g.tensor_scalar(sgm[:], tnh[pr][:], 0.5, 0.5, op0=ALU.mult,
                                        op1=ALU.add)
                        g.drain()
                        g.tensor_tensor(gsb[pr][:], sgm[:], xgb[pr][:], op=ALU.mult)
                        g.drain()
                        return g.tensor_tensor(amb4[ab][:], umf[pr][:], gsb[pr][:],
                                               op=ALU.mult)
                    m_am = sch.add("g", pool_ops,
                                   waits=[m_xg, m_um] + st["amfree"][ab], sem="pool")
                    st["amfree"][ab] = []
                    st["tnh"][pr] = m_am
                    st["xg"][pr] = m_am
                    st["gs"][pr] = m_am
                    st["um"][pr] = m_am
                    c0 = p * PH + tg * 512
                    m_spam = sch.dma("y", lambda y, ei=ei, iblk=iblk, c0=c0, ab=ab:
                                     y.dma_start(out=t_sc["actm_sc"][
                                         ei, iblk * 128:(iblk + 1) * 128, c0:c0 + 512],
                                         in_=amb4[ab][:]),
                                     waits=[m_am], sem=AMSEM[ab])
                    st["amfree"][ab] = [m_spam]
                    q += 1
                    yield
                st["mkfree"][tb] = [st["umf_by_stream"][s][-1]]
                del st["umf_by_stream"][s]
            st["wfree"][eb] = [m_xg, m_um]
            if u == NE - 1:
                issue_xnsl(1, [m_xg, m_um])

    gen = b3a_gen()
    due = [0]

    def b3a_step(k):
        for _ in range(k):
            if due[0] <= 0:
                return
            if next(gen, "END") == "END":
                return
            due[0] -= 1

    luse = {"q": [[], []], "k": [[], []], "v": [[], []], "oo": [[], []]}
    pu = {}
    spu = {}
    sst = {"last_S": m_sinit, "lbc_rd": None}

    def t_phase(ci):
        pc = ci % 2
        ib = ci % 2
        ldsem = "dma_a" if ib == 0 else "dma_b"
        r = slice(ci * 128, (ci + 1) * 128)
        tm = {}
        m_lq = sch.dma("y", lambda y, r=r, ib=ib: y.dma_start(out=qld[ib][:],
                                                              in_=t_sc["q_sc"][r, :]),
                       waits=luse["q"][ib], sem=ldsem)
        m_lk = sch.dma("y", lambda y, r=r, ib=ib: y.dma_start(out=kld[ib][:],
                                                              in_=t_sc["k_sc"][r, :]),
                       waits=luse["k"][ib], sem=ldsem)
        m_lv = sch.dma("y", lambda y, r=r, ib=ib: y.dma_start(out=vld[ib][:],
                                                              in_=t_sc["v_sc"][r, :]),
                       waits=luse["v"][ib], sem=ldsem)
        for kk_ in ["q", "k", "v"]:
            luse[kk_][ib] = []
        m_lmm = sch.add(TEN, lambda t, ci=ci: t.matmul(PB[0][:, 0:128], cumtri,
                                                       g_all[:, ci * 8:ci * 8 + 128],
                                                       start=True, stop=True),
                        waits=[m_gall, m_msg] + BT[0].write_waits(), sem="ten")
        m_lsb = sch.add(VEC, vop(lambda v: v.tensor_copy(l_sb[:], PB[0][:, 0:128])),
                        waits=[m_lmm, pu.get("l_sb")], sem="vec")
        BT[0].add_use(m_lsb)
        m_nl = sch.add(VEC, vop(lambda v: v.tensor_scalar(nl[:], l_sb[:, 0:8], -1.0, None,
                                                          op0=ALU.mult)),
                       waits=[m_lsb, pu.get("nl")], sem="vec")
        m_A = sch.add(SCL, lambda s, pc=pc: s.activation(out=acol[pc][:], in_=l_sb[:, 0:8],
                                                         func=AF.Exp),
                      waits=[m_lsb, spu.get(("acol", pc))], sem="scl")
        m_lt = sch.add(TEN, lambda t: t.transpose(PB[0][0:64, 128:256], l_sb[:, 0:64],
                                                  identf),
                       waits=[m_lsb] + BT[0].write_waits(), sem="ten")
        m_lr8 = sch.add(SCL, lambda s: s.copy(out=lrow8[:], in_=PB[0][0:64, 128:256]),
                        waits=[m_lt, pu.get("lrow8")], sem="scl")
        BT[0].add_use(m_lr8)
        m_lw_ = sch.dma("g", lambda g: g.dma_start(out=lrow_dram, in_=lrow8[0:8, :]),
                        waits=[m_lr8, sst["lbc_rd"]], sem="mka")
        m_lbc = sch.dma("g", lambda g: g.dma_start(
            out=E_all[:], in_=lrow_dram.rearrange("h j -> (h j)")[None, :]
            .broadcast_to([128, 1024])), waits=[m_lw_] + pu.get("E_ld", []), sem="mka")
        sst["lbc_rd"] = m_lbc
        m_ecl = None
        for h in range(H):
            m_ecl = sch.add(VEC, vop(lambda v, h=h: v.tensor_scalar(
                E_all[:, hs(h)], E_all[:, hs(h)], nl[:, h:h + 1], 0.0,
                op0=ALU.add, op1=ALU.min)),
                waits=[m_lbc, m_nl] if h == 0 else [], sem="vec")
        m_E = sch.add(SCL, lambda s: s.activation(out=E_all[:], in_=E_all[:], func=AF.Exp),
                      waits=[m_ecl], sem="scl")
        m_ecol = sch.add(VEC, vop(lambda v, pc=pc: v.tensor_copy(
            ecol[pc][:], E_all[:].rearrange("p (h c) -> p h c", c=128)[:, :, 127])),
            waits=[m_E, spu.get(("ecol", pc))], sem="vec")
        yield
        mm = None
        for h in range(H):
            bt = BT[1 + h // 2]
            w0 = ([m_lv] + bt.write_waits() + [pu.get("kT_all") if h == 0 else None,
                                              spu.get(("qT", pc)) if h == 0 else None]) \
                if h % 2 == 0 else []
            sch.add(TEN, lambda t, h=h, ib=ib, bt=bt: t.matmul(
                bt.ap[:, (h % 2) * 128:(h % 2) * 128 + 128], kld[ib][:, hs(h)], identb,
                start=True, stop=True), waits=w0, sem="ten")
            mm = sch.add(TEN, lambda t, h=h, ib=ib, bt=bt: t.matmul(
                bt.ap[:, 256 + (h % 2) * 128:256 + (h % 2) * 128 + 128],
                qld[ib][:, hs(h)], identb, start=True, stop=True), sem="ten")
        luse["q"][ib].append(mm)
        m_kqev = []
        for bi in range(4):
            ev1 = sch.add(SCL, lambda s, bi=bi: s.copy(out=kT_all[:, bi * 256:(bi + 1) * 256],
                                                       in_=PB[1 + bi][:, 0:256]),
                          waits=[mm], sem="scl")
            ev2 = sch.add(SCL, lambda s, bi=bi, pc=pc: s.copy(
                out=qT_b[pc][:, bi * 256:(bi + 1) * 256], in_=PB[1 + bi][:, 256:512]),
                sem="scl")
            BT[1 + bi].add_use(ev2)
            m_kqev.append(ev2)
        yield
        for h in range(H):
            bt = BT[1 + h // 2]
            w0 = ([m_kqev[h // 2]] + bt.write_waits() +
                  [pu.get("KK_all") if h == 0 else None,
                   pu.get("QK_all") if h == 0 else None]) \
                if h % 2 == 0 else []
            sch.add(TEN, lambda t, h=h, bt=bt, pc=pc: t.matmul(
                bt.ap[:, (h % 2) * 128:(h % 2) * 128 + 128], kT_all[:, hs(h)],
                kT_all[:, hs(h)], start=True, stop=True), waits=w0, sem="ten")
            mm = sch.add(TEN, lambda t, h=h, bt=bt, pc=pc: t.matmul(
                bt.ap[:, 256 + (h % 2) * 128:256 + (h % 2) * 128 + 128],
                kT_all[:, hs(h)], qT_b[pc][:, hs(h)], start=True, stop=True), sem="ten")
        m_kkev = []
        for bi in range(4):
            ev1 = sch.add(SCL, lambda s, bi=bi: s.copy(out=KK_all[:, bi * 256:(bi + 1) * 256],
                                                       in_=PB[1 + bi][:, 0:256]),
                          waits=[mm], sem="scl")
            ev2 = sch.add(SCL, lambda s, bi=bi: s.copy(out=QK_all[:, bi * 256:(bi + 1) * 256],
                                                       in_=PB[1 + bi][:, 256:512]),
                          sem="scl")
            BT[1 + bi].add_use(ev2)
            m_kkev.append(ev2)
        yield
        m_n1 = None
        for h in range(H):
            m_n1 = sch.add(VEC, vop(lambda v, h=h, ci=ci: v.scalar_tensor_tensor(
                out=zt_a[:, hs(h)], in0=KK_all[:, hs(h)],
                scalar=nbeta_all[:, ci * 8 + h:ci * 8 + h + 1], in1=E_all[:, hs(h)],
                op0=ALU.mult, op1=ALU.mult)),
                waits=([m_E, m_nbeta, pu.get("zt_a")] if h == 0 else []) +
                ([m_kkev[h // 2]] if h % 2 == 0 else []),
                sem="vec")
        m_n2 = None
        for h in range(H):
            m_n2 = sch.add(VEC, vop(lambda v, h=h: v.tensor_tensor(
                Ntil_a[:, hs(h)], zt_a[:, hs(h)], mstrict, op=ALU.mult)),
                waits=[m_n1, pu.get("Ntil_a")] if h == 0 else [], sem="vec")
        m_g1 = sch.add(VEC, vop(lambda v: v.tensor_tensor(zt_a[:], E_all[:], QK_all[:],
                                                          op=ALU.mult)),
                       waits=[m_n2], sem="vec")
        m_g2 = None
        for h in range(H):
            m_g2 = sch.add(VEC, vop(lambda v, h=h, pc=pc: v.tensor_tensor(
                G_b[pc][:, hs(h)], zt_a[:, hs(h)], mincl, op=ALU.mult)),
                waits=[m_g1, spu.get(("G", pc))] if h == 0 else [], sem="vec")
        pu["zt_a"] = m_g2
        for h in range(H):
            bt = BT[3] if h < 4 else BT[4]
            w0 = ([m_n2] + bt.write_waits() + [pu.get("NtilT_a") if h == 0 else None]) \
                if h % 4 == 0 else []
            mm = sch.add(TEN, lambda t, h=h, bt=bt: t.matmul(
                bt.ap[:, (h % 4) * 128:(h % 4) * 128 + 128], Ntil_a[:, hs(h)], identb,
                start=True, stop=True), waits=w0, sem="ten")
        m_ntev = []
        for bi in range(2):
            evv = sch.add(SCL, lambda s, bi=bi: s.copy(
                out=NtilT_a[:, bi * 512:(bi + 1) * 512], in_=PB[3 + bi][:]),
                waits=[mm], sem="scl")
            BT[3 + bi].add_use(evv)
            m_ntev.append(evv)
        yield
        m_sqev = None
        sqdep = [m_ntev[0], m_ntev[1]]
        for j in range(5):
            aU = Ntil_a if j == 0 else curU_a
            aT = NtilT_a if j == 0 else NTp_a[j - 1]
            for h in range(H):
                btu = BT[1] if h < 4 else BT[2]
                btt = BT[3] if h < 4 else BT[4]
                w0 = []
                if h == 0:
                    w0 = [sqdep[0]] + btu.write_waits() + btt.write_waits()
                    if j == 0:
                        w0 += [pu.get("curU_a"), pu.get("NTp0"), m_n2]
                elif h == 4:
                    w0 = [sqdep[1]] + btu.write_waits() + btt.write_waits()
                sch.add(TEN, lambda t, h=h, aU=aU, aT=aT, btu=btu: t.matmul(
                    btu.ap[:, (h % 4) * 128:(h % 4) * 128 + 128], aT[:, hs(h)], aU[:, hs(h)],
                    start=True, stop=True), waits=w0, sem="ten")
                mm = sch.add(TEN, lambda t, h=h, aU=aU, aT=aT, btt=btt: t.matmul(
                    btt.ap[:, (h % 4) * 128:(h % 4) * 128 + 128], aU[:, hs(h)], aT[:, hs(h)],
                    start=True, stop=True), sem="ten")
            ev1 = sch.add(SCL, lambda s: s.copy(out=curU_a[:, 0:512], in_=PB[1][:]),
                          waits=[mm], sem="scl")
            ev2 = sch.add(SCL, lambda s: s.copy(out=curU_a[:, 512:1024], in_=PB[2][:]),
                          sem="scl")
            ev3 = sch.add(SCL, lambda s, j=j: s.copy(out=NTp_a[j][:, 0:512], in_=PB[3][:]),
                          sem="scl")
            ev4 = sch.add(SCL, lambda s, j=j: s.copy(out=NTp_a[j][:, 512:1024], in_=PB[4][:]),
                          sem="scl")
            BT[1].add_use(ev1)
            BT[2].add_use(ev2)
            BT[3].add_use(ev3)
            BT[4].add_use(ev4)
            m_sqev = ev4
            sqdep = [ev3, ev4]
            yield
        yield
        mmlo64 = None
        for h in range(H):
            bt = BT[1] if h < 4 else BT[2]
            w0 = ([sqdep[0] if h == 0 else sqdep[1]] + bt.write_waits()) \
                if h % 4 == 0 else []
            mm = sch.add(TEN, lambda t, h=h, bt=bt: t.matmul(
                bt.ap[:, (h % 4) * 128:(h % 4) * 128 + 128], NTp_a[4][:, hs(h)],
                curU_a[:, hs(h)], start=True, stop=True), waits=w0, sem="ten")
            if h == 3:
                mmlo64 = mm
        mq_lo = sch.add(VEC, vop(lambda v: v.tensor_tensor(Qb_a[0][:, 0:512], PB[1][:],
                                                           ident4, op=ALU.add)),
                        waits=[mmlo64, pu.get("Qb0")], sem="vec")
        m_q = sch.add(VEC, vop(lambda v: v.tensor_tensor(Qb_a[0][:, 512:1024], PB[2][:],
                                                         ident4, op=ALU.add)),
                      waits=[mm, mq_lo], sem="vec")
        BT[1].add_use(m_q)
        BT[2].add_use(m_q)
        for lev in range(6):
            src_ = NTp_a[4 - lev] if lev < 5 else NtilT_a
            qin = Qb_a[lev % 2]
            qout = Qb_a[(lev + 1) % 2]
            mmlo = None
            for h in range(H):
                bt = BT[1] if h < 4 else BT[2]
                w0 = []
                if h == 0:
                    w0 = [mq_lo] + bt.write_waits()
                    if lev == 0:
                        w0.append(pu.get("Qb1"))
                elif h == 4:
                    w0 = [m_q] + bt.write_waits()
                mm = sch.add(TEN, lambda t, h=h, src_=src_, qin=qin, bt=bt: t.matmul(
                    bt.ap[:, (h % 4) * 128:(h % 4) * 128 + 128], src_[:, hs(h)],
                    qin[:, hs(h)], start=True, stop=True), waits=w0, sem="ten")
                if h == 3:
                    mmlo = mm
            mq1 = sch.add(VEC, vop(lambda v, qin=qin, qout=qout: v.tensor_tensor(
                qout[:, 0:512], PB[1][:], qin[:, 0:512], op=ALU.add)),
                waits=[mmlo], sem="vec")
            m_q = sch.add(VEC, vop(lambda v, qin=qin, qout=qout: v.tensor_tensor(
                qout[:, 512:1024], PB[2][:], qin[:, 512:1024], op=ALU.add)),
                waits=[mm, mq1], sem="vec")
            mq_lo = mq1
            BT[1].add_use(m_q)
            BT[2].add_use(m_q)
            if lev in (1, 3):
                yield
        m_T = m_q
        yield
        m_negA8 = sch.add(VEC, vop(lambda v, pc=pc: v.tensor_scalar(
            nl[:], acol[pc][:], -1.0, None, op0=ALU.mult)),
            waits=[m_A, m_ecl], sem="vec")
        m_akl = None
        for h in range(H):
            m_akl = sch.add(VEC, vop(lambda v, h=h, ib=ib: v.tensor_scalar(
                ak_a[:, hs(h)], kld[ib][:, hs(h)], nl[:, h:h + 1], None, op0=ALU.mult)),
                waits=[m_negA8] + ([pu.get("ak_a"), m_lv] if h == 0 else []), sem="vec")
        m_kpl = None
        for h in range(H):
            m_kpl = sch.add(VEC, vop(lambda v, h=h, ib=ib, pc=pc: v.tensor_scalar(
                kp_b[pc][:, hs(h)], kld[ib][:, hs(h)],
                E_all[:, h * 128 + 127:h * 128 + 128], None, op0=ALU.mult)),
                waits=[m_akl] + ([spu.get(("kp", pc))] if h == 0 else []), sem="vec")
        luse["k"][ib].append(m_kpl)
        pu["E_ld"] = [m_kpl]
        for h in range(H):
            btz = BT[1 + h // 2]
            w0 = ([m_T, m_lv, m_akl] + btz.write_waits()) if h % 2 == 0 else []
            sch.add(TEN, lambda t, h=h, ib=ib, btz=btz: t.matmul(
                btz.ap[:, (h % 2) * 128:(h % 2) * 128 + 128], Qb_a[0][:, hs(h)],
                vld[ib][:, hs(h)], start=True, stop=True), waits=w0, sem="ten")
            mm = sch.add(TEN, lambda t, h=h, btz=btz: t.matmul(
                btz.ap[:, 256 + (h % 2) * 128:256 + (h % 2) * 128 + 128], ak_a[:, hs(h)],
                Qb_a[0][:, hs(h)], start=True, stop=True), sem="ten")
        luse["v"][ib].append(mm)
        m_zwev = []
        for bi in range(4):
            ev1 = sch.add(SCL, lambda s, bi=bi, pc=pc: s.copy(
                out=Zv_b[pc][:, bi * 256:(bi + 1) * 256], in_=PB[1 + bi][:, 0:256]),
                waits=[mm, spu.get(("Zv", pc)) if bi == 0 else None], sem="scl")
            ev2 = sch.add(SCL, lambda s, bi=bi, pc=pc: s.copy(
                out=W2T_b[pc][:, bi * 256:(bi + 1) * 256], in_=PB[1 + bi][:, 256:512]),
                waits=[spu.get(("W2T", pc)) if bi == 0 else None], sem="scl")
            BT[1 + bi].add_use(ev2)
            m_zwev.append(ev2)
        pu["l_sb"] = m_lt
        pu["nl"] = m_akl
        pu["lrow8"] = m_lw_
        pu["kT_all"] = mm
        pu["KK_all"] = m_n1
        pu["QK_all"] = m_g2
        pu["Ntil_a"] = m_sqev
        pu["NtilT_a"] = mm
        pu["curU_a"] = mm
        pu["NTp0"] = mm
        pu["Qb0"] = mm
        pu["Qb1"] = mm
        pu["ak_a"] = mm
        tm["zw"] = m_zwev[3]
        tm["ecol"] = m_ecol
        tm["A"] = m_A
        st[("tm", ci)] = tm

    def s_phase(ci):
        pc = ci % 2
        ib = ci % 2
        spsem = "spill" if ib == 0 else "spill2"
        r = slice(ci * 128, (ci + 1) * 128)
        tm = st.pop(("tm", ci))
        for h in range(H):
            bt = BT[3] if h < 4 else BT[4]
            w0 = ([tm["zw"], sst["last_S"]] + bt.write_waits()) if h % 4 == 0 else []
            sch.add(TEN, lambda t, h=h, bt=bt, pc=pc: t.matmul(
                bt.ap[:, (h % 4) * 128:(h % 4) * 128 + 128], W2T_b[pc][:, hs(h)],
                Sst_a[:, hs(h)], start=True, stop=False), waits=w0, sem="ten")
            mm = sch.add(TEN, lambda t, h=h, bt=bt, pc=pc: t.matmul(
                bt.ap[:, (h % 4) * 128:(h % 4) * 128 + 128], identb, Zv_b[pc][:, hs(h)],
                start=False, stop=True), sem="ten")
        spu[("Zv", pc)] = mm
        spu[("W2T", pc)] = mm
        ev1 = sch.add(SCL, lambda s: s.copy(out=zt2[:, 0:512], in_=PB[3][:]),
                      waits=[mm, spu.get("zt2")], sem="scl")
        ev2 = sch.add(SCL, lambda s: s.copy(out=zt2[:, 512:1024], in_=PB[4][:]),
                      sem="scl")
        BT[3].add_use(ev1)
        BT[4].add_use(ev2)
        m_dl = None
        for h in range(H):
            m_dl = sch.add(VEC, vop(lambda v, h=h, ci=ci: v.tensor_scalar(
                delta_a[:, hs(h)], zt2[:, hs(h)],
                beta_all[:, ci * 8 + h:ci * 8 + h + 1], None, op0=ALU.mult)),
                waits=[ev2 if h == 0 else None, spu.get("delta") if h == 0 else None],
                sem="vec")
        yield
        for h in range(H):
            bto = BT[1 + h // 2]
            w0 = ([m_dl] + bto.write_waits()) if h % 2 == 0 else []
            sch.add(TEN, lambda t, h=h, bto=bto, pc=pc: t.matmul(
                bto.ap[:, (h % 2) * 128:(h % 2) * 128 + 128], qT_b[pc][:, hs(h)],
                Sst_a[:, hs(h)], start=True, stop=True), waits=w0, sem="ten")
            mm = sch.add(TEN, lambda t, h=h, bto=bto, pc=pc: t.matmul(
                bto.ap[:, 256 + (h % 2) * 128:256 + (h % 2) * 128 + 128], G_b[pc][:, hs(h)],
                delta_a[:, hs(h)], start=True, stop=True), sem="ten")
        spu[("qT", pc)] = mm
        spu[("G", pc)] = mm
        m_ocv = []
        for bi in range(4):
            ev1o = sch.add(SCL, lambda s, bi=bi: s.copy(out=oc_a[:, bi * 256:(bi + 1) * 256],
                                                        in_=PB[1 + bi][:, 0:256]),
                           waits=[mm, spu.get("oc") if bi == 0 else None], sem="scl")
            ev2o = sch.add(SCL, lambda s, bi=bi: s.copy(out=oi_a[:, bi * 256:(bi + 1) * 256],
                                                        in_=PB[1 + bi][:, 256:512]),
                           waits=[spu.get("oi") if bi == 0 else None], sem="scl")
            BT[1 + bi].add_use(ev2o)
            m_ocv.append(ev2o)
        m_o1 = None
        for h in range(H):
            m_o1 = sch.add(VEC, vop(lambda v, h=h, pc=pc: v.tensor_scalar(
                oc_a[:, hs(h)], oc_a[:, hs(h)], acol[pc][:, h:h + 1], None, op0=ALU.mult)),
                waits=[m_ocv[3], tm["A"]] if h == 0 else [], sem="vec")
        spu[("acol", pc)] = m_o1
        m_ov = sch.add(VEC, vop(lambda v, ib=ib: v.tensor_tensor(o_out[ib][:], oc_a[:],
                                                                 oi_a[:], op=ALU.add)),
                       waits=[m_o1] + luse["oo"][ib], sem="vec")
        spu["oc"] = m_ov
        spu["oi"] = m_ov
        mo_sp = sch.dma("y", lambda y, r=r, ib=ib: y.dma_start(out=t_sc["o_sc"][r, :],
                                                               in_=o_out[ib][:]),
                        waits=[m_ov], sem=spsem)
        luse["oo"][ib] = [mo_sp]
        yield
        for h in range(H):
            bt = BT[3] if h < 4 else BT[4]
            w0 = ([m_dl] + bt.write_waits()) if h % 4 == 0 else []
            mm = sch.add(TEN, lambda t, h=h, bt=bt, pc=pc: t.matmul(
                bt.ap[:, (h % 4) * 128:(h % 4) * 128 + 128], kp_b[pc][:, hs(h)],
                delta_a[:, hs(h)], start=True, stop=True), waits=w0, sem="ten")
        spu[("kp", pc)] = mm
        spu["delta"] = mm
        ev1 = sch.add(SCL, lambda s: s.copy(out=zt2[:, 0:512], in_=PB[3][:]),
                      waits=[mm, m_dl], sem="scl")
        ev2 = sch.add(SCL, lambda s: s.copy(out=zt2[:, 512:1024], in_=PB[4][:]),
                      sem="scl")
        BT[3].add_use(ev1)
        BT[4].add_use(ev2)
        m_sup = None
        for h in range(H):
            m_sup = sch.add(VEC, vop(lambda v, h=h, ci=ci, pc=pc: v.scalar_tensor_tensor(
                out=S32_a[(ci + 1) % 2][:, hs(h)], in0=S32_a[ci % 2][:, hs(h)],
                scalar=ecol[pc][:, h:h + 1], in1=zt2[:, hs(h)],
                op0=ALU.mult, op1=ALU.add)),
                waits=[ev2, tm["ecol"], m_sinit if ci == 0 else None] if h == 0 else [],
                sem="vec")
        spu[("ecol", pc)] = m_sup
        spu["zt2"] = m_sup
        sst["last_S"] = sch.add(VEC, vop(lambda v, ci=ci: v.tensor_copy(
            Sst_a[:], S32_a[(ci + 1) % 2][:])), waits=[m_sup], sem="vec")

    QUOTA = [4, 4, 3, 2, 2, 2, 3, 2, 2, 2, 3, 2, 3]
    due[0] += 2 * (PPU * NUNITS // NCH)
    for qi, _ in enumerate(t_phase(0)):
        b3a_step(QUOTA[qi % len(QUOTA)])
    sgen = None
    for ci in range(NCH):
        sgen = s_phase(ci)
        if ci + 1 < NCH:
            due[0] += PPU * NUNITS // NCH
            for qi, _ in enumerate(t_phase(ci + 1)):
                if sgen is not None and next(sgen, "SEND") == "SEND":
                    sgen = None
                b3a_step(QUOTA[qi % len(QUOTA)])
            while sgen is not None and next(sgen, "SEND") != "SEND":
                b3a_step(1)
            sgen = None
            b3a_step(2)
        else:
            while next(sgen, "SEND") != "SEND":
                b3a_step(1)
            b3a_step(4)

    due[0] = 10 ** 9
    while next(gen, "END") != "END":
        pass

    sch.plan()

    @block.tensor
    def _(t):
        sch.emit_engine(TEN, t)

    @block.vector
    def _(v):
        sch.emit_engine(VEC, v)

    @block.scalar
    def _(s):
        sch.emit_engine(SCL, s)

    @block.sync
    def _(y):
        sch.emit_engine("y", y)

    @block.gpsimd
    def _(g):
        sch.emit_engine("g", g)

    return [Ev(sems[s], sch.final_counts[s])
            for s in ["spill", "spill2", "ama", "amb", "amc", "amd"]]


def pass_b12(nc, block, ctx, t_in, t_sc, sems, entry=()):
    e = ctx.enter_context
    identb = t_in["identb_sb"]
    wo = e(nc.sbuf_tensor("b12_wo", [128, 8 * D], BF16))
    onw = e(nc.sbuf_tensor("b12_onw", [128, 128], F32))
    hsel = e(nc.sbuf_tensor("b12_hsel", [128, 2], F32))
    ut = e(nc.sbuf_tensor("b12_ut", [128, 128], F32))
    olo = [e(nc.sbuf_tensor(f"b12_olo{i}", [128, D], BF16)) for i in range(2)]
    ohi = [e(nc.sbuf_tensor(f"b12_ohi{i}", [128, D], BF16)) for i in range(2)]
    xnl = [e(nc.sbuf_tensor(f"b12_xn{i}", [128, D], F32)) for i in range(2)]
    gsl = [e(nc.sbuf_tensor(f"b12_gs{i}", [128, D], BF16)) for i in range(2)]
    osel = [e(nc.sbuf_tensor(f"b12_os{i}", [128, D], F32)) for i in range(2)]
    ogT = [e(nc.sbuf_tensor(f"b12_ogT{i}", [128, D], BF16)) for i in range(2)]
    mb_ = [e(nc.sbuf_tensor(f"b12_mb{i}", [128, D], BF16)) for i in range(2)]
    sq = [e(nc.sbuf_tensor(f"b12_sq{i}", [128, D], F32)) for i in range(2)]
    red = e(nc.sbuf_tensor("b12_red", [128, 18], F32))
    rs = e(nc.sbuf_tensor("b12_rs", [128, 9], F32))
    ntf = e(nc.sbuf_tensor("b12_ntf", [128, 9], F32))
    nti = e(nc.sbuf_tensor("b12_nti", [128, 9], I32))
    wfc = e(nc.sbuf_tensor("b12_wfc", [128, 8 * DFF], BF16))
    wpj = e(nc.sbuf_tensor("b12_wpj", [128, 32 * D], BF16))
    mTt = e(nc.sbuf_tensor("b12_mTt", [128, D], BF16))
    rlt = e(nc.sbuf_tensor("b12_rl", [128, 512], F32))
    actT = e(nc.sbuf_tensor("b12_act", [128, DFF], BF16))

    PBx = [e(nc.psum_tensor(f"b12_pb{i}", [128, 512], F32)) for i in range(8)]
    BTx = [BankTracker(p) for p in PBx]
    bTR = [BTx[0], BTx[1]]
    bWO = [BTx[2], BTx[3]]
    bON = BTx[4]
    bFC = [BTx[5], BTx[6]]
    bPJ = BTx[7]

    sch = Sched(sems)
    sch.entry = [w for w in entry if w is not None]

    def vop(fn):
        return lambda v: (v.drain(), fn(v))[1]

    m_wo = sch.dma("g", lambda g: g.dma_start(out=wo[:].rearrange("p (a j) -> p a j", a=8),
                                              in_=t_in["Wo"].rearrange("(a p) j -> p a j", p=128)),
                   sem="w")
    m_hs = sch.dma("g", lambda g: g.dma_start(out=hsel[:], in_=t_in["hsel"]), sem="w")
    sch.dma("g", lambda g: g.dma_start(out=wfc[:].rearrange("p (a j) -> p a j", a=8),
                                       in_=t_in["W_fc"].rearrange("(a p) j -> p a j", p=128)),
            sem="w")
    m_wfp = sch.dma("g", lambda g: g.dma_start(
        out=wpj[:].rearrange("p (a j) -> p a j", a=32),
        in_=t_in["W_proj"].rearrange("(a p) j -> p a j", p=128)), sem="w")
    m_onr = sch.dma("g", lambda g: g.dma_start(out=ut[0:1, 0:128], in_=t_in["o_norm_w"]),
                    sem="w")
    m_onb = sch.add("ten", lambda t: t.matmul(bON.ap[:, 0:128], t_in["onesrow_sb"],
                                              ut[0:1, 0:128], start=True, stop=True),
                    waits=[m_onr] + bON.write_waits(), sem="ten")
    m_onw = sch.add("vec", vop(lambda v: v.tensor_copy(onw[:], bON.ap[:, 0:128])),
                    waits=[m_onb], sem="vec")
    bON.add_use(m_onw)

    luse = {"olo": [[], []], "ohi": [[], []], "xn": [[], []], "gs": [[], []],
            "sp": [[], []]}
    prev = {}
    sp_marks = {}

    def b1b_tile(ti):
        ib = ti % 2
        ldsem = "dma_a" if ib == 0 else "dma_b"
        spsem = "spill" if ib == 0 else "spill2"
        r = slice(ti * 128, (ti + 1) * 128)
        sch.dma("y", lambda y, r=r, ib=ib: y.dma_start(out=olo[ib][:], in_=t_sc["o_sc"][r, :]),
                waits=luse["olo"][ib], sem=ldsem)
        sch.dma("y", lambda y, ti=ti, ib=ib: y.dma_start(
            out=ohi[ib][:], in_=t_sc["o_sc"][HALF + ti * 128:HALF + (ti + 1) * 128, :]),
            waits=luse["ohi"][ib], sem=ldsem)
        sch.dma("y", lambda y, r=r, ib=ib: y.dma_start(out=xnl[ib][:], in_=t_sc["xn_sc"][r, :]),
                waits=luse["xn"][ib], sem=ldsem)
        m_lg = sch.dma("y", lambda y, r=r, ib=ib: y.dma_start(out=gsl[ib][:],
                                                              in_=t_sc["gate_sc"][r, :]),
                       waits=luse["gs"][ib], sem=ldsem)
        for kk_ in ["olo", "ohi", "xn", "gs"]:
            luse[kk_][ib] = []
        m_os1 = sch.add("vec", vop(lambda v, ib=ib: v.tensor_scalar(
            osel[ib][:], olo[ib][:], hsel[:, 0:1], None, op0=ALU.mult)),
            waits=[m_lg, m_onr, prev.get(("osrd", ib))], sem="vec")
        m_os2 = sch.add("vec", vop(lambda v, ib=ib: v.scalar_tensor_tensor(
            out=osel[ib][:], in0=ohi[ib][:], scalar=hsel[:, 1:2], in1=osel[ib][:],
            op0=ALU.mult, op1=ALU.add)), waits=[m_os1], sem="vec")
        luse["olo"][ib].append(m_os2)
        luse["ohi"][ib].append(m_os2)
        m_sqo = sch.add("scl", lambda s, ib=ib: s.activation(out=sq[ib][:], in_=osel[ib][:],
                                                             func=AF.Square),
                        waits=[m_os2, prev.get(("s2red", ib))], sem="scl")
        m_ro = sch.add("vec", vop(lambda v, ib=ib: v.reduce_sum(
            red[:, 8:16], sq[ib][:].rearrange("p (h d) -> p h d", h=8), axis=AX.X)),
            waits=[m_sqo, prev.get("red")], sem="vec")

        def onorm(v):
            v.drain()
            v.tensor_scalar(red[:, 8:16], red[:, 8:16], 1.0 / DV, EPS, op0=ALU.mult,
                            op1=ALU.add)
            v.drain()
            newton_rsqrt(v, rs[:, 0:8], red[:, 8:16], ntf[:, 0:8], nti[:, 0:8])
            return v.tensor_copy(ntf[:, 0:1], rs[:, 0:1])
        m_on = sch.add("vec", onorm, waits=[m_ro], sem="vec")
        m_gos = sch.add("vec", vop(lambda v, ib=ib: v.tensor_tensor(
            sq[ib][:], osel[ib][:], gsl[ib][:], op=ALU.mult)),
            waits=[m_ro, m_lg], sem="vec")
        m_ogl = None
        for h in range(H):
            hh = slice(h * 128, (h + 1) * 128)
            m_ogl = sch.add("vec", vop(lambda v, h=h, hh=hh, ib=ib: v.scalar_tensor_tensor(
                out=gsl[ib][:, hh], in0=sq[ib][:, hh], scalar=rs[:, h:h + 1], in1=onw[:],
                op0=ALU.mult, op1=ALU.mult)),
                waits=[m_on, m_onw, m_gos,
                       prev.get(("ogrd", ib)) if h == 0 else None], sem="vec")
        prev[("osrd", ib)] = m_ogl
        mto = None
        for a in range(8):
            bt = bTR[a // 4]
            w0 = ([m_ogl] + bt.write_waits()) if a % 4 == 0 else []
            mto = sch.add("ten", lambda t, a=a, ib=ib, bt=bt: t.matmul(
                bt.ap[:, (a % 4) * 128:(a % 4 + 1) * 128], gsl[ib][:, a * 128:(a + 1) * 128],
                identb, start=True, stop=True), waits=w0, sem="ten")
            if a == 3:
                mto3 = mto
        m_ev5 = sch.add("scl", lambda s, ib=ib: s.copy(out=ogT[ib][:, 0:512], in_=bTR[0].ap[:]),
                        waits=[mto3, prev.get(("ogTrd", ib))], sem="scl")
        bTR[0].add_use(m_ev5)
        m_ev6 = sch.add("scl", lambda s, ib=ib: s.copy(out=ogT[ib][:, 512:1024],
                                                       in_=bTR[1].ap[:]),
                        waits=[mto, prev.get(("ogTrd", ib))], sem="scl")
        bTR[1].add_use(m_ev6)
        prev[("ogrd", ib)] = m_ev6
        luse["gs"][ib].append(m_ev6)
        yield
        mwo = None
        for half in range(2):
            bt = bWO[half]
            for a in range(8):
                w0 = ([m_ev5, m_ev6] + bt.write_waits()) if a == 0 else []
                mwo = sch.add("ten", lambda t, a=a, half=half, ib=ib, bt=bt: t.matmul(
                    bt.ap[:], ogT[ib][:, a * 128:(a + 1) * 128],
                    wo[:, a * 1024 + half * 512: a * 1024 + (half + 1) * 512],
                    start=(a == 0), stop=(a == 7)),
                    waits=w0 + ([m_onr] if ti == 0 and half == 0 and a == 0 else []),
                    sem="ten")
            if half == 0:
                mwo0 = mwo
        prev[("ogTrd", ib)] = mwo
        yield
        m_xa1 = sch.add("vec", vop(lambda v, ib=ib: v.tensor_add(
            xnl[ib][:, 0:512], xnl[ib][:, 0:512], bWO[0].ap[:])),
            waits=[mwo0, prev.get(("xard", ib))], sem="vec")
        bWO[0].add_use(m_xa1)
        m_xa2 = sch.add("vec", vop(lambda v, ib=ib: v.tensor_add(
            xnl[ib][:, 512:1024], xnl[ib][:, 512:1024], bWO[1].ap[:])),
            waits=[mwo], sem="vec")
        bWO[1].add_use(m_xa2)
        m_sqa = sch.add("scl", lambda s, ib=ib: s.activation(out=sq[ib][:], in_=xnl[ib][:],
                                                             func=AF.Square),
                        waits=[m_xa2, m_ro, m_ev6], sem="scl")
        m_ra = sch.add("vec", vop(lambda v, ib=ib: v.reduce_sum(red[:, 16:17], sq[ib][:],
                                                                axis=AX.X)),
                       waits=[m_sqa], sem="vec")
        prev["red"] = m_ra
        prev[("s2red", ib)] = m_ra

        def mnorm(v):
            v.drain()
            v.tensor_scalar(red[:, 16:17], red[:, 16:17], 1.0 / D, EPS, op0=ALU.mult,
                            op1=ALU.add)
            v.drain()
            newton_rsqrt(v, rs[:, 8:9], red[:, 16:17], ntf[:, 8:9], nti[:, 8:9])
            return v.tensor_copy(ntf[:, 8:9], rs[:, 8:9])
        m_mn = sch.add("vec", mnorm, waits=[m_ra], sem="vec")
        m_mb = sch.add("vec", vop(lambda v, ib=ib: v.tensor_scalar(
            mb_[ib][:], xnl[ib][:], rs[:, 8:9], None, op0=ALU.mult)),
            waits=[m_mn, prev.get(("mbrd", ib))], sem="vec")
        yield
        mtm = None
        for a in range(8):
            bt = bTR[a // 4]
            w0 = ([m_mb] + bt.write_waits()) if a % 4 == 0 else []
            mtm = sch.add("ten", lambda t, a=a, ib=ib, bt=bt: t.matmul(
                bt.ap[:, (a % 4) * 128:(a % 4 + 1) * 128], mb_[ib][:, a * 128:(a + 1) * 128],
                identb, start=True, stop=True), waits=w0, sem="ten")
            if a == 3:
                mtm3 = mtm
        m_ev7 = sch.add("scl", lambda s, ib=ib: s.copy(out=ogT[ib][:, 0:512],
                                                       in_=bTR[0].ap[:]),
                        waits=[mtm3, mwo, prev.get(("mTrd", ib))], sem="scl")
        bTR[0].add_use(m_ev7)
        m_ev8 = sch.add("scl", lambda s, ib=ib: s.copy(out=ogT[ib][:, 512:1024],
                                                       in_=bTR[1].ap[:]),
                        waits=[mtm, prev.get(("mTrd", ib))], sem="scl")
        bTR[1].add_use(m_ev8)
        prev[("mbrd", ib)] = m_ev8
        wsp = [m_ev8, m_xa2] + luse["sp"][ib]
        m_sp3 = sch.dma("y", lambda y, ti=ti, ib=ib: y.dma_start(
            out=t_sc["mT_sc"].rearrange("(a p) t -> p a t", p=128)[:, :, ti * 128:(ti + 1) * 128],
            in_=ogT[ib][:].rearrange("p (a t) -> p a t", a=8)), waits=wsp, sem=spsem)
        luse["sp"][ib] = [m_sp3]
        prev[("mTrd", ib)] = m_sp3
        prev[("xard", ib)] = m_sp3
        prev[("ogTrd", ib)] = m_sp3
        sp_marks[ti] = m_sp3

    luse2 = {"mT": [[], []], "xa": [[], []], "sp": [[], []]}
    prev2 = {}

    def b2_tile(ti):
        ib = ti % 2
        ldsem = "ewa" if ib == 0 else "ewb"
        spsem = "ama" if ib == 0 else "amb"
        r = slice(ti * 128, (ti + 1) * 128)
        m_lx = sch.dma("y", lambda y, ti=ti, ib=ib: y.dma_start(
            out=mTt[:].rearrange("p (a t) -> p a t", a=8),
            in_=t_sc["mT_sc"].rearrange("(a p) t -> p a t", p=128)[:, :, ti * 128:(ti + 1) * 128]),
            waits=luse2["mT"][0] + [sp_marks[ti]], sem=ldsem)
        luse2["mT"][0] = []
        m_act = None
        for grp in range(8):
            bt = bFC[grp % 2]
            mm = None
            for dblk in range(4):
                d = grp * 4 + dblk
                for a in range(8):
                    w0 = []
                    if dblk == 0 and a == 0:
                        w0 = [m_lx, m_onr] + bt.write_waits()
                    mm = sch.add("ten", lambda t, a=a, d=d, dblk=dblk, bt=bt: t.matmul(
                        bt.ap[:, dblk * 128:(dblk + 1) * 128],
                        wfc[:, a * DFF + d * 128: a * DFF + (d + 1) * 128],
                        mTt[:, a * 128:(a + 1) * 128],
                        start=(a == 0), stop=(a == 7)), waits=w0, sem="ten")
            m_rl = sch.add("scl", lambda s, bt=bt: s.activation(out=rlt[:], in_=bt.ap[:],
                                                                func=AF.Relu),
                           waits=[mm, prev2.get("rl")], sem="scl")
            bt.add_use(m_rl)
            m_act = sch.add("vec", vop(lambda v, grp=grp: v.tensor_tensor(
                actT[:, grp * 512:(grp + 1) * 512], rlt[:], rlt[:], op=ALU.mult)),
                waits=[m_rl, prev2.get("actrd") if grp == 0 else None], sem="vec")
            prev2["rl"] = m_act
            if grp % 2 == 1:
                yield
        luse2["mT"][0].append(m_act)
        my = None
        for half in range(2):
            for d in range(32):
                w0 = ([m_act] + bPJ.write_waits()) if d == 0 else []
                my = sch.add("ten", lambda t, d=d, half=half: t.matmul(
                    bPJ.ap[:], actT[:, d * 128:(d + 1) * 128],
                    wpj[:, d * 1024 + half * 512: d * 1024 + (half + 1) * 512],
                    start=(d == 0), stop=(d == 31)), waits=w0, sem="ten")
            yield
            m_s = sch.add("vec", vop(lambda v, half=half, ib=ib: v.tensor_add(
                osel[ib][:, half * 512:(half + 1) * 512],
                xnl[ib][:, half * 512:(half + 1) * 512], bPJ.ap[:])),
                waits=[my] + ([prev2.get(("srd", ib))] + luse2["sp"][ib]
                              if half == 0 else []), sem="vec")
            bPJ.add_use(m_s)
        prev2["actrd"] = my
        luse["xn"][ib].append(m_s)
        m_sp = sch.dma("y", lambda y, r=r, ib=ib: y.dma_start(out=t_sc["s_sc"][r, :],
                                                              in_=osel[ib][:]),
                       waits=[m_s], sem=spsem)
        luse2["sp"][ib] = [m_sp]
        prev2[("srd", ib)] = m_sp
        prev[("osrd", ib)] = m_sp

    b2g = [None]

    def b2_step(k=1):
        for _ in range(k):
            if b2g[0] is not None and next(b2g[0], "E") == "E":
                b2g[0] = None

    for ti in range(NHT + 1):
        if ti >= 1:
            while b2g[0] is not None:
                b2_step(1)
            b2g[0] = b2_tile(ti - 1)
            b2_step(1)
        if ti < NHT:
            for _ in b1b_tile(ti):
                b2_step(2)
        b2_step(2)
    while b2g[0] is not None:
        b2_step(1)

    sch.plan()

    @block.tensor
    def _(t):
        sch.emit_engine("ten", t)

    @block.vector
    def _(v):
        sch.emit_engine("vec", v)

    @block.scalar
    def _(s):
        sch.emit_engine("scl", s)

    @block.sync
    def _(y):
        sch.emit_engine("y", y)

    @block.gpsimd
    def _(g):
        sch.emit_engine("g", g)

    return [Ev(sems[s], sch.final_counts[s])
            for s in ["spill", "spill2", "ama", "amb"]]


def build_b12_standalone():
    nc = bass.Bass("TRN2", target_bir_lowering=False, debug=False)
    t_in = b_input_tensors(nc)
    t_sc = {
        "o_sc": nc.dram_tensor("o_sc", [S, D], BF16, kind="ExternalInput").ap(),
        "xn_sc": nc.dram_tensor("xn_sc", [HALF, D], F32, kind="Internal").ap(),
        "gate_sc": nc.dram_tensor("gate_sc", [HALF, D], BF16, kind="Internal").ap(),
        "xnT_sc": nc.dram_tensor("xnT_sc", [D, HALF], BF16, kind="Internal").ap(),
        "mT_sc": nc.dram_tensor("mT_sc", [D, HALF], BF16, kind="ExternalOutput").ap(),
        "s_sc": nc.dram_tensor("s_sc", [HALF, D], F32, kind="ExternalOutput").ap(),
    }
    ctx = ExitStack()
    with nc.Block() as block:
        sems = make_sems(nc, ctx)
        load_ident(nc, block, ctx, t_in, sems)
        e = ctx.enter_context
        onesrow_sb = e(nc.sbuf_tensor("onesrow_sb", [1, 128], F32))
        s_w = sems["w"]

        @block.gpsimd
        def _(g):
            s_w.mark(g.dma_start(out=onesrow_sb[:], in_=t_in["onesrow"]), 16)
        t_in["onesrow_sb"] = onesrow_sb[:]
        ctx1a = ExitStack()
        ev1a = pass_b1a(nc, block, ctx1a, t_in, t_sc, sems)
        ctx1a.close()
        ctx12 = ExitStack()
        ev12 = pass_b12(nc, block, ctx12, t_in, t_sc, sems, entry=ev1a)
        ctx12.close()

        @block.gpsimd
        def _(g):
            wait_all(g, *ev12)
    ctx.close()
    return nc
